# version 02
# baseline (speedup 1.0000x reference)
import contextlib
import numpy as np
import ml_dtypes
import concourse.bass as bass
import concourse.mybir as mybir
from concourse.bass_utils import run_bass_kernel_spmd

F32 = mybir.dt.float32
BF16 = mybir.dt.bfloat16
I32 = mybir.dt.int32
AF = mybir.ActivationFunctionType
ALU = mybir.AluOpType
AX = mybir.AxisListType
BFNP = ml_dtypes.bfloat16

D = 1024
T = 2048
TF = T + 4
H = 16
MFT = 256
S = 16
EI = 2048
TC = 512
SP = TC + 1
NTC = (2 * T) // TC
NDT = 8
NET = 16
NETH = 8
EPS_LN = 1e-5
EPS_PHI = 1e-6
S14 = float(1.0 / (64.0 ** 0.25))
COMPUTE_ALL_EXPERTS = True
N_EXP = 8 if COMPUTE_ALL_EXPERTS else 2
PAIRS = [[0, 1], [2, 3], [4, 5], [6, 7]]


def _host_prepare(inputs):
    f = lambda a: np.ascontiguousarray(np.asarray(a, np.float32))
    x = f(inputs["L0_attn_x"])
    big = {
        "wqT": f(inputs["L0_attn_Wq"]).T, "wkT": f(inputs["L0_attn_Wk"]).T,
        "wvT": f(inputs["L0_attn_Wv"]).T, "woT": f(inputs["L0_attn_Wo"]).T,
        "wrT": f(inputs["L0_ffn_W_r"]).T, "wkfT": f(inputs["L0_ffn_W_k"]).T,
        "wvfT": f(inputs["L0_ffn_W_v"]).T,
        "winT": f(inputs["L1_attn_W_in"]).T,
        "woutT": f(inputs["L1_attn_W_out"]).T,
        "outwT": f(inputs["L1_ffn_out_W"]).T,
        "gwT": f(inputs["L1_ffn_gate_W"]).T,
        "wxT": f(inputs["L1_attn_W_x"]).T,
    }
    big = {k: np.ascontiguousarray(v) for k, v in big.items()}
    big["expT"] = np.ascontiguousarray(
        np.transpose(f(inputs["L1_ffn_expert_W"]), (0, 2, 1)))
    bcol = lambda v, n: np.ascontiguousarray(f(v).reshape(n, 128).T)
    sm = {
        "bq_c": bcol(inputs["L0_attn_bq"], NDT),
        "bk_c": bcol(inputs["L0_attn_bk"], NDT),
        "bo_c": bcol(inputs["L0_attn_bo"], NDT),
        "bv_r": f(inputs["L0_attn_bv"]).reshape(1, D),
        "mur_c": bcol(inputs["L0_ffn_mu_r"], NDT),
        "muk_c": bcol(inputs["L0_ffn_mu_k"], NDT),
        "convw_c": np.ascontiguousarray(
            f(inputs["L1_attn_conv_w"])[:, 0, :].reshape(NET, 128, 4)
            .transpose(1, 0, 2).reshape(128, NET * 4)),
        "convb_c": bcol(inputs["L1_attn_conv_b"], NET),
        "wdt_r": f(inputs["L1_attn_W_dt"])[:, 0].reshape(1, EI),
        "bdt_c": bcol(inputs["L1_attn_b_dt"], NET),
        "alog_r": f(inputs["L1_attn_A_log"]).reshape(1, S),
        "dd_c": bcol(inputs["L1_attn_D"], NET),
    }
    orf = f(inputs["L0_attn_orf"])
    ident = np.eye(128, dtype=BFNP)
    orfpad = np.zeros((2, 128, MFT), BFNP)
    for h2 in range(2):
        orfpad[h2, 64 * h2:64 * h2 + 64, :] = orf.T.astype(BFNP)
    sel16 = np.zeros((16, 16 * 128), BFNP)
    for s_ in range(16):
        sel16[s_, 128 * s_:128 * (s_ + 1)] = 1
    bd2 = np.zeros((128, 2), BFNP)
    bd2[0:64, 0] = 1
    bd2[64:128, 1] = 1

    in_maps = []
    for c in range(8):
        b, j = c // 2, c % 2
        x_t = np.ascontiguousarray(x[b, j * T:(j + 1) * T].T)
        halo = (np.zeros((D, 4), np.float32) if j == 0
                else np.ascontiguousarray(x[b, T - 4:T].T))
        m = {"x_t": x_t, "x_halo": halo, "ident": ident, "orfpad": orfpad,
             "sel16": sel16, "bd2": bd2,
             "core_j": np.full((128, 1), float(j), np.float32),
             "joff": np.array([[1024 * j, 2048 * j]], np.int32)}
        m.update(big)
        m.update(sm)
        in_maps.append(m)
    return in_maps


def build_program():
    nc = bass.Bass("TRN2", target_bir_lowering=False)
    es = contextlib.ExitStack()
    dram = {}

    def din(name, shape, dt=F32):
        dram[name] = nc.dram_tensor(name, shape, dt, kind="ExternalInput").ap()

    def dout(name, shape, dt=F32):
        dram[name] = nc.dram_tensor(name, shape, dt,
                                    kind="ExternalOutput").ap()

    def dint(name, shape, dt=F32):
        dram[name] = nc.dram_tensor(name, shape, dt).ap()

    din("x_t", [D, T]); din("x_halo", [D, 4])
    for w in ("wqT", "wkT", "wvT", "woT", "wrT", "wkfT", "wvfT", "outwT"):
        din(w, [D, D])
    din("winT", [D, 4 * D]); din("woutT", [EI, D]); din("gwT", [D, 8])
    din("wxT", [EI, 33]); din("expT", [8, D, D])
    for nm, shp in (("bq_c", [128, NDT]), ("bk_c", [128, NDT]),
                    ("bo_c", [128, NDT]), ("bv_r", [1, D]),
                    ("mur_c", [128, NDT]), ("muk_c", [128, NDT]),
                    ("convw_c", [128, NET * 4]), ("convb_c", [128, NET]),
                    ("wdt_r", [1, EI]), ("bdt_c", [128, NET]),
                    ("alog_r", [1, S]), ("dd_c", [128, NET]),
                    ("core_j", [128, 1])):
        din(nm, shp)
    din("joff", [1, 2], I32)
    din("ident", [128, 128], BF16); din("orfpad", [2, 128, MFT], BF16)
    din("sel16", [16, 16 * 128], BF16); din("bd2", [128, 2], BF16)
    dout("out_t", [D, T])

    dint("phiq_sp", [T, H * MFT], BF16)
    dint("x1_sp", [D, TF], BF16)
    dint("res_sp", [EI, T], BF16)
    dint("xc_sp", [EI, T], BF16)
    dint("dt_sp", [EI, T], BF16)
    dint("w_sp", [EI, T], BF16)
    dint("y_sp", [EI // 2, 2 * T], BF16)
    dint("g2_sp", [D, T], BF16)
    dint("wbf_sp", [9, D, D], BF16)
    dint("winbf_sp", [D, 4 * D], BF16)
    dint("dtg_sp", [EI // 2, 2 * T], BF16)
    dint("wg_sp", [EI // 2, 2 * T], BF16)
    dint("ymine_sp", [EI, T], BF16)
    dint("kv_cin", [128, H * 2 * 65]); dint("kv_cout", [128, H * 2 * 65])
    dint("h4_cin", [D, 4], BF16); dint("h4_cout", [2 * D, 4], BF16)
    dint("dt_cout", [2 * EI, T], BF16)
    dint("w_cout", [2 * EI, T], BF16)
    dint("bc_cin", [32, T], BF16); dint("bc_cout", [64, T], BF16)
    dint("y_cout", [EI, 2 * T], BF16)

    thunks = {e: [] for e in ("sync", "scalar", "vector", "tensor", "gpsimd")}
    cnt = {}
    sems = {}
    for s_ in ("W", "X", "G", "MM", "AC", "VE", "GP", "CC"):
        sems[s_] = es.enter_context(nc.semaphore(s_))
        cnt[s_] = 0

    def emit(eng, fn):
        thunks[eng].append(fn)

    DMA_SEMS = ("W", "X", "G")
    QUEUE_OF = {"W": "sync", "X": "scalar", "G": "gpsimd"}
    awaited = {s_: set() for s_ in DMA_SEMS}
    ENG_SEM = {"sync": "W", "scalar": "AC", "vector": "VE", "tensor": "MM",
               "gpsimd": "GP"}

    writers = {}
    readers = {}
    watermark = {e: {} for e in thunks}
    mm_pending_reads = []

    def _tname(ap):
        try:
            return ap.tensor.name
        except AttributeError:
            return None

    def wait(eng, sem, val):
        if val <= 0:
            return
        if watermark[eng].get(sem, 0) >= val:
            return
        watermark[eng][sem] = val
        if sem in DMA_SEMS:
            awaited[sem].add(val)
        emit(eng, lambda e, s=sems[sem], v=val: e.wait_ge(s, v))

    def dep_wait(eng, read_names, write_names):
        for nm in read_names:
            w = writers.get(nm)
            if w is not None:
                ws, wv = w
                if not (eng == "tensor" and ws == "MM"):
                    wait(eng, ws, wv)
        for nm in write_names:
            w = writers.get(nm)
            if w is not None:
                ws, wv = w
                if not (eng == "tensor" and ws == "MM"):
                    wait(eng, ws, wv)
            for rs, rv in readers.get(nm, {}).items():
                if not (eng == "tensor" and rs == "MM"):
                    wait(eng, rs, rv)

    def dep_record(eng, sem, val, read_names, write_names):
        for nm in read_names:
            readers.setdefault(nm, {})[sem] = val
        for nm in write_names:
            writers[nm] = (sem, val)
            if nm in readers:
                readers[nm] = {}

    def ld(eng, dst, src, sem=None):
        if sem is None:
            sem = {"sync": "W", "scalar": "X", "gpsimd": "G"}[eng]
        assert QUEUE_OF.get(sem) == eng, (eng, sem)
        rn = [_tname(src)]
        wn = [_tname(dst)]
        dep_wait(eng, rn, wn)
        cnt[sem] += 16
        val = cnt[sem]
        dep_record(eng, sem, val, rn, wn)
        emit(eng, lambda e, d=dst, s=src, sh=sems[sem]:
             e.dma_start(out=d, in_=s).then_inc(sh, 16))

        def barrier(e, s=sems[sem], v=val, snm=sem):
            if v in awaited[snm]:
                e.wait_ge(s, v)
        emit(eng, barrier)
        return val

    def mm(outap, lhs, rhs, start, stop, inc=False):
        rn = [_tname(lhs), _tname(rhs)]
        wn = [_tname(outap)]
        if not start:
            rn.append(_tname(outap))
        dep_wait("tensor", rn, wn)
        mm_pending_reads.extend(rn)
        v = None
        if inc:
            cnt["MM"] += 1
            v = cnt["MM"]
            for nm in set(mm_pending_reads):
                readers.setdefault(nm, {})["MM"] = v
            mm_pending_reads.clear()
            writers[wn[0]] = ("MM", v)

        def f(e, o=outap, l=lhs, r=rhs, st=start, sp=stop, vv=v):
            i = e.matmul(o, l, r, start=st, stop=sp)
            if vv is not None:
                i.then_inc(sems["MM"], 1)
        emit("tensor", f)
        return cnt["MM"]

    def act(outap, inap, func, bias=0.0, scale=1.0, inc=True):
        rn = [_tname(inap)]
        if not isinstance(bias, float):
            rn.append(_tname(bias))
        if not isinstance(scale, float):
            rn.append(_tname(scale))
        wn = [_tname(outap)]
        dep_wait("scalar", rn, wn)
        cnt["AC"] += 1
        v = cnt["AC"]
        dep_record("scalar", "AC", v, rn, wn)

        def f(e, o=outap, i=inap, fu=func, b=bias, s=scale, vv=v):
            if fu == AF.Copy and not isinstance(b, float):
                fu = AF.Identity
            e.activation(o, i, fu, bias=b, scale=s).then_inc(sems["AC"], 1)
        emit("scalar", f)
        return cnt["AC"]

    def ve(op, *args, inc=True, eng="vector", **kw):
        sem = "VE" if eng == "vector" else "GP"
        rn = [_tname(a) for a in args[1:] if hasattr(a, "tensor")]
        rn += [_tname(a) for a in kw.values() if hasattr(a, "tensor")]
        wn = [_tname(args[0])] if hasattr(args[0], "tensor") else []
        if op == "memset":
            rn = []
        dep_wait(eng, rn, wn)
        cnt[sem] += 1
        v = cnt[sem]
        dep_record(eng, sem, v, rn, wn)

        def f(e, o=op, a=args, k=kw, vv=v, sh=sems[sem]):
            getattr(e, o)(*a, **k).then_inc(sh, 1)
        emit(eng, f)
        return cnt[sem]

    def transpose(outap, inap, kdim=128):
        rn = [_tname(inap), "ident_t"]
        wn = [_tname(outap)]
        dep_wait("tensor", rn, wn)
        cnt["MM"] += 1
        v = cnt["MM"]
        dep_record("tensor", "MM", v, rn, wn)

        def f(e, o=outap, i=inap, vv=v, k=kdim):
            e.transpose(o, i, ident_t[0:k, 0:k]).then_inc(sems["MM"], 1)
        emit("tensor", f)
        return v

    def collective(kind, op, in_ap, out_ap):
        rn = [_tname(in_ap)]
        wn = [_tname(out_ap)]
        dep_wait("gpsimd", rn, wn)
        cnt["CC"] += 1
        v = cnt["CC"]
        dep_record("gpsimd", "CC", v, rn, wn)

        def f(e, i=in_ap, o=out_ap, k=kind, a=op):
            e.collective_compute(k, a, replica_groups=PAIRS, ins=[i],
                                 outs=[o]).then_inc(sems["CC"], 1)
        emit("gpsimd", f)
        return cnt["CC"]

    def sb(stack, name, shape, dt=F32):
        return stack.enter_context(nc.sbuf_tensor(name, shape, dt))

    xn = [sb(es, f"xn{i}", [128, TF], BF16) for i in range(NDT)]
    wstg = [sb(es, f"wstg{i}", [128, 512], F32) for i in range(2)]
    scr = sb(es, "scr", [128, 1024], F32)
    scr2 = sb(es, "scr2", [128, 1024], F32)
    scrb = sb(es, "scrb", [128, 4160], BF16)
    onescol_b = sb(es, "onescol_b", [128, 1], BF16)
    onesrow_b = sb(es, "onesrow_b", [1, 128], BF16)
    ident_t = sb(es, "ident_t", [128, 128], BF16)
    bd2_t = sb(es, "bd2_t", [128, 2], BF16)
    bias_t = {nm: sb(es, nm + "_t", [128, 16]) for nm in
              ("bq_c", "bk_c", "bo_c", "mur_c", "muk_c", "convb_c", "bdt_c",
               "dd_c")}
    convw_t = sb(es, "convw_t", [128, NET * 4])
    bvrow_b = sb(es, "bvrow_b", [1, D], BF16)

    arow = sb(es, "arow", [1, S]); arow_b = sb(es, "arow_b", [1, S], BF16)
    acols = sb(es, "acols", [128, S])
    recipc = sb(es, "recipc", [128, 1])
    jcol = sb(es, "jcol", [128, 1])
    epsln_c = sb(es, "epsln_c", [128, 1])
    one_c = sb(es, "one_c", [128, 1])
    joff_t = sb(es, "joff_t", [1, 2], I32)
    wdtrow_b = sb(es, "wdtrow_b", [1, EI], BF16)

    pbk = [es.enter_context(nc.psum_tensor(f"pb{i}", [128, 512], F32))
           for i in range(8)]
    ptr = pbk[1].bitcast(BF16)

    ld("sync", ident_t[:], dram["ident"][:], "W")
    ld("sync", bd2_t[:], dram["bd2"][:], "W")
    widths = {"bq_c": NDT, "bk_c": NDT, "bo_c": NDT, "mur_c": NDT,
              "muk_c": NDT, "convb_c": NET, "bdt_c": NET, "dd_c": NET}
    for nm, wdt in widths.items():
        ld("sync", bias_t[nm][:, 0:wdt], dram[nm][:], "W")
    ld("sync", convw_t[:], dram["convw_c"][:], "W")
    ld("sync", scr[0:1, 0:1024], dram["bv_r"][:], "W")
    ld("sync", scr2[0:1, 0:1024], dram["wdt_r"][:, 0:1024], "W")
    ld("sync", wstg[0][0:1, :], dram["wdt_r"][:, 1024:1536], "W")
    ld("sync", wstg[1][0:1, :], dram["wdt_r"][:, 1536:2048], "W")
    ld("sync", arow[:], dram["alog_r"][:], "W")
    w_const = ld("sync", jcol[:], dram["core_j"][:], "W")
    g_joff = ld("gpsimd", joff_t[:], dram["joff"][:], "G")

    ve("memset", onescol_b[:], 1.0, inc=False)
    ve("memset", epsln_c[:], EPS_LN, inc=False)
    ve("memset", one_c[:], 1.0, inc=False)
    ve("memset", onesrow_b[:], 1.0)
    v_init = cnt["VE"]
    wait("vector", "W", w_const)
    ve("tensor_copy", bvrow_b[:], scr[0:1, 0:1024], inc=False)
    ve("tensor_copy", wdtrow_b[:, 0:1024], scr2[0:1, 0:1024], inc=False)
    ve("tensor_copy", wdtrow_b[:, 1024:1536], wstg[0][0:1, :], inc=False)
    ve("tensor_copy", wdtrow_b[:, 1536:2048], wstg[1][0:1, :], inc=False)
    ve("tensor_copy", arow_b[:], arow[:])
    v_rows0 = cnt["VE"]
    wait("tensor", "VE", v_rows0)
    mm(pbk[0][:, 0:S], onesrow_b[:], arow_b[:], True, True, inc=True)
    wait("scalar", "MM", cnt["MM"])
    ac_acol = act(acols[:], pbk[0][:, 0:S], AF.Exp)
    wait("scalar", "AC", ac_acol)
    ac_acol = act(acols[:], acols[:], AF.Copy, scale=-1.0)

    def emit_ln(tiles, rows, pre_sem, pre_val):
        mrow, m2row, mrow_b, rrow_b = rows
        chunks = [(0, 4), (4, 516), (516, 1028), (1028, 1540), (1540, 2052)]
        wait("scalar", pre_sem, pre_val)
        for (c0, c1) in chunks:
            n = c1 - c0
            wait("scalar", "MM", cnt["MM"])
            wait("scalar", "VE", cnt["VE"])
            for i in range(NDT):
                act(scrb[:, 516 * i:516 * i + n], tiles[i][:, c0:c1],
                    AF.Square, inc=(i == NDT - 1))
            ac_sq = cnt["AC"]
            wait("tensor", "AC", ac_sq)
            for i in range(NDT):
                mm(pbk[4][0:1, 0:n], onescol_b[:], tiles[i][:, c0:c1],
                   i == 0, i == NDT - 1, inc=(i == NDT - 1))
            for i in range(NDT):
                mm(pbk[5][0:1, 0:n], onescol_b[:],
                   scrb[:, 516 * i:516 * i + n], i == 0, i == NDT - 1,
                   inc=(i == NDT - 1))
            wait("scalar", "MM", cnt["MM"])
            act(mrow[:, 0:n], pbk[4][0:1, 0:n], AF.Copy, scale=1.0 / D,
                inc=False)
            act(m2row[:, 0:n], pbk[5][0:1, 0:n], AF.Copy, scale=1.0 / D)
            ac_rows = cnt["AC"]
            wait("vector", "AC", ac_rows)
            ve("tensor_tensor", scr[0:1, 0:n], mrow[:, 0:n], mrow[:, 0:n],
               ALU.mult, inc=False)
            ve("tensor_tensor", m2row[:, 0:n], m2row[:, 0:n],
               scr[0:1, 0:n], ALU.subtract)
            wait("scalar", "VE", cnt["VE"])
            act(m2row[:, 0:n], m2row[:, 0:n], AF.Sqrt, bias=epsln_c[0:1, :])
            wait("vector", "AC", cnt["AC"])
            ve("reciprocal", m2row[:, 0:n], m2row[:, 0:n], inc=False)
            ve("tensor_copy", mrow_b[:, 0:n], mrow[:, 0:n], inc=False)
            ve("tensor_copy", rrow_b[:, 0:n], m2row[:, 0:n])
            wait("tensor", "VE", cnt["VE"])
            mm(pbk[6][:, 0:n], onesrow_b[:], mrow_b[:, 0:n], True, True,
               inc=False)
            mm(pbk[7][:, 0:n], onesrow_b[:], rrow_b[:, 0:n], True, True,
               inc=True)
            wait("vector", "MM", cnt["MM"])
            for i in range(NDT):
                ve("tensor_tensor", scr[:, 0:n], tiles[i][:, c0:c1],
                   pbk[6][:, 0:n], ALU.subtract, inc=False)
                ve("tensor_tensor", xn[i][:, c0:c1], scr[:, 0:n],
                   pbk[7][:, 0:n], ALU.mult, inc=(i == NDT - 1))
        return cnt["VE"]

    def load_w_slab(wname, slabs, col0, guards):
        last = 0
        for i in range(NDT):
            for hf in range(2):
                par = (2 * i + hf) % 2
                wait("sync", "AC", guards[par])
                ld("sync", wstg[par][:],
                   dram[wname][128 * i:128 * (i + 1),
                               512 * hf:512 * (hf + 1)], "W")
                wait("scalar", "W", cnt["W"])
                last = act(slabs[i][:, col0 + 512 * hf:col0 + 512 * (hf + 1)],
                           wstg[par][:], AF.Copy)
                guards[par] = last
        return last

    ph = contextlib.ExitStack()
    xb = [sb(ph, f"xb{i}", [128, TF], BF16) for i in range(NDT)]
    wsl = [sb(ph, f"wsl{i}", [128, 3 * D], BF16) for i in range(NDT)]
    mrow = sb(ph, "mrow", [1, 516]); m2row = sb(ph, "m2row", [1, 516])
    mrow_b = sb(ph, "mrow_b", [1, 516], BF16)
    rrow_b = sb(ph, "rrow_b", [1, 516], BF16)
    rows = (mrow, m2row, mrow_b, rrow_b)
    qg = [sb(ph, f"qg{i}", [128, 512], BF16) for i in range(NDT)]
    kg = [sb(ph, f"kg{i}", [128, 512], BF16) for i in range(NDT)]
    sq2 = sb(ph, "sq2", [128, H]); sk2 = sb(ph, "sk2", [128, H])
    phik = sb(ph, "phik", [128, H * MFT], BF16)
    phiq_t = sb(ph, "phiq_t", [128, H * MFT], BF16)
    vplus = sb(ph, "vplus", [128, H * 65], BF16)
    kvacc = sb(ph, "kvacc", [128, H * 2 * 65], F32)
    kvbf = scrb[:, 2048:2048 + H * 2 * 65]
    phiqT = scrb[:, 1024:1280]
    merged = sb(ph, "merged", [128, D], BF16)
    sch = [sb(ph, f"sch{i}", [128, 512], BF16) for i in range(NDT)]
    rch = [sb(ph, f"rch{i}", [128, 512], BF16) for i in range(NDT)]
    xrg = qg
    xkg = kg
    gch = xkg
    h4b = sb(ph, "h4b", [128, 4 * NDT], BF16)
    orfpad_t = sb(ph, "orfpad_t", [128, 2 * MFT], BF16)
    ld("sync", orfpad_t[:, 0:MFT], dram["orfpad"][0], "W")
    ld("sync", orfpad_t[:, MFT:2 * MFT], dram["orfpad"][1], "W")

    wait("sync", "VE", v_rows0)
    cast_guard = [0, 0]
    for i in range(NDT):
        for half in range(4):
            par = half % 2
            wait("sync", "AC", cast_guard[par])
            ld("sync", wstg[par][:],
               dram["x_t"][128 * i:128 * (i + 1),
                           512 * half:512 * (half + 1)], "W")
            wait("scalar", "W", cnt["W"])
            cast_guard[par] = act(
                xb[i][:, 4 + 512 * half:4 + 512 * (half + 1)],
                wstg[par][:], AF.Copy)
    wait("sync", "AC", cast_guard[1])
    for i in range(NDT):
        ld("sync", wstg[1][:, 4 * i:4 * (i + 1)],
           dram["x_halo"][128 * i:128 * (i + 1), :], "W")
    wait("scalar", "W", cnt["W"])
    for i in range(NDT):
        act(xb[i][:, 0:4], wstg[1][:, 4 * i:4 * (i + 1)], AF.Copy,
            inc=(i == NDT - 1))
    x_cast = cnt["AC"]

    v_xn = emit_ln(xb, rows, "AC", x_cast)

    wguards = [x_cast, x_cast]
    ac_wq = load_w_slab("wqT", wsl, 0, wguards)
    ac_wk = load_w_slab("wkT", wsl, D, wguards)
    ac_wv = load_w_slab("wvT", wsl, 2 * D, wguards)

    def kv_slot(h, mc):
        idx = h * 2 + mc
        if idx < 28:
            return pbk[4 + idx // 7][:, 65 * (idx % 7):65 * (idx % 7) + 65]
        return pbk[3][:, 65 * (idx - 28):65 * (idx - 27)]

    wait("tensor", "AC", ac_wv)
    wait("tensor", "VE", v_xn)
    ve("memset", kvacc[:], 0.0)
    v_kvx0 = cnt["VE"]
    qk_guard = 0
    prev_kv_mm = 0
    prev_kvx_ve = v_kvx0
    prev_vplus_ac = 0
    prev_sq_ac = 0
    for tg in range(4):
        cg = 4 + 512 * tg
        wait("tensor", "AC", qk_guard)
        for (dst, col0, bias_nm) in ((qg, 0, "bq_c"), (kg, D, "bk_c")):
            for ot in range(NDT):
                bank = pbk[0 if dst is qg else 2]
                if not (dst is qg and ot == 0):
                    wait("tensor", "AC", cnt["AC"])
                for dc in range(NDT):
                    mm(bank[:, 0:512],
                       wsl[dc][:, col0 + 128 * ot:col0 + 128 * (ot + 1)],
                       xn[dc][:, cg:cg + 512], dc == 0, dc == NDT - 1,
                       inc=(dc == NDT - 1))
                wait("scalar", "MM", cnt["MM"])
                act(dst[ot][:], bank[:, 0:512], AF.Copy,
                    bias=bias_t[bias_nm][:, ot:ot + 1])
        qk_guard = cnt["AC"]

        for tcs in range(4):
            tcc = 4 * tg + tcs
            c0 = 4 + 128 * tcc
            cl = 128 * tcs
            wait("tensor", "AC", max(prev_vplus_ac, qk_guard))
            for half in range(2):
                bank = pbk[0 if half == 0 else 2]
                for dc in range(NDT):
                    mm(bank[:, 0:512], xn[dc][:, c0:c0 + 128],
                       wsl[dc][:, 2 * D + 512 * half:
                               2 * D + 512 * (half + 1)],
                       dc == 0, False)
                mm(bank[:, 0:512], onesrow_b[:],
                   bvrow_b[:, 512 * half:512 * (half + 1)], False, True,
                   inc=True)
            mm_v = cnt["MM"]
            wait("vector", "MM", prev_kv_mm)
            ve("memset", vplus[:], 1.0)
            v_vones = cnt["VE"]
            wait("scalar", "MM", mm_v)
            wait("scalar", "VE", v_vones)
            for half in range(2):
                act(vplus[:].rearrange("p (h c) -> p h c", c=65)
                    [:, 8 * half:8 * (half + 1), 0:64],
                    pbk[0 if half == 0 else 2][:, 0:512], AF.Copy,
                    inc=(half == 1))
            ac_vplus = cnt["AC"]
            prev_vplus_ac = ac_vplus

            wait("scalar", "AC", max(prev_sq_ac, qk_guard))
            for i in range(NDT):
                act(scrb[:, 128 * i:128 * (i + 1)], qg[i][:, cl:cl + 128],
                    AF.Square, inc=False)
                act(scrb[:, 1024 + 128 * i:1024 + 128 * (i + 1)],
                    kg[i][:, cl:cl + 128], AF.Square, inc=(i == NDT - 1))
            ac_sq = cnt["AC"]
            wait("tensor", "AC", max(ac_sq, ac_vplus))
            for i in range(NDT):
                mm(pbk[2][:, 2 * i:2 * i + 2],
                   scrb[:, 128 * i:128 * (i + 1)], bd2_t[:], True, True,
                   inc=False)
                mm(pbk[2][:, H + 2 * i:H + 2 * i + 2],
                   scrb[:, 1024 + 128 * i:1024 + 128 * (i + 1)], bd2_t[:],
                   True, True, inc=(i == NDT - 1))
            wait("scalar", "MM", cnt["MM"])
            act(sq2[:], pbk[2][:, 0:H], AF.Copy, scale=-1.0 / 16.0,
                inc=False)
            act(sk2[:], pbk[2][:, H:2 * H], AF.Copy, scale=-1.0 / 16.0)
            ac_sig = cnt["AC"]
            prev_sq_ac = ac_sig

            wait("tensor", "AC", ac_sig)
            wait("tensor", "VE", prev_kvx_ve)
            ac_phi_prev = [ac_sig, ac_sig]
            for (srct, dstphi, sig) in ((qg, phiq_t, sq2), (kg, phik, sk2)):
                for h in range(H):
                    pair, h2 = h // 2, h % 2
                    par = h % 2
                    bank = pbk[2 + par]
                    wait("tensor", "AC", ac_phi_prev[par])
                    mm(bank[:, 0:MFT], srct[pair][:, cl:cl + 128],
                       orfpad_t[:, MFT * h2:MFT * (h2 + 1)], True, True,
                       inc=True)
                    wait("scalar", "MM", cnt["MM"])
                    ac_phi_prev[par] = act(dstphi[:, MFT * h:MFT * (h + 1)],
                                           bank[:, 0:MFT], AF.Exp,
                                           scale=S14, bias=sig[:, h:h + 1])
            ac_phi = cnt["AC"]
            wait("vector", "AC", ac_phi)
            ve("tensor_scalar", phiq_t[:], phiq_t[:], 1.0, EPS_PHI,
               ALU.mult, ALU.add, inc=False)
            ve("tensor_scalar", phik[:], phik[:], 1.0, EPS_PHI, ALU.mult,
               ALU.add)
            v_phi = cnt["VE"]
            wait("scalar", "VE", v_phi)
            ld("scalar", dram["phiq_sp"][128 * tcc:128 * (tcc + 1), :],
               phiq_t[:], "X")

            wait("tensor", "VE", v_phi)
            for h in range(H):
                for mc in range(2):
                    mm(kv_slot(h, mc),
                       phik[:, MFT * h + 128 * mc:MFT * h + 128 * (mc + 1)],
                       vplus[:, 65 * h:65 * (h + 1)],
                       True, True,
                       inc=(h * 2 + mc) % 7 == 6 or (h == H - 1 and mc == 1))
            prev_kv_mm = cnt["MM"]
            wait("vector", "MM", prev_kv_mm)
            for r5 in range(5):
                w5 = 455 if r5 < 4 else 260
                srcp = pbk[4 + r5][:, 0:455] if r5 < 4 else pbk[3][:, 0:260]
                ve("tensor_tensor", kvacc[:, 455 * r5:455 * r5 + w5],
                   kvacc[:, 455 * r5:455 * r5 + w5], srcp, ALU.add)
            prev_kvx_ve = cnt["VE"]

    wait("gpsimd", "VE", prev_kvx_ve)
    wait("gpsimd", "G", g_joff)
    gv = ld("gpsimd", dram["kv_cin"][:], kvacc[:], "G")
    wait("gpsimd", "G", gv)
    cc_kv = collective("AllReduce", ALU.add, dram["kv_cin"][:],
                       dram["kv_cout"][:])
    wait("gpsimd", "CC", cc_kv)
    gv = ld("gpsimd", kvacc[:], dram["kv_cout"][:], "G")
    wait("scalar", "G", gv)
    ac_kvbf = act(scrb[:, 2048:2048 + 2080], kvacc[:], AF.Copy)

    def kvbf_slot(h, mc):
        idx = h * 2 + mc
        return scrb[:, 2048 + 65 * idx:2048 + 65 * (idx + 1)]

    ac_wo = load_w_slab("woT", wsl, 0, wguards)
    ac_wr = load_w_slab("wrT", wsl, D, wguards)
    ac_wkf = load_w_slab("wkfT", wsl, 2 * D, wguards)

    wait("tensor", "AC", ac_kvbf)
    v_x1 = 0
    for tg in range(4):
        cg = 4 + 512 * tg
        for tcs in range(4):
            tcc = 4 * tg + tcs
            cl = 128 * tcs
            v_phl = ld("scalar", phiq_t[:],
                       dram["phiq_sp"][128 * tcc:128 * (tcc + 1), :], "X")
            wait("tensor", "X", v_phl)
            if tg > 0 or tcs > 0:
                wait("tensor", "VE", v_x1)
            for h in range(H):
                for mc in range(2):
                    transpose(ptr[:, 0:128],
                              phiq_t[:, MFT * h + 128 * mc:
                                     MFT * h + 128 * (mc + 1)])
                    wait("scalar", "MM", cnt["MM"])
                    act(phiqT[:, 128 * mc:128 * (mc + 1)], ptr[:, 0:128],
                        AF.Copy)
                wait("tensor", "AC", cnt["AC"])
                for mc in range(2):
                    mm(pbk[0][:, 0:65], phiqT[:, 128 * mc:128 * (mc + 1)],
                       kvbf_slot(h, mc), mc == 0, mc == 1, inc=(mc == 1))
                wait("vector", "MM", cnt["MM"])
                ve("reciprocal", recipc[:], pbk[0][:, 64:65])
                wait("scalar", "VE", cnt["VE"])
                act(merged[:, 64 * h:64 * (h + 1)], pbk[0][:, 0:64],
                    AF.Copy, scale=recipc[:])
                wait("tensor", "AC", cnt["AC"])
            for i in range(NDT):
                transpose(ptr[:, 0:128], merged[:, 128 * i:128 * (i + 1)])
                wait("scalar", "MM", cnt["MM"])
                act(qg[i][:, cl:cl + 128], ptr[:, 0:128], AF.Copy,
                    inc=(i == NDT - 1))
            wait("tensor", "AC", cnt["AC"])
        ac_mrg = cnt["AC"]

        wait("tensor", "AC", max(ac_mrg, ac_wo))
        for ot in range(NDT):
            bank = pbk[2 + ot % 2]
            if ot >= 2:
                wait("tensor", "AC", cnt["AC"] - 1)
            for dc in range(NDT):
                mm(bank[:, 0:512], wsl[dc][:, 128 * ot:128 * (ot + 1)],
                   qg[dc][:], dc == 0, dc == NDT - 1, inc=(dc == NDT - 1))
            wait("scalar", "MM", cnt["MM"])
            act(sch[ot][:], bank[:, 0:512], AF.Copy,
                bias=bias_t["bo_c"][:, ot:ot + 1])
        ac_s = cnt["AC"]

        wait("vector", "AC", ac_s)
        for i in range(NDT):
            ve("tensor_tensor", scrb[:, 0:512], xn[i][:, cg:cg + 512],
               xn[i][:, cg - 1:cg + 511], ALU.subtract, inc=False)
            ve("scalar_tensor_tensor", xrg[i][:], scrb[:, 0:512],
               bias_t["mur_c"][:, i:i + 1], xn[i][:, cg - 1:cg + 511],
               ALU.mult, ALU.add, inc=False)
            ve("scalar_tensor_tensor", xkg[i][:], scrb[:, 0:512],
               bias_t["muk_c"][:, i:i + 1], xn[i][:, cg - 1:cg + 511],
               ALU.mult, ALU.add, inc=(i == NDT - 1))
        v_mix = cnt["VE"]
        wait("tensor", "VE", v_mix)
        wait("tensor", "AC", ac_wr)
        for ot in range(NDT):
            bank = pbk[2 + ot % 2]
            if ot >= 2:
                wait("tensor", "AC", cnt["AC"] - 1)
            for dc in range(NDT):
                mm(bank[:, 0:512],
                   wsl[dc][:, D + 128 * ot:D + 128 * (ot + 1)],
                   xrg[dc][:], dc == 0, dc == NDT - 1, inc=(dc == NDT - 1))
            wait("scalar", "MM", cnt["MM"])
            act(rch[ot][:], bank[:, 0:512], AF.Sigmoid)
        ac_r = cnt["AC"]
        wait("tensor", "AC", ac_wkf)
        for ot in range(NDT):
            bank = pbk[2 + ot % 2]
            if ot >= 2:
                wait("tensor", "VE", cnt["VE"] - 1)
            for dc in range(NDT):
                mm(bank[:, 0:512],
                   wsl[dc][:, 2 * D + 128 * ot:2 * D + 128 * (ot + 1)],
                   xkg[dc][:], dc == 0, dc == NDT - 1, inc=(dc == NDT - 1))
            wait("scalar", "MM", cnt["MM"])
            act(scrb[:, 512:1024], bank[:, 0:512], AF.Copy)
            wait("vector", "AC", cnt["AC"])
            ve("scalar_tensor_tensor", xrg[ot][:], scrb[:, 512:1024], 0.0,
               scrb[:, 512:1024], ALU.max, ALU.mult)
            wait("scalar", "VE", cnt["VE"])
        v_kact = cnt["VE"]
        wg2 = [ac_r, ac_r]
        ac_wvf = load_w_slab("wvfT", wsl, 0, wg2)
        wait("tensor", "VE", v_kact)
        wait("tensor", "AC", ac_wvf)
        for ot in range(NDT):
            bank = pbk[2 + ot % 2]
            if ot >= 2:
                wait("tensor", "AC", cnt["AC"] - 1)
            for dc in range(NDT):
                mm(bank[:, 0:512], wsl[dc][:, 128 * ot:128 * (ot + 1)],
                   xrg[dc][:], dc == 0, dc == NDT - 1, inc=(dc == NDT - 1))
            wait("vector", "MM", cnt["MM"])
            ve("tensor_tensor", scr[:, 0:512], rch[ot][:], bank[:, 0:512],
               ALU.mult)
            wait("scalar", "VE", cnt["VE"])
            act(gch[ot][:], scr[:, 0:512], AF.Silu)
            wait("vector", "AC", cnt["AC"])
        ac_g = cnt["AC"]
        wait("vector", "AC", ac_g)
        for i in range(NDT):
            ve("tensor_tensor", scrb[:, 0:512], sch[i][:], gch[i][:],
               ALU.mult, inc=False)
            ve("tensor_tensor", xb[i][:, cg:cg + 512],
               xb[i][:, cg:cg + 512], scrb[:, 0:512], ALU.add,
               inc=(i == NDT - 1))
        v_x1 = cnt["VE"]
        if tg < 3:
            wg3 = [ac_g, ac_g]
            ac_wo = load_w_slab("woT", wsl, 0, wg3)

    wait("gpsimd", "VE", v_x1)
    gh = 0
    for i in range(NDT):
        gh = ld("gpsimd", dram["h4_cin"][128 * i:128 * (i + 1), :],
                xb[i][:, TF - 4:TF], "G")
    wait("gpsimd", "G", gh)
    cc_h4 = collective("AllGather", ALU.bypass, dram["h4_cin"][:],
                       dram["h4_cout"][:])
    wait("gpsimd", "CC", cc_h4)
    gh2 = 0
    for i in range(NDT):
        gh2 = ld("gpsimd", h4b[:, 4 * i:4 * (i + 1)],
                 dram["h4_cout"][128 * i:128 * (i + 1), :], "G")
    wait("vector", "G", gh2)
    for i in range(NDT):
        ve("tensor_single_scalar", xb[i][:, 0:4], h4b[:, 4 * i:4 * (i + 1)],
           jcol[:], ALU.mult, inc=(i == NDT - 1))
    v_x1h = cnt["VE"]

    v_xn2 = emit_ln(xb, rows, "VE", v_x1h)
    wait("scalar", "VE", v_xn2)
    x1_sp_v = 0
    for i in range(NDT):
        x1_sp_v = ld("scalar", dram["x1_sp"][128 * i:128 * (i + 1), :],
                     xb[i][:], "X")
    ph.close()

    ph = contextlib.ExitStack()
    winbf = [sb(ph, f"winbf{i}", [128, D], BF16) for i in range(2)]
    xcch = [sb(ph, f"xcch{i}", [128, 516], BF16) for i in range(NET)]
    ssm_t = sb(ph, "ssm_t", [33, T], BF16)
    dtpre = sb(ph, "dtpre", [1, T], BF16)
    wxsl = sb(ph, "wxsl", [128, 33 * NET], BF16)
    xptail = [sb(ph, f"xpt{i}", [128, 4], BF16) for i in range(NET)]

    wcg = [0, 0]
    wv_ = 0
    for dc in range(NDT):
        for half in range(8):
            par = half % 2
            wait("sync", "AC", wcg[par])
            ld("sync", wstg[par][:],
               dram["winT"][128 * dc:128 * (dc + 1),
                            512 * half:512 * (half + 1)], "W")
            wait("scalar", "W", cnt["W"])
            wait("scalar", "X", wv_)
            wcg[par] = act(winbf[par][:, 0:512], wstg[par][:], AF.Copy)
            wait("scalar", "AC", wcg[par])
            wv_ = ld("scalar",
                     dram["winbf_sp"][128 * dc:128 * (dc + 1),
                                      512 * half:512 * (half + 1)],
                     winbf[par][:, 0:512], "X")
    winbf_done = cnt["X"]
    for i in range(8):
        ld("sync", wstg[0][:, 33 * i:33 * (i + 1)],
           dram["wxT"][128 * i:128 * (i + 1), :], "W")
    for i in range(8):
        ld("sync", wstg[1][:, 33 * i:33 * (i + 1)],
           dram["wxT"][128 * (8 + i):128 * (9 + i), :], "W")
    wait("scalar", "W", cnt["W"])
    wait("scalar", "X", winbf_done)
    act(wxsl[:, 0:264], wstg[0][:, 0:264], AF.Copy, inc=False)
    ac_wx = act(wxsl[:, 264:528], wstg[1][:, 0:264], AF.Copy)

    wait("tensor", "AC", ac_wx)
    wait("tensor", "X", winbf_done)
    wait("tensor", "VE", v_xn2)
    wld_guard = [0, 0]
    dtw_prev_ve = 0
    vdw = 0
    xc_sp_v = 0
    evac_guard = [0, 0]
    for tc4 in range(4):
        c0 = 4 + 512 * tc4
        res_v = 0
        for ot in range(32):
            par = ot % 2
            wait("sync", "MM", wld_guard[par])
            vw = ld("sync",
                    winbf[par][:].rearrange("p (dc q) -> p dc q", q=128),
                    dram["winbf_sp"].rearrange(
                        "(dc p) (o q) -> o p dc q", p=128, q=128)[ot], "W")
            bank = pbk[ot % 2]
            wait("tensor", "W", vw)
            wait("tensor", "AC", evac_guard[par])
            for dc in range(NDT):
                mm(bank[:, 0:512], winbf[par][:, 128 * dc:128 * (dc + 1)],
                   xn[dc][:, c0:c0 + 512], dc == 0, dc == NDT - 1,
                   inc=(dc == NDT - 1))
            wld_guard[par] = cnt["MM"]
            wait("scalar", "MM", cnt["MM"])
            if ot < NET:
                evac_guard[par] = act(xcch[ot][:, 4:516], bank[:, 0:512],
                                      AF.Copy)
            else:
                wait("scalar", "X", res_v)
                evac_guard[par] = act(scrb[:, 0:512], bank[:, 0:512],
                                      AF.Copy)
                wait("scalar", "AC", cnt["AC"])
                i = ot - NET
                res_v = ld("scalar",
                           dram["res_sp"][128 * i:128 * (i + 1),
                                          512 * tc4:512 * (tc4 + 1)],
                           scrb[:, 0:512], "X")
        ac_xpc = cnt["AC"]
        wait("vector", "AC", ac_xpc)
        if tc4 == 0:
            for et in range(NET):
                par = et % 2
                wait("sync", "MM", wld_guard[par])
                vw = ld("sync",
                        winbf[par][:].rearrange("p (dc q) -> p dc q", q=128),
                        dram["winbf_sp"].rearrange(
                            "(dc p) (o q) -> o p dc q", p=128, q=128)[et], "W")
                bank = pbk[2]
                wait("tensor", "W", vw)
                wait("tensor", "AC", cnt["AC"])
                for dc in range(NDT):
                    mm(bank[:, 0:3], winbf[par][:, 128 * dc:128 * (dc + 1)],
                       xn[dc][:, 1:4], dc == 0, dc == NDT - 1,
                       inc=(dc == NDT - 1))
                wld_guard[par] = cnt["MM"]
                wait("scalar", "MM", cnt["MM"])
                act(xcch[et][:, 1:4], bank[:, 0:3], AF.Copy,
                    inc=(et == NET - 1))
            ac_xph = cnt["AC"]
            wait("vector", "AC", ac_xph)
        for et in range(NET):
            if tc4 > 0:
                ve("tensor_copy", xcch[et][:, 0:4], xptail[et][:],
                   inc=False)
            ve("tensor_scalar", scrb[:, 512:1024], xcch[et][:, 1:513],
               convw_t[:, 4 * et:4 * et + 1], None, ALU.mult, inc=False)
            for tap in range(1, 4):
                ve("scalar_tensor_tensor", scrb[:, 512:1024],
                   xcch[et][:, 1 + tap:513 + tap],
                   convw_t[:, 4 * et + tap:4 * et + tap + 1],
                   scrb[:, 512:1024], ALU.mult, ALU.add, inc=(tap == 3))
            ve("tensor_copy", xptail[et][:], xcch[et][:, 512:516])
            wait("scalar", "VE", cnt["VE"])
            act(xcch[et][:, 4:516], scrb[:, 512:1024], AF.Silu,
                bias=bias_t["convb_c"][:, et:et + 1])
            wait("vector", "AC", cnt["AC"])
            wait("scalar", "AC", cnt["AC"])
            wait("scalar", "X", xc_sp_v)
            xc_sp_v = ld("scalar",
                         dram["xc_sp"][128 * et:128 * (et + 1),
                                       512 * tc4:512 * (tc4 + 1)],
                         xcch[et][:, 4:516], "X")
        ac_xcc = cnt["AC"]
        wait("tensor", "AC", ac_xcc)
        bank = pbk[4]
        for et in range(NET):
            mm(bank[0:33, 0:512], wxsl[:, 33 * et:33 * (et + 1)],
               xcch[et][:, 4:516], et == 0, et == NET - 1,
               inc=(et == NET - 1))
        wait("scalar", "MM", cnt["MM"])
        act(ssm_t[:, 512 * tc4:512 * (tc4 + 1)], bank[0:33, 0:512],
            AF.Copy)
        ac_ssmc = cnt["AC"]
        wait("gpsimd", "AC", ac_ssmc)
        gdt = ld("gpsimd", dtpre[:, 512 * tc4:512 * (tc4 + 1)],
                 ssm_t[32:33, 512 * tc4:512 * (tc4 + 1)], "G")
        wait("tensor", "G", gdt)
        wait("tensor", "VE", v_rows0)
        for et in range(NET):
            bank = pbk[5 + et % 2]
            wait("tensor", "VE", dtw_prev_ve)
            mm(bank[:, 0:512], wdtrow_b[:, 128 * et:128 * (et + 1)],
               dtpre[:, 512 * tc4:512 * (tc4 + 1)], True, True, inc=True)
            wait("scalar", "MM", cnt["MM"])
            wait("scalar", "X", vdw)
            act(scr[:, 0:512], bank[:, 0:512], AF.Abs,
                bias=bias_t["bdt_c"][:, et:et + 1], inc=False)
            act(scr[:, 512:1024], scr[:, 0:512], AF.Exp, scale=-1.0,
                inc=False)
            act(scr[:, 512:1024], scr[:, 512:1024], AF.Ln,
                bias=one_c[:], inc=False)
            act(scr2[:, 0:512], bank[:, 0:512], AF.Copy,
                bias=bias_t["bdt_c"][:, et:et + 1])
            wait("vector", "AC", cnt["AC"])
            ve("scalar_tensor_tensor", scrb[:, 1024:1536], scr2[:, 0:512],
               0.0, scr[:, 512:1024], ALU.max, ALU.add, inc=False)
            ve("tensor_tensor", scrb[:, 1536:2048], scrb[:, 1024:1536],
               xcch[et][:, 4:516], ALU.mult)
            dtw_prev_ve = cnt["VE"]
            wait("scalar", "VE", dtw_prev_ve)
            ld("scalar", dram["dt_sp"][128 * et:128 * (et + 1),
                                       512 * tc4:512 * (tc4 + 1)],
               scrb[:, 1024:1536], "X")
            vdw = ld("scalar", dram["w_sp"][128 * et:128 * (et + 1),
                                            512 * tc4:512 * (tc4 + 1)],
                     scrb[:, 1536:2048], "X")
    dtw_sp = cnt["X"]
    wait("gpsimd", "AC", cnt["AC"])
    ld("gpsimd", dram["bc_cin"][0:16, :], ssm_t[0:16, :], "G")
    gbc = ld("gpsimd", dram["bc_cin"][16:32, :], ssm_t[16:32, :], "G")
    ph.close()

    wait("gpsimd", "X", dtw_sp)
    wait("gpsimd", "G", gbc)
    collective("AllGather", ALU.bypass, dram["dt_sp"][:], dram["dt_cout"][:])
    collective("AllGather", ALU.bypass, dram["w_sp"][:], dram["w_cout"][:])
    cc_bc = collective("AllGather", ALU.bypass, dram["bc_cin"][:],
                       dram["bc_cout"][:])
    wait("gpsimd", "CC", cc_bc)

    def dyn_assemble(e):
        with e.register("jrow") as jr:
            e.reg_load(jr, joff_t[0:1, 0:1])
            off = e.snap(jr)
            e.dma_start(out=dram["dtg_sp"][:, 0:T],
                        in_=dram["dt_cout"][bass.ds(off, EI // 2), :]
                        ).then_inc(sems["G"], 16)
            e.dma_start(out=dram["wg_sp"][:, 0:T],
                        in_=dram["w_cout"][bass.ds(off, EI // 2), :]
                        ).then_inc(sems["G"], 16)
        with e.register("jrow2") as jr2:
            e.reg_load(jr2, joff_t[0:1, 0:1])
            e.reg_add(jr2, jr2, EI)
            off2 = e.snap(jr2)
            e.dma_start(out=dram["dtg_sp"][:, T:2 * T],
                        in_=dram["dt_cout"][bass.ds(off2, EI // 2), :]
                        ).then_inc(sems["G"], 16)
            e.dma_start(out=dram["wg_sp"][:, T:2 * T],
                        in_=dram["w_cout"][bass.ds(off2, EI // 2), :]
                        ).then_inc(sems["G"], 16)
    emit("gpsimd", dyn_assemble)
    cnt["G"] += 64
    g_asm = cnt["G"]
    emit("gpsimd", lambda e, v=g_asm: e.wait_ge(sems["G"], v))

    ph = contextlib.ExitStack()
    ball = sb(ph, "ball", [16, 2 * T], BF16)
    call = sb(ph, "call", [16, 2 * T], BF16)
    dtw = [sb(ph, f"dtw{i}", [128, TC], BF16) for i in range(2)]
    wwt = [sb(ph, f"wwt{i}", [128, TC], BF16) for i in range(2)]
    abuf = [sb(ph, f"abuf{i}", [128, SP], F32) for i in range(2)]
    ubuf = [sb(ph, f"ubuf{i}", [128, SP], BF16) for i in range(2)]
    hsup = sb(ph, "hsup", [128, S * SP], BF16)
    cbsup = sb(ph, "cbsup", [128, S * SP], BF16)
    brsup = sb(ph, "brsup", [128, S * SP], BF16)
    carryt = [sb(ph, f"carry{i}", [128, S], BF16) for i in range(NETH)]
    yred = sb(ph, "yred", [128, TC], F32)
    yredb = sb(ph, "yredb", [128, TC], BF16)
    srep = [sb(ph, f"srep{i}", [128, T], BF16) for i in range(8)]
    glrows = sb(ph, "glrows", [8, T], BF16)
    scrows = sb(ph, "scrows", [8, T], BF16)
    scrow1 = sb(ph, "scrow1", [1, T], BF16)
    sctok = sb(ph, "sctok", [128, 8], BF16)
    gtok = sb(ph, "gtok", [128, 8])
    gcol = sb(ph, "gcol", [128, 1])
    xs_t = [sb(ph, f"xs{i}", [128, 512], BF16) for i in range(NDT)]
    fin_t = [sb(ph, f"fin{i}", [128, 512], BF16) for i in range(NDT)]
    sel16_t = sb(ph, "sel16_t", [16, 16 * 128], BF16)
    moewt = [sb(ph, f"moewt{i}", [128, D], BF16) for i in range(2)]
    ld("sync", sel16_t[:], dram["sel16"][:], "W")

    gb1 = ld("gpsimd", ball[:, 0:T], dram["bc_cout"][0:16, :], "G")
    gb2 = ld("gpsimd", ball[:, T:2 * T], dram["bc_cout"][32:48, :], "G")
    gb3 = ld("gpsimd", call[:, 0:T], dram["bc_cout"][16:32, :], "G")
    g_bc2 = ld("gpsimd", call[:, T:2 * T], dram["bc_cout"][48:64, :], "G")

    for dc in range(NDT):
        ld("sync", wstg[0][:, 8 * dc:8 * (dc + 1)],
           dram["gwT"][128 * dc:128 * (dc + 1), :], "W")
    wait("scalar", "W", cnt["W"])
    wait("scalar", "X", dtw_sp)
    ac_gw = act(scrb[:, 0:64], wstg[0][:, 0:64], AF.Copy)
    wait("tensor", "AC", ac_gw)
    gl_prev = 0
    for tc4 in range(4):
        bank = pbk[4]
        wait("tensor", "AC", gl_prev)
        for dc in range(NDT):
            mm(bank[0:8, 0:512], scrb[:, 8 * dc:8 * (dc + 1)],
               xn[dc][:, 4 + 512 * tc4:4 + 512 * (tc4 + 1)],
               dc == 0, dc == NDT - 1, inc=(dc == NDT - 1))
        wait("scalar", "MM", cnt["MM"])
        gl_prev = act(glrows[:, 512 * tc4:512 * (tc4 + 1)],
                      bank[0:8, 0:512], AF.Copy)
    ac_gl = cnt["AC"]
    wait("tensor", "AC", ac_gl)
    sc_prev_ac = 0
    for tcc in range(16):
        transpose(ptr[:, 0:8], glrows[:, 128 * tcc:128 * (tcc + 1)], kdim=8)
        wait("vector", "MM", cnt["MM"])
        wait("vector", "AC", sc_prev_ac)
        ve("tensor_copy", gtok[:], ptr[:, 0:8], inc=False)
        ve("tensor_reduce", gcol[:], gtok[:], AX.X, ALU.max, negate=True)
        wait("scalar", "VE", cnt["VE"])
        act(gtok[:], gtok[:], AF.Exp, bias=gcol[:])
        wait("vector", "AC", cnt["AC"])
        ve("tensor_reduce", gcol[:], gtok[:], AX.X, ALU.add, inc=False)
        ve("reciprocal", gcol[:], gcol[:], inc=False)
        ve("tensor_scalar", gtok[:], gtok[:], gcol[:], None, ALU.mult,
           inc=False)
        ve("tensor_single_scalar", sctok[:], gtok[:], 0.3, ALU.is_gt,
           inc=False)
        ve("tensor_reduce", gcol[:], sctok[:], AX.X, ALU.add, inc=False)
        ve("memset", sctok[:], 0.0, inc=False)
        ve("tensor_scalar", sctok[:, 1:2], gtok[:, 1:2], gcol[:], None,
           ALU.mult, inc=False)
        ve("tensor_scalar", gcol[:], gcol[:], -1.0, 8.0, ALU.mult, ALU.add,
           inc=False)
        ve("tensor_scalar", sctok[:, 0:1], gtok[:, 0:1], gcol[:], None,
           ALU.mult)
        v_sc = cnt["VE"]
        wait("tensor", "VE", v_sc)
        transpose(ptr[0:8, 896:1024], sctok[:])
        wait("scalar", "MM", cnt["MM"])
        sc_prev_ac = act(scrows[:, 128 * tcc:128 * (tcc + 1)],
                         ptr[0:8, 896:1024], AF.Copy)
    ac_scrow = cnt["AC"]
    wait("tensor", "AC", ac_scrow)
    srep_prev = 0
    for e8 in range(8):
        gvv = ld("gpsimd", scrow1[:], scrows[e8:e8 + 1, :], "G")
        wait("tensor", "G", gvv)
        for tc4 in range(4):
            wait("tensor", "AC", srep_prev)
            mm(pbk[5][:, 0:512], onesrow_b[:],
               scrow1[:, 512 * tc4:512 * (tc4 + 1)], True, True, inc=True)
            wait("scalar", "MM", cnt["MM"])
            srep_prev = act(srep[e8][:, 512 * tc4:512 * (tc4 + 1)],
                            pbk[5][:, 0:512], AF.Copy)
        wait("gpsimd", "AC", srep_prev)
    ac_srep = cnt["AC"]

    wc_prev = [0, 0]
    for widx in range(9):
        for dc in range(NDT):
            for hf in range(2):
                par = hf
                wait("sync", "X", wc_prev[par])
                wsrc = (dram["expT"][widx, 128 * dc:128 * (dc + 1),
                                     512 * hf:512 * (hf + 1)]
                        if widx < 8 else
                        dram["outwT"][128 * dc:128 * (dc + 1),
                                      512 * hf:512 * (hf + 1)])
                ld("sync", wstg[par][:], wsrc, "W")
                wait("scalar", "W", cnt["W"])
                ac_c = act(scrb[:, 2048 + 512 * par:2048 + 512 * (par + 1)],
                           wstg[par][:], AF.Copy)
                wait("scalar", "AC", ac_c)
                wc_prev[par] = ld(
                    "scalar",
                    dram["wbf_sp"][widx, 128 * dc:128 * (dc + 1),
                                   512 * hf:512 * (hf + 1)],
                    scrb[:, 2048 + 512 * par:2048 + 512 * (par + 1)], "X")
    wcache_done = cnt["X"]

    wait("gpsimd", "AC", ac_srep)
    wait("vector", "G", g_asm)
    wait("vector", "G", max(gb1, gb2, gb3, g_bc2))
    wait("vector", "AC", ac_acol)
    wait("scalar", "G", g_asm)
    wait("tensor", "AC", ac_srep)
    wait("tensor", "X", wcache_done)

    dtw_guard_ac = [0, 0]
    dtw_guard_ve = [0, 0]
    moewt_guard = [0, 0]
    scan_hist = []
    sup_ve_prev = 0
    g2_prev_ac = 0
    gp_hist = []
    y_done = 0
    g2_sp_v = 0

    def emit_moe_chunk(mtc):
        nonlocal g2_prev_ac, g2_sp_v
        c0 = 4 + 512 * mtc
        for e8 in range(N_EXP):
            si = mtc * N_EXP + e8
            if len(gp_hist) >= 1:
                wait("gpsimd", "MM", gp_hist[-1][1])
            for dc in range(NDT):
                ve("tensor_tensor", xs_t[dc][:], xn[dc][:, c0:c0 + 512],
                   srep[e8][:, 512 * mtc:512 * (mtc + 1)], ALU.mult,
                   eng="gpsimd", inc=(dc == NDT - 1))
            v_xs = cnt["GP"]
            for dc in range(NDT):
                wpar = (si * NDT + dc) % 2
                wait("sync", "MM", moewt_guard[wpar])
                ld("sync", moewt[wpar][:],
                   dram["wbf_sp"][e8, 128 * dc:128 * (dc + 1), :], "W")
                wait("tensor", "W", cnt["W"])
                if dc == 0:
                    wait("tensor", "GP", v_xs)
                if mtc > 0 and e8 == 0 and dc == 0:
                    wait("tensor", "AC", g2_prev_ac)
                for dt_o in range(NDT):
                    mm(pbk[dt_o][:, 0:512],
                       moewt[wpar][:, 128 * dt_o:128 * (dt_o + 1)],
                       xs_t[dc][:], e8 == 0 and dc == 0,
                       e8 == N_EXP - 1 and dc == NDT - 1,
                       inc=(dt_o == NDT - 1))
                moewt_guard[wpar] = cnt["MM"]
            gp_hist.append((v_xs, cnt["MM"]))
        wait("scalar", "MM", cnt["MM"])
        for dt_o in range(NDT):
            act(fin_t[dt_o][:], pbk[dt_o][:, 0:512], AF.Copy,
                inc=(dt_o == NDT - 1))
        ac_fin = cnt["AC"]
        for dc in range(NDT):
            wpar = dc % 2
            wait("sync", "MM", moewt_guard[wpar])
            ld("sync", moewt[wpar][:],
               dram["wbf_sp"][8, 128 * dc:128 * (dc + 1), :], "W")
            wait("tensor", "W", cnt["W"])
            if dc == 0:
                wait("tensor", "AC", ac_fin)
            for dt_o in range(NDT):
                mm(pbk[dt_o][:, 0:512],
                   moewt[wpar][:, 128 * dt_o:128 * (dt_o + 1)],
                   fin_t[dc][:], dc == 0, dc == NDT - 1,
                   inc=(dt_o == NDT - 1))
            moewt_guard[wpar] = cnt["MM"]
        wait("scalar", "MM", cnt["MM"])
        for dt_o in range(NDT):
            wait("scalar", "X", g2_sp_v)
            act(scrb[:, 0:512], pbk[dt_o][:, 0:512], AF.Silu)
            wait("scalar", "AC", cnt["AC"])
            g2_sp_v = ld("scalar",
                         dram["g2_sp"][128 * dt_o:128 * (dt_o + 1),
                                       512 * mtc:512 * (mtc + 1)],
                         scrb[:, 0:512], "X")
        g2_prev_ac = cnt["AC"]

    for tcg in range(NTC):
        c0g = TC * tcg
        sup_ac0 = 0
        for s_ in range(S):
            for src_t, dst_sup in ((ball, brsup), (call, cbsup)):
                wait("tensor", "AC", sup_ac0)
                if tcg > 0:
                    wait("tensor", "VE", sup_ve_prev)
                mm(pbk[6][:, 0:512], sel16_t[:, 128 * s_:128 * (s_ + 1)],
                   src_t[:, c0g:c0g + TC], True, True, inc=True)
                wait("scalar", "MM", cnt["MM"])
                if tcg > 0:
                    wait("scalar", "VE", sup_ve_prev)
                sup_ac0 = act(dst_sup[:, SP * s_ + 1:SP * (s_ + 1)],
                              pbk[6][:, 0:TC], AF.Copy)
        ac_sup = cnt["AC"]

        if tcg % 2 == 0:
            emit_moe_chunk(tcg // 2)

        for et in range(NETH):
            par = et % 2
            wait("scalar", "AC", dtw_guard_ac[par])
            wait("scalar", "VE", dtw_guard_ve[par])
            ld("scalar", dtw[par][:],
               dram["dtg_sp"][128 * et:128 * (et + 1), c0g:c0g + TC], "X")
            vww = ld("scalar", wwt[par][:],
                     dram["wg_sp"][128 * et:128 * (et + 1), c0g:c0g + TC],
                     "X")
            wait("scalar", "X", vww)
            wait("vector", "X", vww)
            wait("vector", "AC", ac_sup)
            if tcg == 0:
                ve("memset", carryt[et][:], 0.0, inc=False)
                ve("memset", abuf[0][:, 0:1], 0.0, inc=False)
                ve("memset", abuf[1][:, 0:1], 0.0)
            for s_ in range(S):
                sp = s_ % 2
                if len(scan_hist) >= 2:
                    wait("scalar", "VE", scan_hist[-2])
                act(abuf[sp][:, 1:SP], dtw[par][:], AF.Exp,
                    scale=acols[:, s_:s_ + 1])
                ac_a = cnt["AC"]
                ve("tensor_tensor", ubuf[sp][:, 1:SP], wwt[par][:],
                   brsup[:, SP * s_ + 1:SP * (s_ + 1)], ALU.mult, inc=False)
                ve("tensor_copy", ubuf[sp][:, 0:1],
                   carryt[et][:, s_:s_ + 1], inc=False)
                wait("vector", "AC", ac_a)
                ve("tensor_tensor_scan", hsup[:, SP * s_:SP * (s_ + 1)],
                   abuf[sp][:], ubuf[sp][:], 0.0, ALU.mult, ALU.add)
                scan_hist.append(cnt["VE"])
            v_scans = cnt["VE"]
            dtw_guard_ve[par] = v_scans
            wait("scalar", "VE", v_scans)
            ac_carry = act(
                carryt[et][:],
                hsup[:].rearrange("p (s c) -> p s c", c=SP)[:, :, SP - 1:SP]
                .rearrange("p s c -> p (s c)"), AF.Copy)
            dtw_guard_ac[par] = ac_carry
            wait("vector", "AC", ac_carry)
            for s_ in range(S):
                ve("tensor_tensor", hsup[:, SP * s_ + 1:SP * (s_ + 1)],
                   hsup[:, SP * s_ + 1:SP * (s_ + 1)],
                   cbsup[:, SP * s_ + 1:SP * (s_ + 1)], ALU.mult,
                   inc=(s_ == S - 1))
            ve("tensor_reduce", yred[:],
               hsup[:].rearrange("p (s c) -> p s c", c=SP)[:, :, 1:SP]
               .rearrange("p s c -> p c s"), AX.X, ALU.add)
            v_y = cnt["VE"]
            sup_ve_prev = v_y
            wait("scalar", "VE", v_y)
            ac_y = act(yredb[:], yred[:], AF.Copy)
            wait("scalar", "AC", ac_y)
            y_done = ld("scalar", dram["y_sp"][128 * et:128 * (et + 1),
                                               c0g:c0g + TC], yredb[:], "X")
            wait("vector", "AC", ac_y)
    ph.close()

    ph = contextlib.ExitStack()
    wosl = [sb(ph, f"wosl{i}", [128, 2 * D], BF16) for i in range(NDT)]
    zt = [sb(ph, f"zt{i}", [128, T], BF16) for i in range(NET)]
    g2t = [sb(ph, f"g2t{i}", [128, 512], BF16) for i in range(2)]
    x1t = [sb(ph, f"x1t{i}", [128, 512], BF16) for i in range(2)]

    wait("gpsimd", "X", y_done)
    cc_y = collective("AllGather", ALU.bypass, dram["y_sp"][:],
                      dram["y_cout"][:])
    wait("gpsimd", "CC", cc_y)

    def dyn_ymine(e):
        with e.register("jc2") as jr:
            e.reg_load(jr, joff_t[0:1, 1:2])
            off = e.snap(jr)
            e.dma_start(out=dram["ymine_sp"][:],
                        in_=dram["y_cout"][:, bass.ds(off, T)]
                        ).then_inc(sems["G"], 16)
    emit("gpsimd", dyn_ymine)
    cnt["G"] += 16
    g_ym = cnt["G"]
    emit("gpsimd", lambda e, v=g_ym: e.wait_ge(sems["G"], v))

    def wout_slot(i):
        return wosl[i % 8][:, (i // 8) * D:(i // 8 + 1) * D]

    wg4 = [0, 0]
    ac_wout = 0
    for i in range(NET):
        for hf in range(2):
            par = hf
            wait("sync", "AC", wg4[par])
            ld("sync", wstg[par][:],
               dram["woutT"][128 * i:128 * (i + 1),
                             512 * hf:512 * (hf + 1)], "W")
            wait("scalar", "W", cnt["W"])
            ac_wout = act(wout_slot(i)[:, 512 * hf:512 * (hf + 1)],
                          wstg[par][:], AF.Copy)
            wg4[par] = ac_wout

    wait("scalar", "G", g_ym)
    z_prev = 0
    for i in range(NET):
        wait("scalar", "AC", z_prev)
        ld("scalar", zt[i][:], dram["xc_sp"][128 * i:128 * (i + 1), :], "X")
        ld("scalar", scrb[:, 0:T],
           dram["ymine_sp"][128 * i:128 * (i + 1), :], "X")
        vrs = ld("scalar", scrb[:, 2048:2048 + T],
                 dram["res_sp"][128 * i:128 * (i + 1), :], "X")
        wait("vector", "X", vrs)
        ve("scalar_tensor_tensor", scrb[:, 0:T], zt[i][:],
           bias_t["dd_c"][:, i:i + 1], scrb[:, 0:T], ALU.mult, ALU.add,
           inc=False)
        ve("tensor_tensor", scrb[:, 0:T], scrb[:, 0:T],
           scrb[:, 2048:2048 + T], ALU.add)
        wait("scalar", "VE", cnt["VE"])
        z_prev = act(zt[i][:], scrb[:, 0:T], AF.Silu)
    ac_z = cnt["AC"]

    wait("tensor", "AC", max(ac_z, ac_wout))
    out_prev_g = [0, 0]
    for dt_o in range(NDT):
        for tc4 in range(4):
            par = tc4 % 2
            bank = pbk[par]
            wait("scalar", "G", out_prev_g[par])
            vg2 = ld("scalar", g2t[par][:],
                     dram["g2_sp"][128 * dt_o:128 * (dt_o + 1),
                                   512 * tc4:512 * (tc4 + 1)], "X")
            vx1 = ld("scalar", x1t[par][:],
                     dram["x1_sp"][128 * dt_o:128 * (dt_o + 1),
                                   4 + 512 * tc4:4 + 512 * (tc4 + 1)], "X")
            wait("tensor", "G", out_prev_g[par])
            wait("tensor", "VE", cnt["VE"])
            for ec in range(NET):
                mm(bank[:, 0:512],
                   wout_slot(ec)[:, 128 * dt_o:128 * (dt_o + 1)],
                   zt[ec][:, 512 * tc4:512 * (tc4 + 1)], ec == 0,
                   ec == NET - 1, inc=(ec == NET - 1))
            wait("vector", "MM", cnt["MM"])
            wait("vector", "X", vx1)
            wait("vector", "G", out_prev_g[par])
            ve("tensor_tensor", scr[:, 512 * par:512 * (par + 1)],
               bank[:, 0:512], g2t[par][:], ALU.mult, inc=False)
            ve("tensor_tensor", scr2[:, 512 * par:512 * (par + 1)],
               x1t[par][:], scr[:, 512 * par:512 * (par + 1)], ALU.add)
            wait("gpsimd", "VE", cnt["VE"])
            out_prev_g[par] = ld(
                "gpsimd",
                dram["out_t"][128 * dt_o:128 * (dt_o + 1),
                              512 * tc4:512 * (tc4 + 1)],
                scr2[:, 512 * par:512 * (par + 1)], "G")
    out_done = cnt["G"]
    wait("gpsimd", "G", out_done)
    ph.close()

    with nc.Block() as block:
        @block.sync
        def _(e):
            for f in thunks["sync"]:
                f(e)

        @block.scalar
        def _(e):
            for f in thunks["scalar"]:
                f(e)

        @block.vector
        def _(e):
            for f in thunks["vector"]:
                f(e)

        @block.tensor
        def _(e):
            for f in thunks["tensor"]:
                f(e)

        @block.gpsimd
        def _(e):
            for f in thunks["gpsimd"]:
                f(e)

    es.close()
    return nc


_NC_CACHE = {}


def kernel(**inputs):
    in_maps = _host_prepare(inputs)
    if "nc" not in _NC_CACHE:
        _NC_CACHE["nc"] = build_program()
    nc = _NC_CACHE["nc"]
    res = run_bass_kernel_spmd(nc, in_maps, core_ids=list(range(8)))
    out = np.zeros((4, 4096, 1024), np.float32)
    for c in range(8):
        b, j = c // 2, c % 2
        out[b, j * T:(j + 1) * T] = np.asarray(res.results[c]["out_t"]).T
    return out




# version 03
# speedup vs baseline: 1.1109x; 1.1109x over previous
import contextlib
import numpy as np
import ml_dtypes
import concourse.bass as bass
import concourse.mybir as mybir
from concourse.bass_utils import run_bass_kernel_spmd

F32 = mybir.dt.float32
BF16 = mybir.dt.bfloat16
I32 = mybir.dt.int32
AF = mybir.ActivationFunctionType
ALU = mybir.AluOpType
AX = mybir.AxisListType
BFNP = ml_dtypes.bfloat16

D = 1024
T = 2048
TF = T + 4
H = 16
MFT = 256
S = 16
EI = 2048
TC = 512
SP = TC + 1
NTC = (2 * T) // TC
NDT = 8
NET = 16
NETH = 8
EPS_LN = 1e-5
EPS_PHI = 1e-6
S14 = float(1.0 / (64.0 ** 0.25))
COMPUTE_ALL_EXPERTS = False
N_EXP = 8 if COMPUTE_ALL_EXPERTS else 2
PAIRS = [[0, 1], [2, 3], [4, 5], [6, 7]]


def _host_prepare(inputs):
    f = lambda a: np.ascontiguousarray(np.asarray(a, np.float32))
    x = f(inputs["L0_attn_x"])
    big = {
        "wqT": f(inputs["L0_attn_Wq"]).T, "wkT": f(inputs["L0_attn_Wk"]).T,
        "wvT": f(inputs["L0_attn_Wv"]).T, "woT": f(inputs["L0_attn_Wo"]).T,
        "wrT": f(inputs["L0_ffn_W_r"]).T, "wkfT": f(inputs["L0_ffn_W_k"]).T,
        "wvfT": f(inputs["L0_ffn_W_v"]).T,
        "winT": f(inputs["L1_attn_W_in"]).T,
        "woutT": f(inputs["L1_attn_W_out"]).T,
        "outwT": f(inputs["L1_ffn_out_W"]).T,
        "gwT": f(inputs["L1_ffn_gate_W"]).T,
        "wxT": f(inputs["L1_attn_W_x"]).T,
    }
    big = {k: np.ascontiguousarray(v) for k, v in big.items()}
    big["expT"] = np.ascontiguousarray(
        np.transpose(f(inputs["L1_ffn_expert_W"]), (0, 2, 1)))
    bcol = lambda v, n: np.ascontiguousarray(f(v).reshape(n, 128).T)
    sm = {
        "bq_c": bcol(inputs["L0_attn_bq"], NDT),
        "bk_c": bcol(inputs["L0_attn_bk"], NDT),
        "bo_c": bcol(inputs["L0_attn_bo"], NDT),
        "bv_r": f(inputs["L0_attn_bv"]).reshape(1, D),
        "mur_c": bcol(inputs["L0_ffn_mu_r"], NDT),
        "muk_c": bcol(inputs["L0_ffn_mu_k"], NDT),
        "convw_c": np.ascontiguousarray(
            f(inputs["L1_attn_conv_w"])[:, 0, :].reshape(NET, 128, 4)
            .transpose(1, 0, 2).reshape(128, NET * 4)),
        "convb_c": bcol(inputs["L1_attn_conv_b"], NET),
        "wdt_r": f(inputs["L1_attn_W_dt"])[:, 0].reshape(1, EI),
        "bdt_c": bcol(inputs["L1_attn_b_dt"], NET),
        "alog_r": f(inputs["L1_attn_A_log"]).reshape(1, S),
        "dd_c": bcol(inputs["L1_attn_D"], NET),
    }
    orf = f(inputs["L0_attn_orf"])
    ident = np.eye(128, dtype=BFNP)
    orfpad = np.zeros((2, 128, MFT), BFNP)
    for h2 in range(2):
        orfpad[h2, 64 * h2:64 * h2 + 64, :] = orf.T.astype(BFNP)
    sel16 = np.zeros((16, 16 * 128), BFNP)
    for s_ in range(16):
        sel16[s_, 128 * s_:128 * (s_ + 1)] = 1
    bd2 = np.zeros((128, 2), BFNP)
    bd2[0:64, 0] = 1
    bd2[64:128, 1] = 1

    in_maps = []
    for c in range(8):
        b, j = c // 2, c % 2
        x_t = np.ascontiguousarray(x[b, j * T:(j + 1) * T].T)
        halo = (np.zeros((D, 4), np.float32) if j == 0
                else np.ascontiguousarray(x[b, T - 4:T].T))
        m = {"x_t": x_t, "x_halo": halo, "ident": ident, "orfpad": orfpad,
             "sel16": sel16, "bd2": bd2,
             "core_j": np.full((128, 1), float(j), np.float32),
             "joff": np.array([[1024 * j, 2048 * j]], np.int32)}
        m.update(big)
        m.update(sm)
        in_maps.append(m)
    return in_maps


def build_program():
    nc = bass.Bass("TRN2", target_bir_lowering=False)
    es = contextlib.ExitStack()
    dram = {}

    def din(name, shape, dt=F32):
        dram[name] = nc.dram_tensor(name, shape, dt, kind="ExternalInput").ap()

    def dout(name, shape, dt=F32):
        dram[name] = nc.dram_tensor(name, shape, dt,
                                    kind="ExternalOutput").ap()

    def dint(name, shape, dt=F32):
        dram[name] = nc.dram_tensor(name, shape, dt).ap()

    din("x_t", [D, T]); din("x_halo", [D, 4])
    for w in ("wqT", "wkT", "wvT", "woT", "wrT", "wkfT", "wvfT", "outwT"):
        din(w, [D, D])
    din("winT", [D, 4 * D]); din("woutT", [EI, D]); din("gwT", [D, 8])
    din("wxT", [EI, 33]); din("expT", [8, D, D])
    for nm, shp in (("bq_c", [128, NDT]), ("bk_c", [128, NDT]),
                    ("bo_c", [128, NDT]), ("bv_r", [1, D]),
                    ("mur_c", [128, NDT]), ("muk_c", [128, NDT]),
                    ("convw_c", [128, NET * 4]), ("convb_c", [128, NET]),
                    ("wdt_r", [1, EI]), ("bdt_c", [128, NET]),
                    ("alog_r", [1, S]), ("dd_c", [128, NET]),
                    ("core_j", [128, 1])):
        din(nm, shp)
    din("joff", [1, 2], I32)
    din("ident", [128, 128], BF16); din("orfpad", [2, 128, MFT], BF16)
    din("sel16", [16, 16 * 128], BF16); din("bd2", [128, 2], BF16)
    dout("out_t", [D, T])

    dint("phiq_sp", [T, H * MFT], BF16)
    dint("x1_sp", [D, TF], BF16)
    dint("res_sp", [EI, T], BF16)
    dint("xc_sp", [EI, T], BF16)
    dint("dt_sp", [EI, T], BF16)
    dint("w_sp", [EI, T], BF16)
    dint("y_sp", [EI // 2, 2 * T], BF16)
    dint("g2_sp", [D, T], BF16)
    dint("wbf_sp", [9, D, D], BF16)
    dint("winbf_sp", [D, 4 * D], BF16)
    dint("dtg_sp", [EI // 2, 2 * T], BF16)
    dint("wg_sp", [EI // 2, 2 * T], BF16)
    dint("ymine_sp", [EI, T], BF16)
    dint("kv_cin", [128, H * 2 * 65]); dint("kv_cout", [128, H * 2 * 65])
    dint("h4_cin", [D, 4], BF16); dint("h4_cout", [2 * D, 4], BF16)
    dint("dt_cout", [2 * EI, T], BF16)
    dint("w_cout", [2 * EI, T], BF16)
    dint("bc_cin", [32, T], BF16); dint("bc_cout", [64, T], BF16)
    dint("y_cout", [EI, 2 * T], BF16)

    thunks = {e: [] for e in ("sync", "scalar", "vector", "tensor", "gpsimd")}
    cnt = {}
    sems = {}
    for s_ in ("W", "X", "G", "MM", "AC", "VE", "GP", "CC"):
        sems[s_] = es.enter_context(nc.semaphore(s_))
        cnt[s_] = 0

    def emit(eng, fn):
        thunks[eng].append(fn)

    DMA_SEMS = ("W", "X", "G")
    QUEUE_OF = {"W": "sync", "X": "scalar", "G": "gpsimd"}
    awaited = {s_: set() for s_ in DMA_SEMS}
    ENG_SEM = {"sync": "W", "scalar": "AC", "vector": "VE", "tensor": "MM",
               "gpsimd": "GP"}

    writers = {}
    readers = {}
    watermark = {e: {} for e in thunks}
    mm_pending_reads = []

    def _tname(ap):
        try:
            return ap.tensor.name
        except AttributeError:
            return None

    def wait(eng, sem, val):
        if val <= 0:
            return
        if watermark[eng].get(sem, 0) >= val:
            return
        watermark[eng][sem] = val
        if sem in DMA_SEMS:
            awaited[sem].add(val)
        emit(eng, lambda e, s=sems[sem], v=val: e.wait_ge(s, v))

    def dep_wait(eng, read_names, write_names):
        for nm in read_names:
            w = writers.get(nm)
            if w is not None:
                ws, wv = w
                if not (eng == "tensor" and ws == "MM"):
                    wait(eng, ws, wv)
        for nm in write_names:
            w = writers.get(nm)
            if w is not None:
                ws, wv = w
                if not (eng == "tensor" and ws == "MM"):
                    wait(eng, ws, wv)
            for rs, rv in readers.get(nm, {}).items():
                if not (eng == "tensor" and rs == "MM"):
                    wait(eng, rs, rv)

    def dep_record(eng, sem, val, read_names, write_names):
        for nm in read_names:
            readers.setdefault(nm, {})[sem] = val
        for nm in write_names:
            writers[nm] = (sem, val)
            if nm in readers:
                readers[nm] = {}

    def ld(eng, dst, src, sem=None):
        if sem is None:
            sem = {"sync": "W", "scalar": "X", "gpsimd": "G"}[eng]
        assert QUEUE_OF.get(sem) == eng, (eng, sem)
        rn = [_tname(src)]
        wn = [_tname(dst)]
        dep_wait(eng, rn, wn)
        cnt[sem] += 16
        val = cnt[sem]
        dep_record(eng, sem, val, rn, wn)
        emit(eng, lambda e, d=dst, s=src, sh=sems[sem]:
             e.dma_start(out=d, in_=s).then_inc(sh, 16))

        def barrier(e, s=sems[sem], v=val, snm=sem):
            if v in awaited[snm]:
                e.wait_ge(s, v)
        emit(eng, barrier)
        return val

    def mm(outap, lhs, rhs, start, stop, inc=False):
        rn = [_tname(lhs), _tname(rhs)]
        wn = [_tname(outap)]
        if not start:
            rn.append(_tname(outap))
        dep_wait("tensor", rn, wn)
        mm_pending_reads.extend(rn)
        v = None
        if inc:
            cnt["MM"] += 1
            v = cnt["MM"]
            for nm in set(mm_pending_reads):
                readers.setdefault(nm, {})["MM"] = v
            mm_pending_reads.clear()
            writers[wn[0]] = ("MM", v)

        def f(e, o=outap, l=lhs, r=rhs, st=start, sp=stop, vv=v):
            i = e.matmul(o, l, r, start=st, stop=sp)
            if vv is not None:
                i.then_inc(sems["MM"], 1)
        emit("tensor", f)
        return cnt["MM"]

    def act(outap, inap, func, bias=0.0, scale=1.0, inc=True):
        rn = [_tname(inap)]
        if not isinstance(bias, float):
            rn.append(_tname(bias))
        if not isinstance(scale, float):
            rn.append(_tname(scale))
        wn = [_tname(outap)]
        dep_wait("scalar", rn, wn)
        cnt["AC"] += 1
        v = cnt["AC"]
        dep_record("scalar", "AC", v, rn, wn)

        def f(e, o=outap, i=inap, fu=func, b=bias, s=scale, vv=v):
            if fu == AF.Copy and not isinstance(b, float):
                fu = AF.Identity
            e.activation(o, i, fu, bias=b, scale=s).then_inc(sems["AC"], 1)
        emit("scalar", f)
        return cnt["AC"]

    def ve(op, *args, inc=True, eng="vector", **kw):
        sem = "VE" if eng == "vector" else "GP"
        rn = [_tname(a) for a in args[1:] if hasattr(a, "tensor")]
        rn += [_tname(a) for a in kw.values() if hasattr(a, "tensor")]
        wn = [_tname(args[0])] if hasattr(args[0], "tensor") else []
        if op == "memset":
            rn = []
        dep_wait(eng, rn, wn)
        cnt[sem] += 1
        v = cnt[sem]
        dep_record(eng, sem, v, rn, wn)

        def f(e, o=op, a=args, k=kw, vv=v, sh=sems[sem]):
            getattr(e, o)(*a, **k).then_inc(sh, 1)
        emit(eng, f)
        return cnt[sem]

    def transpose(outap, inap, kdim=128):
        rn = [_tname(inap), "ident_t"]
        wn = [_tname(outap)]
        dep_wait("tensor", rn, wn)
        cnt["MM"] += 1
        v = cnt["MM"]
        dep_record("tensor", "MM", v, rn, wn)

        def f(e, o=outap, i=inap, vv=v, k=kdim):
            e.transpose(o, i, ident_t[0:k, 0:k]).then_inc(sems["MM"], 1)
        emit("tensor", f)
        return v

    def collective(kind, op, in_ap, out_ap):
        rn = [_tname(in_ap)]
        wn = [_tname(out_ap)]
        dep_wait("gpsimd", rn, wn)
        cnt["CC"] += 1
        v = cnt["CC"]
        dep_record("gpsimd", "CC", v, rn, wn)

        def f(e, i=in_ap, o=out_ap, k=kind, a=op):
            e.collective_compute(k, a, replica_groups=PAIRS, ins=[i],
                                 outs=[o]).then_inc(sems["CC"], 1)
        emit("gpsimd", f)
        return cnt["CC"]

    def sb(stack, name, shape, dt=F32):
        return stack.enter_context(nc.sbuf_tensor(name, shape, dt))

    xn = [sb(es, f"xn{i}", [128, TF], BF16) for i in range(NDT)]
    wstg = [sb(es, f"wstg{i}", [128, 512], F32) for i in range(2)]
    scr = sb(es, "scr", [128, 1024], F32)
    scr2 = sb(es, "scr2", [128, 1024], F32)
    scrb = sb(es, "scrb", [128, 4160], BF16)
    onescol_b = sb(es, "onescol_b", [128, 1], BF16)
    onesrow_b = sb(es, "onesrow_b", [1, 128], BF16)
    ident_t = sb(es, "ident_t", [128, 128], BF16)
    bd2_t = sb(es, "bd2_t", [128, 2], BF16)
    bias_t = {nm: sb(es, nm + "_t", [128, 16]) for nm in
              ("bq_c", "bk_c", "bo_c", "mur_c", "muk_c", "convb_c", "bdt_c",
               "dd_c")}
    convw_t = sb(es, "convw_t", [128, NET * 4])
    bvrow_b = sb(es, "bvrow_b", [1, D], BF16)

    arow = sb(es, "arow", [1, S]); arow_b = sb(es, "arow_b", [1, S], BF16)
    acols = sb(es, "acols", [128, S])
    recipc = sb(es, "recipc", [128, 1])
    jcol = sb(es, "jcol", [128, 1])
    epsln_c = sb(es, "epsln_c", [128, 1])
    one_c = sb(es, "one_c", [128, 1])
    joff_t = sb(es, "joff_t", [1, 2], I32)
    wdtrow_b = sb(es, "wdtrow_b", [1, EI], BF16)

    pbk = [es.enter_context(nc.psum_tensor(f"pb{i}", [128, 512], F32))
           for i in range(8)]
    ptr = pbk[1].bitcast(BF16)

    ld("sync", ident_t[:], dram["ident"][:], "W")
    ld("sync", bd2_t[:], dram["bd2"][:], "W")
    widths = {"bq_c": NDT, "bk_c": NDT, "bo_c": NDT, "mur_c": NDT,
              "muk_c": NDT, "convb_c": NET, "bdt_c": NET, "dd_c": NET}
    for nm, wdt in widths.items():
        ld("sync", bias_t[nm][:, 0:wdt], dram[nm][:], "W")
    ld("sync", convw_t[:], dram["convw_c"][:], "W")
    ld("sync", scr[0:1, 0:1024], dram["bv_r"][:], "W")
    ld("sync", scr2[0:1, 0:1024], dram["wdt_r"][:, 0:1024], "W")
    ld("sync", wstg[0][0:1, :], dram["wdt_r"][:, 1024:1536], "W")
    ld("sync", wstg[1][0:1, :], dram["wdt_r"][:, 1536:2048], "W")
    ld("sync", arow[:], dram["alog_r"][:], "W")
    w_const = ld("sync", jcol[:], dram["core_j"][:], "W")
    g_joff = ld("gpsimd", joff_t[:], dram["joff"][:], "G")

    ve("memset", onescol_b[:], 1.0, inc=False)
    ve("memset", epsln_c[:], EPS_LN, inc=False)
    ve("memset", one_c[:], 1.0, inc=False)
    ve("memset", onesrow_b[:], 1.0)
    v_init = cnt["VE"]
    wait("vector", "W", w_const)
    ve("tensor_copy", bvrow_b[:], scr[0:1, 0:1024], inc=False)
    ve("tensor_copy", wdtrow_b[:, 0:1024], scr2[0:1, 0:1024], inc=False)
    ve("tensor_copy", wdtrow_b[:, 1024:1536], wstg[0][0:1, :], inc=False)
    ve("tensor_copy", wdtrow_b[:, 1536:2048], wstg[1][0:1, :], inc=False)
    ve("tensor_copy", arow_b[:], arow[:])
    v_rows0 = cnt["VE"]
    wait("tensor", "VE", v_rows0)
    mm(pbk[0][:, 0:S], onesrow_b[:], arow_b[:], True, True, inc=True)
    wait("scalar", "MM", cnt["MM"])
    ac_acol = act(acols[:], pbk[0][:, 0:S], AF.Exp)
    wait("scalar", "AC", ac_acol)
    ac_acol = act(acols[:], acols[:], AF.Copy, scale=-1.0)

    def emit_ln(tiles, rows, pre_sem, pre_val):
        mrow, m2row, mrow_b, rrow_b = rows
        chunks = [(0, 4), (4, 516), (516, 1028), (1028, 1540), (1540, 2052)]
        wait("scalar", pre_sem, pre_val)
        for (c0, c1) in chunks:
            n = c1 - c0
            wait("scalar", "MM", cnt["MM"])
            wait("scalar", "VE", cnt["VE"])
            for i in range(NDT):
                act(scrb[:, 516 * i:516 * i + n], tiles[i][:, c0:c1],
                    AF.Square, inc=(i == NDT - 1))
            ac_sq = cnt["AC"]
            wait("tensor", "AC", ac_sq)
            for i in range(NDT):
                mm(pbk[4][0:1, 0:n], onescol_b[:], tiles[i][:, c0:c1],
                   i == 0, i == NDT - 1, inc=(i == NDT - 1))
            for i in range(NDT):
                mm(pbk[5][0:1, 0:n], onescol_b[:],
                   scrb[:, 516 * i:516 * i + n], i == 0, i == NDT - 1,
                   inc=(i == NDT - 1))
            wait("scalar", "MM", cnt["MM"])
            act(mrow[:, 0:n], pbk[4][0:1, 0:n], AF.Copy, scale=1.0 / D,
                inc=False)
            act(m2row[:, 0:n], pbk[5][0:1, 0:n], AF.Copy, scale=1.0 / D)
            ac_rows = cnt["AC"]
            wait("vector", "AC", ac_rows)
            ve("tensor_tensor", scr[0:1, 0:n], mrow[:, 0:n], mrow[:, 0:n],
               ALU.mult, inc=False)
            ve("tensor_tensor", m2row[:, 0:n], m2row[:, 0:n],
               scr[0:1, 0:n], ALU.subtract)
            wait("scalar", "VE", cnt["VE"])
            act(m2row[:, 0:n], m2row[:, 0:n], AF.Sqrt, bias=epsln_c[0:1, :])
            wait("vector", "AC", cnt["AC"])
            ve("reciprocal", m2row[:, 0:n], m2row[:, 0:n], inc=False)
            ve("tensor_copy", mrow_b[:, 0:n], mrow[:, 0:n], inc=False)
            ve("tensor_copy", rrow_b[:, 0:n], m2row[:, 0:n])
            wait("tensor", "VE", cnt["VE"])
            mm(pbk[6][:, 0:n], onesrow_b[:], mrow_b[:, 0:n], True, True,
               inc=False)
            mm(pbk[7][:, 0:n], onesrow_b[:], rrow_b[:, 0:n], True, True,
               inc=True)
            wait("vector", "MM", cnt["MM"])
            for i in range(NDT):
                ve("tensor_tensor", scr[:, 0:n], tiles[i][:, c0:c1],
                   pbk[6][:, 0:n], ALU.subtract, inc=False)
                ve("tensor_tensor", xn[i][:, c0:c1], scr[:, 0:n],
                   pbk[7][:, 0:n], ALU.mult, inc=(i == NDT - 1))
        return cnt["VE"]

    def load_w_slab(wname, slabs, col0, guards):
        last = 0
        for i in range(NDT):
            for hf in range(2):
                par = (2 * i + hf) % 2
                wait("sync", "AC", guards[par])
                ld("sync", wstg[par][:],
                   dram[wname][128 * i:128 * (i + 1),
                               512 * hf:512 * (hf + 1)], "W")
                wait("scalar", "W", cnt["W"])
                last = act(slabs[i][:, col0 + 512 * hf:col0 + 512 * (hf + 1)],
                           wstg[par][:], AF.Copy)
                guards[par] = last
        return last

    ph = contextlib.ExitStack()
    xb = [sb(ph, f"xb{i}", [128, TF], BF16) for i in range(NDT)]
    wsl = [sb(ph, f"wsl{i}", [128, 3 * D], BF16) for i in range(NDT)]
    mrow = sb(ph, "mrow", [1, 516]); m2row = sb(ph, "m2row", [1, 516])
    mrow_b = sb(ph, "mrow_b", [1, 516], BF16)
    rrow_b = sb(ph, "rrow_b", [1, 516], BF16)
    rows = (mrow, m2row, mrow_b, rrow_b)
    qg = [sb(ph, f"qg{i}", [128, 512], BF16) for i in range(NDT)]
    kg = [sb(ph, f"kg{i}", [128, 512], BF16) for i in range(NDT)]
    sq2 = sb(ph, "sq2", [128, H]); sk2 = sb(ph, "sk2", [128, H])
    phik = sb(ph, "phik", [128, H * MFT], BF16)
    phiq_t = sb(ph, "phiq_t", [128, H * MFT], BF16)
    vplus = sb(ph, "vplus", [128, H * 65], BF16)
    kvacc = sb(ph, "kvacc", [128, H * 2 * 65], F32)
    kvbf = scrb[:, 2048:2048 + H * 2 * 65]
    phiqT = scrb[:, 1024:1280]
    merged = sb(ph, "merged", [128, D], BF16)
    sch = [sb(ph, f"sch{i}", [128, 512], BF16) for i in range(NDT)]
    rch = [sb(ph, f"rch{i}", [128, 512], BF16) for i in range(NDT)]
    xrg = qg
    xkg = kg
    gch = xkg
    h4b = sb(ph, "h4b", [128, 4 * NDT], BF16)
    orfpad_t = sb(ph, "orfpad_t", [128, 2 * MFT], BF16)
    ld("sync", orfpad_t[:, 0:MFT], dram["orfpad"][0], "W")
    ld("sync", orfpad_t[:, MFT:2 * MFT], dram["orfpad"][1], "W")

    wait("sync", "VE", v_rows0)
    cast_guard = [0, 0]
    for i in range(NDT):
        for half in range(4):
            par = half % 2
            wait("sync", "AC", cast_guard[par])
            ld("sync", wstg[par][:],
               dram["x_t"][128 * i:128 * (i + 1),
                           512 * half:512 * (half + 1)], "W")
            wait("scalar", "W", cnt["W"])
            cast_guard[par] = act(
                xb[i][:, 4 + 512 * half:4 + 512 * (half + 1)],
                wstg[par][:], AF.Copy)
    wait("sync", "AC", cast_guard[1])
    for i in range(NDT):
        ld("sync", wstg[1][:, 4 * i:4 * (i + 1)],
           dram["x_halo"][128 * i:128 * (i + 1), :], "W")
    wait("scalar", "W", cnt["W"])
    for i in range(NDT):
        act(xb[i][:, 0:4], wstg[1][:, 4 * i:4 * (i + 1)], AF.Copy,
            inc=(i == NDT - 1))
    x_cast = cnt["AC"]

    v_xn = emit_ln(xb, rows, "AC", x_cast)

    wguards = [x_cast, x_cast]
    ac_wq = load_w_slab("wqT", wsl, 0, wguards)
    ac_wk = load_w_slab("wkT", wsl, D, wguards)
    ac_wv = load_w_slab("wvT", wsl, 2 * D, wguards)

    def kv_slot(h, mc):
        idx = h * 2 + mc
        if idx < 28:
            return pbk[4 + idx // 7][:, 65 * (idx % 7):65 * (idx % 7) + 65]
        return pbk[3][:, 65 * (idx - 28):65 * (idx - 27)]

    wait("tensor", "AC", ac_wv)
    wait("tensor", "VE", v_xn)
    ve("memset", kvacc[:], 0.0)
    v_kvx0 = cnt["VE"]
    qk_guard = 0
    prev_kv_mm = 0
    prev_kvx_ve = v_kvx0
    prev_vplus_ac = 0
    prev_sq_ac = 0
    for tg in range(4):
        cg = 4 + 512 * tg
        wait("tensor", "AC", qk_guard)
        for (dst, col0, bias_nm) in ((qg, 0, "bq_c"), (kg, D, "bk_c")):
            for ot in range(NDT):
                bank = pbk[0 if dst is qg else 2]
                if not (dst is qg and ot == 0):
                    wait("tensor", "AC", cnt["AC"])
                for dc in range(NDT):
                    mm(bank[:, 0:512],
                       wsl[dc][:, col0 + 128 * ot:col0 + 128 * (ot + 1)],
                       xn[dc][:, cg:cg + 512], dc == 0, dc == NDT - 1,
                       inc=(dc == NDT - 1))
                wait("scalar", "MM", cnt["MM"])
                act(dst[ot][:], bank[:, 0:512], AF.Copy,
                    bias=bias_t[bias_nm][:, ot:ot + 1])
        qk_guard = cnt["AC"]

        for tcs in range(4):
            tcc = 4 * tg + tcs
            c0 = 4 + 128 * tcc
            cl = 128 * tcs
            wait("tensor", "AC", max(prev_vplus_ac, qk_guard))
            for half in range(2):
                bank = pbk[0 if half == 0 else 2]
                for dc in range(NDT):
                    mm(bank[:, 0:512], xn[dc][:, c0:c0 + 128],
                       wsl[dc][:, 2 * D + 512 * half:
                               2 * D + 512 * (half + 1)],
                       dc == 0, False)
                mm(bank[:, 0:512], onesrow_b[:],
                   bvrow_b[:, 512 * half:512 * (half + 1)], False, True,
                   inc=True)
            mm_v = cnt["MM"]
            wait("vector", "MM", prev_kv_mm)
            ve("memset", vplus[:], 1.0)
            v_vones = cnt["VE"]
            wait("scalar", "MM", mm_v)
            wait("scalar", "VE", v_vones)
            for half in range(2):
                act(vplus[:].rearrange("p (h c) -> p h c", c=65)
                    [:, 8 * half:8 * (half + 1), 0:64],
                    pbk[0 if half == 0 else 2][:, 0:512], AF.Copy,
                    inc=(half == 1))
            ac_vplus = cnt["AC"]
            prev_vplus_ac = ac_vplus

            wait("scalar", "AC", max(prev_sq_ac, qk_guard))
            for i in range(NDT):
                act(scrb[:, 128 * i:128 * (i + 1)], qg[i][:, cl:cl + 128],
                    AF.Square, inc=False)
                act(scrb[:, 1024 + 128 * i:1024 + 128 * (i + 1)],
                    kg[i][:, cl:cl + 128], AF.Square, inc=(i == NDT - 1))
            ac_sq = cnt["AC"]
            wait("tensor", "AC", max(ac_sq, ac_vplus))
            for i in range(NDT):
                mm(pbk[2][:, 2 * i:2 * i + 2],
                   scrb[:, 128 * i:128 * (i + 1)], bd2_t[:], True, True,
                   inc=False)
                mm(pbk[2][:, H + 2 * i:H + 2 * i + 2],
                   scrb[:, 1024 + 128 * i:1024 + 128 * (i + 1)], bd2_t[:],
                   True, True, inc=(i == NDT - 1))
            wait("scalar", "MM", cnt["MM"])
            act(sq2[:], pbk[2][:, 0:H], AF.Copy, scale=-1.0 / 16.0,
                inc=False)
            act(sk2[:], pbk[2][:, H:2 * H], AF.Copy, scale=-1.0 / 16.0)
            ac_sig = cnt["AC"]
            prev_sq_ac = ac_sig

            wait("tensor", "AC", ac_sig)
            wait("tensor", "VE", prev_kvx_ve)
            ac_phi_prev = [ac_sig, ac_sig]
            for (srct, dstphi, sig) in ((qg, phiq_t, sq2), (kg, phik, sk2)):
                for h in range(H):
                    pair, h2 = h // 2, h % 2
                    par = h % 2
                    bank = pbk[2 + par]
                    wait("tensor", "AC", ac_phi_prev[par])
                    mm(bank[:, 0:MFT], srct[pair][:, cl:cl + 128],
                       orfpad_t[:, MFT * h2:MFT * (h2 + 1)], True, True,
                       inc=True)
                    wait("scalar", "MM", cnt["MM"])
                    ac_phi_prev[par] = act(dstphi[:, MFT * h:MFT * (h + 1)],
                                           bank[:, 0:MFT], AF.Exp,
                                           scale=S14, bias=sig[:, h:h + 1])
            ac_phi = cnt["AC"]
            wait("vector", "AC", ac_phi)
            ve("tensor_scalar", phiq_t[:], phiq_t[:], 1.0, EPS_PHI,
               ALU.mult, ALU.add, inc=False)
            ve("tensor_scalar", phik[:], phik[:], 1.0, EPS_PHI, ALU.mult,
               ALU.add)
            v_phi = cnt["VE"]
            wait("scalar", "VE", v_phi)
            ld("scalar", dram["phiq_sp"][128 * tcc:128 * (tcc + 1), :],
               phiq_t[:], "X")

            wait("tensor", "VE", v_phi)
            for h in range(H):
                for mc in range(2):
                    mm(kv_slot(h, mc),
                       phik[:, MFT * h + 128 * mc:MFT * h + 128 * (mc + 1)],
                       vplus[:, 65 * h:65 * (h + 1)],
                       True, True,
                       inc=(h * 2 + mc) % 7 == 6 or (h == H - 1 and mc == 1))
            prev_kv_mm = cnt["MM"]
            wait("vector", "MM", prev_kv_mm)
            for r5 in range(5):
                w5 = 455 if r5 < 4 else 260
                srcp = pbk[4 + r5][:, 0:455] if r5 < 4 else pbk[3][:, 0:260]
                ve("tensor_tensor", kvacc[:, 455 * r5:455 * r5 + w5],
                   kvacc[:, 455 * r5:455 * r5 + w5], srcp, ALU.add)
            prev_kvx_ve = cnt["VE"]

    wait("gpsimd", "VE", prev_kvx_ve)
    wait("gpsimd", "G", g_joff)
    gv = ld("gpsimd", dram["kv_cin"][:], kvacc[:], "G")
    wait("gpsimd", "G", gv)
    cc_kv = collective("AllReduce", ALU.add, dram["kv_cin"][:],
                       dram["kv_cout"][:])
    wait("gpsimd", "CC", cc_kv)
    gv = ld("gpsimd", kvacc[:], dram["kv_cout"][:], "G")
    wait("scalar", "G", gv)
    ac_kvbf = act(scrb[:, 2048:2048 + 2080], kvacc[:], AF.Copy)

    def kvbf_slot(h, mc):
        idx = h * 2 + mc
        return scrb[:, 2048 + 65 * idx:2048 + 65 * (idx + 1)]

    ac_wo = load_w_slab("woT", wsl, 0, wguards)
    ac_wr = load_w_slab("wrT", wsl, D, wguards)
    ac_wkf = load_w_slab("wkfT", wsl, 2 * D, wguards)

    wait("tensor", "AC", ac_kvbf)
    v_x1 = 0
    for tg in range(4):
        cg = 4 + 512 * tg
        for tcs in range(4):
            tcc = 4 * tg + tcs
            cl = 128 * tcs
            v_phl = ld("scalar", phiq_t[:],
                       dram["phiq_sp"][128 * tcc:128 * (tcc + 1), :], "X")
            wait("tensor", "X", v_phl)
            if tg > 0 or tcs > 0:
                wait("tensor", "VE", v_x1)
            for h in range(H):
                for mc in range(2):
                    transpose(ptr[:, 0:128],
                              phiq_t[:, MFT * h + 128 * mc:
                                     MFT * h + 128 * (mc + 1)])
                    wait("scalar", "MM", cnt["MM"])
                    act(phiqT[:, 128 * mc:128 * (mc + 1)], ptr[:, 0:128],
                        AF.Copy)
                wait("tensor", "AC", cnt["AC"])
                for mc in range(2):
                    mm(pbk[0][:, 0:65], phiqT[:, 128 * mc:128 * (mc + 1)],
                       kvbf_slot(h, mc), mc == 0, mc == 1, inc=(mc == 1))
                wait("vector", "MM", cnt["MM"])
                ve("reciprocal", recipc[:], pbk[0][:, 64:65])
                wait("scalar", "VE", cnt["VE"])
                act(merged[:, 64 * h:64 * (h + 1)], pbk[0][:, 0:64],
                    AF.Copy, scale=recipc[:])
                wait("tensor", "AC", cnt["AC"])
            for i in range(NDT):
                transpose(ptr[:, 0:128], merged[:, 128 * i:128 * (i + 1)])
                wait("scalar", "MM", cnt["MM"])
                act(qg[i][:, cl:cl + 128], ptr[:, 0:128], AF.Copy,
                    inc=(i == NDT - 1))
            wait("tensor", "AC", cnt["AC"])
        ac_mrg = cnt["AC"]

        wait("tensor", "AC", max(ac_mrg, ac_wo))
        for ot in range(NDT):
            bank = pbk[2 + ot % 2]
            if ot >= 2:
                wait("tensor", "AC", cnt["AC"] - 1)
            for dc in range(NDT):
                mm(bank[:, 0:512], wsl[dc][:, 128 * ot:128 * (ot + 1)],
                   qg[dc][:], dc == 0, dc == NDT - 1, inc=(dc == NDT - 1))
            wait("scalar", "MM", cnt["MM"])
            act(sch[ot][:], bank[:, 0:512], AF.Copy,
                bias=bias_t["bo_c"][:, ot:ot + 1])
        ac_s = cnt["AC"]

        wait("vector", "AC", ac_s)
        for i in range(NDT):
            ve("tensor_tensor", scrb[:, 0:512], xn[i][:, cg:cg + 512],
               xn[i][:, cg - 1:cg + 511], ALU.subtract, inc=False)
            ve("scalar_tensor_tensor", xrg[i][:], scrb[:, 0:512],
               bias_t["mur_c"][:, i:i + 1], xn[i][:, cg - 1:cg + 511],
               ALU.mult, ALU.add, inc=False)
            ve("scalar_tensor_tensor", xkg[i][:], scrb[:, 0:512],
               bias_t["muk_c"][:, i:i + 1], xn[i][:, cg - 1:cg + 511],
               ALU.mult, ALU.add, inc=(i == NDT - 1))
        v_mix = cnt["VE"]
        wait("tensor", "VE", v_mix)
        wait("tensor", "AC", ac_wr)
        for ot in range(NDT):
            bank = pbk[2 + ot % 2]
            if ot >= 2:
                wait("tensor", "AC", cnt["AC"] - 1)
            for dc in range(NDT):
                mm(bank[:, 0:512],
                   wsl[dc][:, D + 128 * ot:D + 128 * (ot + 1)],
                   xrg[dc][:], dc == 0, dc == NDT - 1, inc=(dc == NDT - 1))
            wait("scalar", "MM", cnt["MM"])
            act(rch[ot][:], bank[:, 0:512], AF.Sigmoid)
        ac_r = cnt["AC"]
        wait("tensor", "AC", ac_wkf)
        for ot in range(NDT):
            bank = pbk[2 + ot % 2]
            if ot >= 2:
                wait("tensor", "VE", cnt["VE"] - 1)
            for dc in range(NDT):
                mm(bank[:, 0:512],
                   wsl[dc][:, 2 * D + 128 * ot:2 * D + 128 * (ot + 1)],
                   xkg[dc][:], dc == 0, dc == NDT - 1, inc=(dc == NDT - 1))
            wait("scalar", "MM", cnt["MM"])
            act(scrb[:, 512:1024], bank[:, 0:512], AF.Copy)
            wait("vector", "AC", cnt["AC"])
            ve("scalar_tensor_tensor", xrg[ot][:], scrb[:, 512:1024], 0.0,
               scrb[:, 512:1024], ALU.max, ALU.mult)
            wait("scalar", "VE", cnt["VE"])
        v_kact = cnt["VE"]
        wg2 = [ac_r, ac_r]
        ac_wvf = load_w_slab("wvfT", wsl, 0, wg2)
        wait("tensor", "VE", v_kact)
        wait("tensor", "AC", ac_wvf)
        for ot in range(NDT):
            bank = pbk[2 + ot % 2]
            if ot >= 2:
                wait("tensor", "AC", cnt["AC"] - 1)
            for dc in range(NDT):
                mm(bank[:, 0:512], wsl[dc][:, 128 * ot:128 * (ot + 1)],
                   xrg[dc][:], dc == 0, dc == NDT - 1, inc=(dc == NDT - 1))
            wait("vector", "MM", cnt["MM"])
            ve("tensor_tensor", scr[:, 0:512], rch[ot][:], bank[:, 0:512],
               ALU.mult)
            wait("scalar", "VE", cnt["VE"])
            act(gch[ot][:], scr[:, 0:512], AF.Silu)
            wait("vector", "AC", cnt["AC"])
        ac_g = cnt["AC"]
        wait("vector", "AC", ac_g)
        for i in range(NDT):
            ve("tensor_tensor", scrb[:, 0:512], sch[i][:], gch[i][:],
               ALU.mult, inc=False)
            ve("tensor_tensor", xb[i][:, cg:cg + 512],
               xb[i][:, cg:cg + 512], scrb[:, 0:512], ALU.add,
               inc=(i == NDT - 1))
        v_x1 = cnt["VE"]
        if tg < 3:
            wg3 = [ac_g, ac_g]
            ac_wo = load_w_slab("woT", wsl, 0, wg3)

    wait("gpsimd", "VE", v_x1)
    gh = 0
    for i in range(NDT):
        gh = ld("gpsimd", dram["h4_cin"][128 * i:128 * (i + 1), :],
                xb[i][:, TF - 4:TF], "G")
    wait("gpsimd", "G", gh)
    cc_h4 = collective("AllGather", ALU.bypass, dram["h4_cin"][:],
                       dram["h4_cout"][:])
    wait("gpsimd", "CC", cc_h4)
    gh2 = 0
    for i in range(NDT):
        gh2 = ld("gpsimd", h4b[:, 4 * i:4 * (i + 1)],
                 dram["h4_cout"][128 * i:128 * (i + 1), :], "G")
    wait("vector", "G", gh2)
    for i in range(NDT):
        ve("tensor_single_scalar", xb[i][:, 0:4], h4b[:, 4 * i:4 * (i + 1)],
           jcol[:], ALU.mult, inc=(i == NDT - 1))
    v_x1h = cnt["VE"]

    v_xn2 = emit_ln(xb, rows, "VE", v_x1h)
    wait("scalar", "VE", v_xn2)
    x1_sp_v = 0
    for i in range(NDT):
        x1_sp_v = ld("scalar", dram["x1_sp"][128 * i:128 * (i + 1), :],
                     xb[i][:], "X")
    ph.close()

    ph = contextlib.ExitStack()
    winbf = [sb(ph, f"winbf{i}", [128, D], BF16) for i in range(2)]
    xcch = [sb(ph, f"xcch{i}", [128, 516], BF16) for i in range(NET)]
    ssm_t = sb(ph, "ssm_t", [33, T], BF16)
    dtpre = sb(ph, "dtpre", [1, T], BF16)
    wxsl = sb(ph, "wxsl", [128, 33 * NET], BF16)
    xptail = [sb(ph, f"xpt{i}", [128, 4], BF16) for i in range(NET)]

    wcg = [0, 0]
    wv_ = 0
    for dc in range(NDT):
        for half in range(8):
            par = half % 2
            wait("sync", "AC", wcg[par])
            ld("sync", wstg[par][:],
               dram["winT"][128 * dc:128 * (dc + 1),
                            512 * half:512 * (half + 1)], "W")
            wait("scalar", "W", cnt["W"])
            wait("scalar", "X", wv_)
            wcg[par] = act(winbf[par][:, 0:512], wstg[par][:], AF.Copy)
            wait("scalar", "AC", wcg[par])
            wv_ = ld("scalar",
                     dram["winbf_sp"][128 * dc:128 * (dc + 1),
                                      512 * half:512 * (half + 1)],
                     winbf[par][:, 0:512], "X")
    winbf_done = cnt["X"]
    for i in range(8):
        ld("sync", wstg[0][:, 33 * i:33 * (i + 1)],
           dram["wxT"][128 * i:128 * (i + 1), :], "W")
    for i in range(8):
        ld("sync", wstg[1][:, 33 * i:33 * (i + 1)],
           dram["wxT"][128 * (8 + i):128 * (9 + i), :], "W")
    wait("scalar", "W", cnt["W"])
    wait("scalar", "X", winbf_done)
    act(wxsl[:, 0:264], wstg[0][:, 0:264], AF.Copy, inc=False)
    ac_wx = act(wxsl[:, 264:528], wstg[1][:, 0:264], AF.Copy)

    wait("tensor", "AC", ac_wx)
    wait("tensor", "X", winbf_done)
    wait("tensor", "VE", v_xn2)
    wld_guard = [0, 0]
    dtw_prev_ve = 0
    vdw = 0
    xc_sp_v = 0
    evac_guard = [0, 0]
    for tc4 in range(4):
        c0 = 4 + 512 * tc4
        res_v = 0
        for ot in range(32):
            par = ot % 2
            wait("sync", "MM", wld_guard[par])
            vw = ld("sync",
                    winbf[par][:].rearrange("p (dc q) -> p dc q", q=128),
                    dram["winbf_sp"].rearrange(
                        "(dc p) (o q) -> o p dc q", p=128, q=128)[ot], "W")
            bank = pbk[ot % 2]
            wait("tensor", "W", vw)
            wait("tensor", "AC", evac_guard[par])
            for dc in range(NDT):
                mm(bank[:, 0:512], winbf[par][:, 128 * dc:128 * (dc + 1)],
                   xn[dc][:, c0:c0 + 512], dc == 0, dc == NDT - 1,
                   inc=(dc == NDT - 1))
            wld_guard[par] = cnt["MM"]
            wait("scalar", "MM", cnt["MM"])
            if ot < NET:
                evac_guard[par] = act(xcch[ot][:, 4:516], bank[:, 0:512],
                                      AF.Copy)
            else:
                wait("scalar", "X", res_v)
                evac_guard[par] = act(scrb[:, 0:512], bank[:, 0:512],
                                      AF.Copy)
                wait("scalar", "AC", cnt["AC"])
                i = ot - NET
                res_v = ld("scalar",
                           dram["res_sp"][128 * i:128 * (i + 1),
                                          512 * tc4:512 * (tc4 + 1)],
                           scrb[:, 0:512], "X")
        ac_xpc = cnt["AC"]
        wait("vector", "AC", ac_xpc)
        if tc4 == 0:
            for et in range(NET):
                par = et % 2
                wait("sync", "MM", wld_guard[par])
                vw = ld("sync",
                        winbf[par][:].rearrange("p (dc q) -> p dc q", q=128),
                        dram["winbf_sp"].rearrange(
                            "(dc p) (o q) -> o p dc q", p=128, q=128)[et], "W")
                bank = pbk[2]
                wait("tensor", "W", vw)
                wait("tensor", "AC", cnt["AC"])
                for dc in range(NDT):
                    mm(bank[:, 0:3], winbf[par][:, 128 * dc:128 * (dc + 1)],
                       xn[dc][:, 1:4], dc == 0, dc == NDT - 1,
                       inc=(dc == NDT - 1))
                wld_guard[par] = cnt["MM"]
                wait("scalar", "MM", cnt["MM"])
                act(xcch[et][:, 1:4], bank[:, 0:3], AF.Copy,
                    inc=(et == NET - 1))
            ac_xph = cnt["AC"]
            wait("vector", "AC", ac_xph)
        for et in range(NET):
            if tc4 > 0:
                ve("tensor_copy", xcch[et][:, 0:4], xptail[et][:],
                   inc=False)
            ve("tensor_scalar", scrb[:, 512:1024], xcch[et][:, 1:513],
               convw_t[:, 4 * et:4 * et + 1], None, ALU.mult, inc=False)
            for tap in range(1, 4):
                ve("scalar_tensor_tensor", scrb[:, 512:1024],
                   xcch[et][:, 1 + tap:513 + tap],
                   convw_t[:, 4 * et + tap:4 * et + tap + 1],
                   scrb[:, 512:1024], ALU.mult, ALU.add, inc=(tap == 3))
            ve("tensor_copy", xptail[et][:], xcch[et][:, 512:516])
            wait("scalar", "VE", cnt["VE"])
            act(xcch[et][:, 4:516], scrb[:, 512:1024], AF.Silu,
                bias=bias_t["convb_c"][:, et:et + 1])
            wait("vector", "AC", cnt["AC"])
            wait("scalar", "AC", cnt["AC"])
            wait("scalar", "X", xc_sp_v)
            xc_sp_v = ld("scalar",
                         dram["xc_sp"][128 * et:128 * (et + 1),
                                       512 * tc4:512 * (tc4 + 1)],
                         xcch[et][:, 4:516], "X")
        ac_xcc = cnt["AC"]
        wait("tensor", "AC", ac_xcc)
        bank = pbk[4]
        for et in range(NET):
            mm(bank[0:33, 0:512], wxsl[:, 33 * et:33 * (et + 1)],
               xcch[et][:, 4:516], et == 0, et == NET - 1,
               inc=(et == NET - 1))
        wait("scalar", "MM", cnt["MM"])
        act(ssm_t[:, 512 * tc4:512 * (tc4 + 1)], bank[0:33, 0:512],
            AF.Copy)
        ac_ssmc = cnt["AC"]
        wait("gpsimd", "AC", ac_ssmc)
        gdt = ld("gpsimd", dtpre[:, 512 * tc4:512 * (tc4 + 1)],
                 ssm_t[32:33, 512 * tc4:512 * (tc4 + 1)], "G")
        wait("tensor", "G", gdt)
        wait("tensor", "VE", v_rows0)
        for et in range(NET):
            bank = pbk[5 + et % 2]
            wait("tensor", "VE", dtw_prev_ve)
            mm(bank[:, 0:512], wdtrow_b[:, 128 * et:128 * (et + 1)],
               dtpre[:, 512 * tc4:512 * (tc4 + 1)], True, True, inc=True)
            wait("scalar", "MM", cnt["MM"])
            wait("scalar", "X", vdw)
            act(scr[:, 0:512], bank[:, 0:512], AF.Abs,
                bias=bias_t["bdt_c"][:, et:et + 1], inc=False)
            act(scr[:, 512:1024], scr[:, 0:512], AF.Exp, scale=-1.0,
                inc=False)
            act(scr[:, 512:1024], scr[:, 512:1024], AF.Ln,
                bias=one_c[:], inc=False)
            act(scr2[:, 0:512], bank[:, 0:512], AF.Copy,
                bias=bias_t["bdt_c"][:, et:et + 1])
            wait("vector", "AC", cnt["AC"])
            ve("scalar_tensor_tensor", scrb[:, 1024:1536], scr2[:, 0:512],
               0.0, scr[:, 512:1024], ALU.max, ALU.add, inc=False)
            ve("tensor_tensor", scrb[:, 1536:2048], scrb[:, 1024:1536],
               xcch[et][:, 4:516], ALU.mult)
            dtw_prev_ve = cnt["VE"]
            wait("scalar", "VE", dtw_prev_ve)
            ld("scalar", dram["dt_sp"][128 * et:128 * (et + 1),
                                       512 * tc4:512 * (tc4 + 1)],
               scrb[:, 1024:1536], "X")
            vdw = ld("scalar", dram["w_sp"][128 * et:128 * (et + 1),
                                            512 * tc4:512 * (tc4 + 1)],
                     scrb[:, 1536:2048], "X")
    dtw_sp = cnt["X"]
    wait("gpsimd", "AC", cnt["AC"])
    ld("gpsimd", dram["bc_cin"][0:16, :], ssm_t[0:16, :], "G")
    gbc = ld("gpsimd", dram["bc_cin"][16:32, :], ssm_t[16:32, :], "G")
    ph.close()

    wait("gpsimd", "X", dtw_sp)
    wait("gpsimd", "G", gbc)
    collective("AllGather", ALU.bypass, dram["dt_sp"][:], dram["dt_cout"][:])
    collective("AllGather", ALU.bypass, dram["w_sp"][:], dram["w_cout"][:])
    cc_bc = collective("AllGather", ALU.bypass, dram["bc_cin"][:],
                       dram["bc_cout"][:])
    wait("gpsimd", "CC", cc_bc)

    def dyn_assemble(e):
        with e.register("jrow") as jr:
            e.reg_load(jr, joff_t[0:1, 0:1])
            off = e.snap(jr)
            e.dma_start(out=dram["dtg_sp"][:, 0:T],
                        in_=dram["dt_cout"][bass.ds(off, EI // 2), :]
                        ).then_inc(sems["G"], 16)
            e.dma_start(out=dram["wg_sp"][:, 0:T],
                        in_=dram["w_cout"][bass.ds(off, EI // 2), :]
                        ).then_inc(sems["G"], 16)
        with e.register("jrow2") as jr2:
            e.reg_load(jr2, joff_t[0:1, 0:1])
            e.reg_add(jr2, jr2, EI)
            off2 = e.snap(jr2)
            e.dma_start(out=dram["dtg_sp"][:, T:2 * T],
                        in_=dram["dt_cout"][bass.ds(off2, EI // 2), :]
                        ).then_inc(sems["G"], 16)
            e.dma_start(out=dram["wg_sp"][:, T:2 * T],
                        in_=dram["w_cout"][bass.ds(off2, EI // 2), :]
                        ).then_inc(sems["G"], 16)
    emit("gpsimd", dyn_assemble)
    cnt["G"] += 64
    g_asm = cnt["G"]
    emit("gpsimd", lambda e, v=g_asm: e.wait_ge(sems["G"], v))

    ph = contextlib.ExitStack()
    ball = sb(ph, "ball", [16, 2 * T], BF16)
    call = sb(ph, "call", [16, 2 * T], BF16)
    dtw = [sb(ph, f"dtw{i}", [128, TC], BF16) for i in range(2)]
    wwt = [sb(ph, f"wwt{i}", [128, TC], BF16) for i in range(2)]
    abuf = [sb(ph, f"abuf{i}", [128, SP], F32) for i in range(2)]
    ubuf = [sb(ph, f"ubuf{i}", [128, SP], BF16) for i in range(2)]
    hsup = sb(ph, "hsup", [128, S * SP], BF16)
    cbsup = sb(ph, "cbsup", [128, S * SP], BF16)
    brsup = sb(ph, "brsup", [128, S * SP], BF16)
    carryt = [sb(ph, f"carry{i}", [128, S], BF16) for i in range(NETH)]
    yred = sb(ph, "yred", [128, TC], F32)
    yredb = sb(ph, "yredb", [128, TC], BF16)
    srep = [sb(ph, f"srep{i}", [128, T], BF16) for i in range(N_EXP)]
    glrows = sb(ph, "glrows", [8, T], BF16)
    scrows = sb(ph, "scrows", [8, T], BF16)
    scrow1 = sb(ph, "scrow1", [1, T], BF16)
    sctok = sb(ph, "sctok", [128, 8], BF16)
    gtok = sb(ph, "gtok", [128, 8])
    gcol = sb(ph, "gcol", [128, 1])
    xs_t = [sb(ph, f"xs{i}", [128, 512], BF16) for i in range(NDT)]
    fin_t = [sb(ph, f"fin{i}", [128, 512], BF16) for i in range(NDT)]
    sel16_t = sb(ph, "sel16_t", [16, 16 * 128], BF16)
    moewt = [sb(ph, f"moewt{i}", [128, D], BF16) for i in range(2)]
    ld("sync", sel16_t[:], dram["sel16"][:], "W")

    gb1 = ld("gpsimd", ball[:, 0:T], dram["bc_cout"][0:16, :], "G")
    gb2 = ld("gpsimd", ball[:, T:2 * T], dram["bc_cout"][32:48, :], "G")
    gb3 = ld("gpsimd", call[:, 0:T], dram["bc_cout"][16:32, :], "G")
    g_bc2 = ld("gpsimd", call[:, T:2 * T], dram["bc_cout"][48:64, :], "G")

    for dc in range(NDT):
        ld("sync", wstg[0][:, 8 * dc:8 * (dc + 1)],
           dram["gwT"][128 * dc:128 * (dc + 1), :], "W")
    wait("scalar", "W", cnt["W"])
    wait("scalar", "X", dtw_sp)
    ac_gw = act(scrb[:, 0:64], wstg[0][:, 0:64], AF.Copy)
    wait("tensor", "AC", ac_gw)
    gl_prev = 0
    for tc4 in range(4):
        bank = pbk[4]
        wait("tensor", "AC", gl_prev)
        for dc in range(NDT):
            mm(bank[0:8, 0:512], scrb[:, 8 * dc:8 * (dc + 1)],
               xn[dc][:, 4 + 512 * tc4:4 + 512 * (tc4 + 1)],
               dc == 0, dc == NDT - 1, inc=(dc == NDT - 1))
        wait("scalar", "MM", cnt["MM"])
        gl_prev = act(glrows[:, 512 * tc4:512 * (tc4 + 1)],
                      bank[0:8, 0:512], AF.Copy)
    ac_gl = cnt["AC"]
    wait("tensor", "AC", ac_gl)
    sc_prev_ac = 0
    for tcc in range(16):
        transpose(ptr[:, 0:8], glrows[:, 128 * tcc:128 * (tcc + 1)], kdim=8)
        wait("vector", "MM", cnt["MM"])
        wait("vector", "AC", sc_prev_ac)
        ve("tensor_copy", gtok[:], ptr[:, 0:8], inc=False)
        ve("tensor_reduce", gcol[:], gtok[:], AX.X, ALU.max, negate=True)
        wait("scalar", "VE", cnt["VE"])
        act(gtok[:], gtok[:], AF.Exp, bias=gcol[:])
        wait("vector", "AC", cnt["AC"])
        ve("tensor_reduce", gcol[:], gtok[:], AX.X, ALU.add, inc=False)
        ve("reciprocal", gcol[:], gcol[:], inc=False)
        ve("tensor_scalar", gtok[:], gtok[:], gcol[:], None, ALU.mult,
           inc=False)
        ve("tensor_single_scalar", sctok[:], gtok[:], 0.3, ALU.is_gt,
           inc=False)
        ve("tensor_reduce", gcol[:], sctok[:], AX.X, ALU.add, inc=False)
        ve("memset", sctok[:], 0.0, inc=False)
        ve("tensor_scalar", sctok[:, 1:2], gtok[:, 1:2], gcol[:], None,
           ALU.mult, inc=False)
        ve("tensor_scalar", gcol[:], gcol[:], -1.0, 8.0, ALU.mult, ALU.add,
           inc=False)
        ve("tensor_scalar", sctok[:, 0:1], gtok[:, 0:1], gcol[:], None,
           ALU.mult)
        v_sc = cnt["VE"]
        wait("tensor", "VE", v_sc)
        transpose(ptr[0:8, 896:1024], sctok[:])
        wait("scalar", "MM", cnt["MM"])
        sc_prev_ac = act(scrows[:, 128 * tcc:128 * (tcc + 1)],
                         ptr[0:8, 896:1024], AF.Copy)
    ac_scrow = cnt["AC"]
    wait("tensor", "AC", ac_scrow)
    srep_prev = 0
    for e8 in range(N_EXP):
        gvv = ld("gpsimd", scrow1[:], scrows[e8:e8 + 1, :], "G")
        wait("tensor", "G", gvv)
        for tc4 in range(4):
            wait("tensor", "AC", srep_prev)
            mm(pbk[5][:, 0:512], onesrow_b[:],
               scrow1[:, 512 * tc4:512 * (tc4 + 1)], True, True, inc=True)
            wait("scalar", "MM", cnt["MM"])
            srep_prev = act(srep[e8][:, 512 * tc4:512 * (tc4 + 1)],
                            pbk[5][:, 0:512], AF.Copy)
        wait("gpsimd", "AC", srep_prev)
    ac_srep = cnt["AC"]

    wc_prev = [0, 0]
    for widx in list(range(N_EXP)) + [8]:
        for dc in range(NDT):
            for hf in range(2):
                par = hf
                wait("sync", "X", wc_prev[par])
                wsrc = (dram["expT"][widx, 128 * dc:128 * (dc + 1),
                                     512 * hf:512 * (hf + 1)]
                        if widx < 8 else
                        dram["outwT"][128 * dc:128 * (dc + 1),
                                      512 * hf:512 * (hf + 1)])
                ld("sync", wstg[par][:], wsrc, "W")
                wait("scalar", "W", cnt["W"])
                ac_c = act(scrb[:, 2048 + 512 * par:2048 + 512 * (par + 1)],
                           wstg[par][:], AF.Copy)
                wait("scalar", "AC", ac_c)
                wc_prev[par] = ld(
                    "scalar",
                    dram["wbf_sp"][widx, 128 * dc:128 * (dc + 1),
                                   512 * hf:512 * (hf + 1)],
                    scrb[:, 2048 + 512 * par:2048 + 512 * (par + 1)], "X")
    wcache_done = cnt["X"]

    wait("gpsimd", "AC", ac_srep)
    wait("vector", "G", g_asm)
    wait("vector", "G", max(gb1, gb2, gb3, g_bc2))
    wait("vector", "AC", ac_acol)
    wait("scalar", "G", g_asm)
    wait("tensor", "AC", ac_srep)
    wait("tensor", "X", wcache_done)

    dtw_guard_ac = [0, 0]
    dtw_guard_ve = [0, 0]
    moewt_guard = [0, 0]
    scan_hist = []
    sup_ve_prev = 0
    g2_prev_ac = 0
    gp_hist = []
    y_done = 0
    g2_sp_v = 0

    def emit_moe_chunk(mtc):
        nonlocal g2_prev_ac, g2_sp_v
        c0 = 4 + 512 * mtc
        for e8 in range(N_EXP):
            si = mtc * N_EXP + e8
            if len(gp_hist) >= 1:
                wait("gpsimd", "MM", gp_hist[-1][1])
            for dc in range(NDT):
                ve("tensor_tensor", xs_t[dc][:], xn[dc][:, c0:c0 + 512],
                   srep[e8][:, 512 * mtc:512 * (mtc + 1)], ALU.mult,
                   eng="gpsimd", inc=(dc == NDT - 1))
            v_xs = cnt["GP"]
            for dc in range(NDT):
                wpar = (si * NDT + dc) % 2
                wait("sync", "MM", moewt_guard[wpar])
                ld("sync", moewt[wpar][:],
                   dram["wbf_sp"][e8, 128 * dc:128 * (dc + 1), :], "W")
                wait("tensor", "W", cnt["W"])
                if dc == 0:
                    wait("tensor", "GP", v_xs)
                if mtc > 0 and e8 == 0 and dc == 0:
                    wait("tensor", "AC", g2_prev_ac)
                for dt_o in range(NDT):
                    mm(pbk[dt_o][:, 0:512],
                       moewt[wpar][:, 128 * dt_o:128 * (dt_o + 1)],
                       xs_t[dc][:], e8 == 0 and dc == 0,
                       e8 == N_EXP - 1 and dc == NDT - 1,
                       inc=(dt_o == NDT - 1))
                moewt_guard[wpar] = cnt["MM"]
            gp_hist.append((v_xs, cnt["MM"]))
        wait("scalar", "MM", cnt["MM"])
        for dt_o in range(NDT):
            act(fin_t[dt_o][:], pbk[dt_o][:, 0:512], AF.Copy,
                inc=(dt_o == NDT - 1))
        ac_fin = cnt["AC"]
        for dc in range(NDT):
            wpar = dc % 2
            wait("sync", "MM", moewt_guard[wpar])
            ld("sync", moewt[wpar][:],
               dram["wbf_sp"][8, 128 * dc:128 * (dc + 1), :], "W")
            wait("tensor", "W", cnt["W"])
            if dc == 0:
                wait("tensor", "AC", ac_fin)
            for dt_o in range(NDT):
                mm(pbk[dt_o][:, 0:512],
                   moewt[wpar][:, 128 * dt_o:128 * (dt_o + 1)],
                   fin_t[dc][:], dc == 0, dc == NDT - 1,
                   inc=(dt_o == NDT - 1))
            moewt_guard[wpar] = cnt["MM"]
        wait("scalar", "MM", cnt["MM"])
        for dt_o in range(NDT):
            wait("scalar", "X", g2_sp_v)
            act(scrb[:, 0:512], pbk[dt_o][:, 0:512], AF.Silu)
            wait("scalar", "AC", cnt["AC"])
            g2_sp_v = ld("scalar",
                         dram["g2_sp"][128 * dt_o:128 * (dt_o + 1),
                                       512 * mtc:512 * (mtc + 1)],
                         scrb[:, 0:512], "X")
        g2_prev_ac = cnt["AC"]

    for tcg in range(NTC):
        c0g = TC * tcg
        sup_ac0 = 0
        for s_ in range(S):
            for src_t, dst_sup in ((ball, brsup), (call, cbsup)):
                wait("tensor", "AC", sup_ac0)
                if tcg > 0:
                    wait("tensor", "VE", sup_ve_prev)
                mm(pbk[6][:, 0:512], sel16_t[:, 128 * s_:128 * (s_ + 1)],
                   src_t[:, c0g:c0g + TC], True, True, inc=True)
                wait("scalar", "MM", cnt["MM"])
                if tcg > 0:
                    wait("scalar", "VE", sup_ve_prev)
                sup_ac0 = act(dst_sup[:, SP * s_ + 1:SP * (s_ + 1)],
                              pbk[6][:, 0:TC], AF.Copy)
        ac_sup = cnt["AC"]

        if tcg % 2 == 0:
            emit_moe_chunk(tcg // 2)

        for et in range(NETH):
            par = et % 2
            wait("scalar", "AC", dtw_guard_ac[par])
            wait("scalar", "VE", dtw_guard_ve[par])
            ld("scalar", dtw[par][:],
               dram["dtg_sp"][128 * et:128 * (et + 1), c0g:c0g + TC], "X")
            vww = ld("scalar", wwt[par][:],
                     dram["wg_sp"][128 * et:128 * (et + 1), c0g:c0g + TC],
                     "X")
            wait("scalar", "X", vww)
            wait("vector", "X", vww)
            wait("vector", "AC", ac_sup)
            if tcg == 0:
                ve("memset", carryt[et][:], 0.0, inc=False)
                ve("memset", abuf[0][:, 0:1], 0.0, inc=False)
                ve("memset", abuf[1][:, 0:1], 0.0)
            for s_ in range(S):
                sp = s_ % 2
                if len(scan_hist) >= 2:
                    wait("scalar", "VE", scan_hist[-2])
                act(abuf[sp][:, 1:SP], dtw[par][:], AF.Exp,
                    scale=acols[:, s_:s_ + 1])
                ac_a = cnt["AC"]
                ve("tensor_tensor", ubuf[sp][:, 1:SP], wwt[par][:],
                   brsup[:, SP * s_ + 1:SP * (s_ + 1)], ALU.mult, inc=False)
                ve("tensor_copy", ubuf[sp][:, 0:1],
                   carryt[et][:, s_:s_ + 1], inc=False)
                wait("vector", "AC", ac_a)
                ve("tensor_tensor_scan", hsup[:, SP * s_:SP * (s_ + 1)],
                   abuf[sp][:], ubuf[sp][:], 0.0, ALU.mult, ALU.add)
                scan_hist.append(cnt["VE"])
            v_scans = cnt["VE"]
            dtw_guard_ve[par] = v_scans
            wait("scalar", "VE", v_scans)
            ac_carry = act(
                carryt[et][:],
                hsup[:].rearrange("p (s c) -> p s c", c=SP)[:, :, SP - 1:SP]
                .rearrange("p s c -> p (s c)"), AF.Copy)
            dtw_guard_ac[par] = ac_carry
            wait("vector", "AC", ac_carry)
            for s_ in range(S):
                ve("tensor_tensor", hsup[:, SP * s_ + 1:SP * (s_ + 1)],
                   hsup[:, SP * s_ + 1:SP * (s_ + 1)],
                   cbsup[:, SP * s_ + 1:SP * (s_ + 1)], ALU.mult,
                   inc=(s_ == S - 1))
            ve("tensor_reduce", yred[:],
               hsup[:].rearrange("p (s c) -> p s c", c=SP)[:, :, 1:SP]
               .rearrange("p s c -> p c s"), AX.X, ALU.add)
            v_y = cnt["VE"]
            sup_ve_prev = v_y
            wait("scalar", "VE", v_y)
            ac_y = act(yredb[:], yred[:], AF.Copy)
            wait("scalar", "AC", ac_y)
            y_done = ld("scalar", dram["y_sp"][128 * et:128 * (et + 1),
                                               c0g:c0g + TC], yredb[:], "X")
            wait("vector", "AC", ac_y)
    ph.close()

    ph = contextlib.ExitStack()
    wosl = [sb(ph, f"wosl{i}", [128, 2 * D], BF16) for i in range(NDT)]
    zt = [sb(ph, f"zt{i}", [128, T], BF16) for i in range(NET)]
    g2t = [sb(ph, f"g2t{i}", [128, 512], BF16) for i in range(2)]
    x1t = [sb(ph, f"x1t{i}", [128, 512], BF16) for i in range(2)]

    wait("gpsimd", "X", y_done)
    cc_y = collective("AllGather", ALU.bypass, dram["y_sp"][:],
                      dram["y_cout"][:])
    wait("gpsimd", "CC", cc_y)

    def dyn_ymine(e):
        with e.register("jc2") as jr:
            e.reg_load(jr, joff_t[0:1, 1:2])
            off = e.snap(jr)
            e.dma_start(out=dram["ymine_sp"][:],
                        in_=dram["y_cout"][:, bass.ds(off, T)]
                        ).then_inc(sems["G"], 16)
    emit("gpsimd", dyn_ymine)
    cnt["G"] += 16
    g_ym = cnt["G"]
    emit("gpsimd", lambda e, v=g_ym: e.wait_ge(sems["G"], v))

    def wout_slot(i):
        return wosl[i % 8][:, (i // 8) * D:(i // 8 + 1) * D]

    wg4 = [0, 0]
    ac_wout = 0
    for i in range(NET):
        for hf in range(2):
            par = hf
            wait("sync", "AC", wg4[par])
            ld("sync", wstg[par][:],
               dram["woutT"][128 * i:128 * (i + 1),
                             512 * hf:512 * (hf + 1)], "W")
            wait("scalar", "W", cnt["W"])
            ac_wout = act(wout_slot(i)[:, 512 * hf:512 * (hf + 1)],
                          wstg[par][:], AF.Copy)
            wg4[par] = ac_wout

    wait("scalar", "G", g_ym)
    z_prev = 0
    for i in range(NET):
        wait("scalar", "AC", z_prev)
        ld("scalar", zt[i][:], dram["xc_sp"][128 * i:128 * (i + 1), :], "X")
        ld("scalar", scrb[:, 0:T],
           dram["ymine_sp"][128 * i:128 * (i + 1), :], "X")
        vrs = ld("scalar", scrb[:, 2048:2048 + T],
                 dram["res_sp"][128 * i:128 * (i + 1), :], "X")
        wait("vector", "X", vrs)
        ve("scalar_tensor_tensor", scrb[:, 0:T], zt[i][:],
           bias_t["dd_c"][:, i:i + 1], scrb[:, 0:T], ALU.mult, ALU.add,
           inc=False)
        ve("tensor_tensor", scrb[:, 0:T], scrb[:, 0:T],
           scrb[:, 2048:2048 + T], ALU.add)
        wait("scalar", "VE", cnt["VE"])
        z_prev = act(zt[i][:], scrb[:, 0:T], AF.Silu)
    ac_z = cnt["AC"]

    wait("tensor", "AC", max(ac_z, ac_wout))
    out_prev_g = [0, 0]
    for dt_o in range(NDT):
        for tc4 in range(4):
            par = tc4 % 2
            bank = pbk[par]
            wait("scalar", "G", out_prev_g[par])
            vg2 = ld("scalar", g2t[par][:],
                     dram["g2_sp"][128 * dt_o:128 * (dt_o + 1),
                                   512 * tc4:512 * (tc4 + 1)], "X")
            vx1 = ld("scalar", x1t[par][:],
                     dram["x1_sp"][128 * dt_o:128 * (dt_o + 1),
                                   4 + 512 * tc4:4 + 512 * (tc4 + 1)], "X")
            wait("tensor", "G", out_prev_g[par])
            wait("tensor", "VE", cnt["VE"])
            for ec in range(NET):
                mm(bank[:, 0:512],
                   wout_slot(ec)[:, 128 * dt_o:128 * (dt_o + 1)],
                   zt[ec][:, 512 * tc4:512 * (tc4 + 1)], ec == 0,
                   ec == NET - 1, inc=(ec == NET - 1))
            wait("vector", "MM", cnt["MM"])
            wait("vector", "X", vx1)
            wait("vector", "G", out_prev_g[par])
            ve("tensor_tensor", scr[:, 512 * par:512 * (par + 1)],
               bank[:, 0:512], g2t[par][:], ALU.mult, inc=False)
            ve("tensor_tensor", scr2[:, 512 * par:512 * (par + 1)],
               x1t[par][:], scr[:, 512 * par:512 * (par + 1)], ALU.add)
            wait("gpsimd", "VE", cnt["VE"])
            out_prev_g[par] = ld(
                "gpsimd",
                dram["out_t"][128 * dt_o:128 * (dt_o + 1),
                              512 * tc4:512 * (tc4 + 1)],
                scr2[:, 512 * par:512 * (par + 1)], "G")
    out_done = cnt["G"]
    wait("gpsimd", "G", out_done)
    ph.close()

    with nc.Block() as block:
        @block.sync
        def _(e):
            for f in thunks["sync"]:
                f(e)

        @block.scalar
        def _(e):
            for f in thunks["scalar"]:
                f(e)

        @block.vector
        def _(e):
            for f in thunks["vector"]:
                f(e)

        @block.tensor
        def _(e):
            for f in thunks["tensor"]:
                f(e)

        @block.gpsimd
        def _(e):
            for f in thunks["gpsimd"]:
                f(e)

    es.close()
    return nc


_NC_CACHE = {}


def kernel(**inputs):
    in_maps = _host_prepare(inputs)
    if "nc" not in _NC_CACHE:
        _NC_CACHE["nc"] = build_program()
    nc = _NC_CACHE["nc"]
    res = run_bass_kernel_spmd(nc, in_maps, core_ids=list(range(8)))
    out = np.zeros((4, 4096, 1024), np.float32)
    for c in range(8):
        b, j = c // 2, c % 2
        out[b, j * T:(j + 1) * T] = np.asarray(res.results[c]["out_t"]).T
    return out




# version 04
# speedup vs baseline: 1.1310x; 1.0181x over previous
import contextlib
import numpy as np
import ml_dtypes
import concourse.bass as bass
import concourse.mybir as mybir
from concourse.bass_utils import run_bass_kernel_spmd

F32 = mybir.dt.float32
BF16 = mybir.dt.bfloat16
I32 = mybir.dt.int32
AF = mybir.ActivationFunctionType
ALU = mybir.AluOpType
AX = mybir.AxisListType
BFNP = ml_dtypes.bfloat16

D = 1024
T = 2048
TF = T + 4
H = 16
MFT = 256
S = 16
EI = 2048
TC = 512
SP = TC + 1
NTC = (2 * T) // TC
NDT = 8
NET = 16
NETH = 8
EPS_LN = 1e-5
EPS_PHI = 1e-6
S14 = float(1.0 / (64.0 ** 0.25))
COMPUTE_ALL_EXPERTS = False
N_EXP = 8 if COMPUTE_ALL_EXPERTS else 2
PAIRS = [[0, 1], [2, 3], [4, 5], [6, 7]]


def _host_prepare(inputs):
    f = lambda a: np.ascontiguousarray(np.asarray(a, np.float32))
    x = f(inputs["L0_attn_x"])
    big = {
        "wqT": f(inputs["L0_attn_Wq"]).T, "wkT": f(inputs["L0_attn_Wk"]).T,
        "wvT": f(inputs["L0_attn_Wv"]).T, "woT": f(inputs["L0_attn_Wo"]).T,
        "wrT": f(inputs["L0_ffn_W_r"]).T, "wkfT": f(inputs["L0_ffn_W_k"]).T,
        "wvfT": f(inputs["L0_ffn_W_v"]).T,
        "winT": f(inputs["L1_attn_W_in"]).T,
        "woutT": f(inputs["L1_attn_W_out"]).T,
        "outwT": f(inputs["L1_ffn_out_W"]).T,
        "gwT": f(inputs["L1_ffn_gate_W"]).T,
        "wxT": f(inputs["L1_attn_W_x"]).T,
    }
    big = {k: np.ascontiguousarray(v) for k, v in big.items()}
    big["expT"] = np.ascontiguousarray(
        np.transpose(f(inputs["L1_ffn_expert_W"]), (0, 2, 1)))
    bcol = lambda v, n: np.ascontiguousarray(f(v).reshape(n, 128).T)
    sm = {
        "bq_c": bcol(inputs["L0_attn_bq"], NDT),
        "bk_c": bcol(inputs["L0_attn_bk"], NDT),
        "bo_c": bcol(inputs["L0_attn_bo"], NDT),
        "bv_r": f(inputs["L0_attn_bv"]).reshape(1, D),
        "mur_c": bcol(inputs["L0_ffn_mu_r"], NDT),
        "muk_c": bcol(inputs["L0_ffn_mu_k"], NDT),
        "convw_c": np.ascontiguousarray(
            f(inputs["L1_attn_conv_w"])[:, 0, :].reshape(NET, 128, 4)
            .transpose(1, 0, 2).reshape(128, NET * 4)),
        "convb_c": bcol(inputs["L1_attn_conv_b"], NET),
        "wdt_r": f(inputs["L1_attn_W_dt"])[:, 0].reshape(1, EI),
        "bdt_c": bcol(inputs["L1_attn_b_dt"], NET),
        "alog_r": f(inputs["L1_attn_A_log"]).reshape(1, S),
        "dd_c": bcol(inputs["L1_attn_D"], NET),
    }
    orf = f(inputs["L0_attn_orf"])
    ident = np.eye(128, dtype=BFNP)
    orfpad = np.zeros((2, 128, MFT), BFNP)
    for h2 in range(2):
        orfpad[h2, 64 * h2:64 * h2 + 64, :] = orf.T.astype(BFNP)
    sel16 = np.zeros((16, 16 * 128), BFNP)
    for s_ in range(16):
        sel16[s_, 128 * s_:128 * (s_ + 1)] = 1
    bd2 = np.zeros((128, 2), BFNP)
    bd2[0:64, 0] = 1
    bd2[64:128, 1] = 1

    in_maps = []
    for c in range(8):
        b, j = c // 2, c % 2
        x_t = np.ascontiguousarray(x[b, j * T:(j + 1) * T].T)
        halo = (np.zeros((D, 4), np.float32) if j == 0
                else np.ascontiguousarray(x[b, T - 4:T].T))
        m = {"x_t": x_t, "x_halo": halo, "ident": ident, "orfpad": orfpad,
             "sel16": sel16, "bd2": bd2,
             "core_j": np.full((128, 1), float(j), np.float32),
             "joff": np.array([[1024 * j, 2048 * j]], np.int32)}
        m.update(big)
        m.update(sm)
        in_maps.append(m)
    return in_maps


def build_program():
    nc = bass.Bass("TRN2", target_bir_lowering=False)
    es = contextlib.ExitStack()
    dram = {}

    def din(name, shape, dt=F32):
        dram[name] = nc.dram_tensor(name, shape, dt, kind="ExternalInput").ap()

    def dout(name, shape, dt=F32):
        dram[name] = nc.dram_tensor(name, shape, dt,
                                    kind="ExternalOutput").ap()

    def dint(name, shape, dt=F32):
        dram[name] = nc.dram_tensor(name, shape, dt).ap()

    din("x_t", [D, T]); din("x_halo", [D, 4])
    for w in ("wqT", "wkT", "wvT", "woT", "wrT", "wkfT", "wvfT", "outwT"):
        din(w, [D, D])
    din("winT", [D, 4 * D]); din("woutT", [EI, D]); din("gwT", [D, 8])
    din("wxT", [EI, 33]); din("expT", [8, D, D])
    for nm, shp in (("bq_c", [128, NDT]), ("bk_c", [128, NDT]),
                    ("bo_c", [128, NDT]), ("bv_r", [1, D]),
                    ("mur_c", [128, NDT]), ("muk_c", [128, NDT]),
                    ("convw_c", [128, NET * 4]), ("convb_c", [128, NET]),
                    ("wdt_r", [1, EI]), ("bdt_c", [128, NET]),
                    ("alog_r", [1, S]), ("dd_c", [128, NET]),
                    ("core_j", [128, 1])):
        din(nm, shp)
    din("joff", [1, 2], I32)
    din("ident", [128, 128], BF16); din("orfpad", [2, 128, MFT], BF16)
    din("sel16", [16, 16 * 128], BF16); din("bd2", [128, 2], BF16)
    dout("out_t", [D, T])

    dint("phiq_sp", [T, H * MFT], BF16)
    dint("x1_sp", [D, TF], BF16)
    dint("res_sp", [EI, T], BF16)
    dint("xc_sp", [EI, T], BF16)
    dint("dt_sp", [EI, T], BF16)
    dint("w_sp", [EI, T], BF16)
    dint("y_sp", [EI // 2, 2 * T], BF16)
    dint("g2_sp", [D, T], BF16)
    dint("wbf_sp", [9, D, D], BF16)
    dint("winbf_sp", [D, 4 * D], BF16)
    dint("dtg_sp", [EI // 2, 2 * T], BF16)
    dint("wg_sp", [EI // 2, 2 * T], BF16)
    dint("ymine_sp", [EI, T], BF16)
    dint("kv_cin", [128, H * 2 * 65]); dint("kv_cout", [128, H * 2 * 65])
    dint("h4_cin", [D, 4], BF16); dint("h4_cout", [2 * D, 4], BF16)
    dint("dt_cout", [2 * EI, T], BF16)
    dint("w_cout", [2 * EI, T], BF16)
    dint("bc_cin", [32, T], BF16); dint("bc_cout", [64, T], BF16)
    dint("y_cout", [EI, 2 * T], BF16)

    thunks = {e: [] for e in ("sync", "scalar", "vector", "tensor", "gpsimd")}
    cnt = {}
    sems = {}
    for s_ in ("MM", "AC", "VE", "GP", "CC"):
        sems[s_] = es.enter_context(nc.semaphore(s_))
        cnt[s_] = 0

    def emit(eng, fn):
        thunks[eng].append(fn)

    DMA_SEMS = ("W", "X", "G")
    QUEUE_OF = {"W": "sync", "X": "scalar", "G": "gpsimd"}
    ENG_SEM = {"sync": "W", "scalar": "AC", "vector": "VE", "tensor": "MM",
               "gpsimd": "GP"}

    R_DMA = 8
    for q_ in DMA_SEMS:
        cnt[q_] = 0
        for i_ in range(R_DMA):
            sems[f"{q_}{i_}"] = es.enter_context(nc.semaphore(f"{q_}{i_}"))
    phys_tot = {f"{q_}{i_}": 0 for q_ in DMA_SEMS for i_ in range(R_DMA)}
    rot_idx = {q_: 0 for q_ in DMA_SEMS}
    pend_bar = {f"{q_}{i_}": None for q_ in DMA_SEMS for i_ in range(R_DMA)}
    snap = {q_: {} for q_ in DMA_SEMS}
    awaited = {f"{q_}{i_}": set() for q_ in DMA_SEMS for i_ in range(R_DMA)}

    def _issue(q):
        i = rot_idx[q]
        rot_idx[q] = (i + 1) % R_DMA
        phys = f"{q}{i}"
        if pend_bar[phys] is not None:
            def bar(e, s=sems[phys], v=pend_bar[phys], p=phys):
                if v in awaited[p]:
                    e.wait_ge(s, v)
            emit(QUEUE_OF[q], bar)
        phys_tot[phys] += 16
        pend_bar[phys] = phys_tot[phys]
        cnt[q] += 16
        snap[q][cnt[q]] = tuple(phys_tot[f"{q}{k}"] for k in range(R_DMA))
        return phys, phys_tot[phys]

    def flush_bar(q):
        for i in range(R_DMA):
            phys = f"{q}{i}"
            if pend_bar[phys] is not None:
                def bar(e, s=sems[phys], v=pend_bar[phys], p=phys):
                    if v in awaited[p]:
                        e.wait_ge(s, v)
                emit(QUEUE_OF[q], bar)
                pend_bar[phys] = None

    writers = {}
    readers = {}
    watermark = {e: {} for e in thunks}
    mm_pending_reads = []

    def _tname(ap):
        try:
            return ap.tensor.name
        except AttributeError:
            return None

    def wait(eng, sem, val):
        if val <= 0:
            return
        if watermark[eng].get(sem, 0) >= val:
            return
        watermark[eng][sem] = val
        if sem in DMA_SEMS:
            totals = snap[sem][val]
            for k in range(R_DMA):
                t = totals[k]
                phys = f"{sem}{k}"
                if t > 0 and watermark[eng].get(phys, 0) < t:
                    watermark[eng][phys] = t
                    awaited[phys].add(t)
                    emit(eng, lambda e, s=sems[phys], v=t: e.wait_ge(s, v))
            return
        emit(eng, lambda e, s=sems[sem], v=val: e.wait_ge(s, v))

    def dep_wait(eng, read_names, write_names):
        for nm in read_names:
            w = writers.get(nm)
            if w is not None:
                ws, wv = w
                if not (eng == "tensor" and ws == "MM"):
                    wait(eng, ws, wv)
        for nm in write_names:
            w = writers.get(nm)
            if w is not None:
                ws, wv = w
                if not (eng == "tensor" and ws == "MM"):
                    wait(eng, ws, wv)
            for rs, rv in readers.get(nm, {}).items():
                if not (eng == "tensor" and rs == "MM"):
                    wait(eng, rs, rv)

    def dep_record(eng, sem, val, read_names, write_names):
        for nm in read_names:
            readers.setdefault(nm, {})[sem] = val
        for nm in write_names:
            writers[nm] = (sem, val)
            if nm in readers:
                readers[nm] = {}

    def ld(eng, dst, src, sem=None):
        if sem is None:
            sem = {"sync": "W", "scalar": "X", "gpsimd": "G"}[eng]
        assert QUEUE_OF.get(sem) == eng, (eng, sem)
        rn = [_tname(src)]
        wn = [_tname(dst)]
        dep_wait(eng, rn, wn)
        phys, _ = _issue(sem)
        val = cnt[sem]
        dep_record(eng, sem, val, rn, wn)
        emit(eng, lambda e, d=dst, s=src, sh=sems[phys]:
             e.dma_start(out=d, in_=s).then_inc(sh, 16))
        return val

    def mm(outap, lhs, rhs, start, stop, inc=False):
        rn = [_tname(lhs), _tname(rhs)]
        wn = [_tname(outap)]
        if not start:
            rn.append(_tname(outap))
        dep_wait("tensor", rn, wn)
        mm_pending_reads.extend(rn)
        v = None
        if inc:
            cnt["MM"] += 1
            v = cnt["MM"]
            for nm in set(mm_pending_reads):
                readers.setdefault(nm, {})["MM"] = v
            mm_pending_reads.clear()
            writers[wn[0]] = ("MM", v)

        def f(e, o=outap, l=lhs, r=rhs, st=start, sp=stop, vv=v):
            i = e.matmul(o, l, r, start=st, stop=sp)
            if vv is not None:
                i.then_inc(sems["MM"], 1)
        emit("tensor", f)
        return cnt["MM"]

    def act(outap, inap, func, bias=0.0, scale=1.0, inc=True):
        rn = [_tname(inap)]
        if not isinstance(bias, float):
            rn.append(_tname(bias))
        if not isinstance(scale, float):
            rn.append(_tname(scale))
        wn = [_tname(outap)]
        dep_wait("scalar", rn, wn)
        cnt["AC"] += 1
        v = cnt["AC"]
        dep_record("scalar", "AC", v, rn, wn)

        def f(e, o=outap, i=inap, fu=func, b=bias, s=scale, vv=v):
            if fu == AF.Copy and not isinstance(b, float):
                fu = AF.Identity
            e.activation(o, i, fu, bias=b, scale=s).then_inc(sems["AC"], 1)
        emit("scalar", f)
        return cnt["AC"]

    def ve(op, *args, inc=True, eng="vector", **kw):
        sem = "VE" if eng == "vector" else "GP"
        rn = [_tname(a) for a in args[1:] if hasattr(a, "tensor")]
        rn += [_tname(a) for a in kw.values() if hasattr(a, "tensor")]
        wn = [_tname(args[0])] if hasattr(args[0], "tensor") else []
        if op == "memset":
            rn = []
        dep_wait(eng, rn, wn)
        cnt[sem] += 1
        v = cnt[sem]
        dep_record(eng, sem, v, rn, wn)

        def f(e, o=op, a=args, k=kw, vv=v, sh=sems[sem]):
            getattr(e, o)(*a, **k).then_inc(sh, 1)
        emit(eng, f)
        return cnt[sem]

    def transpose(outap, inap, kdim=128):
        rn = [_tname(inap), "ident_t"]
        wn = [_tname(outap)]
        dep_wait("tensor", rn, wn)
        cnt["MM"] += 1
        v = cnt["MM"]
        dep_record("tensor", "MM", v, rn, wn)

        def f(e, o=outap, i=inap, vv=v, k=kdim):
            e.transpose(o, i, ident_t[0:k, 0:k]).then_inc(sems["MM"], 1)
        emit("tensor", f)
        return v

    def collective(kind, op, in_ap, out_ap):
        rn = [_tname(in_ap)]
        wn = [_tname(out_ap)]
        dep_wait("gpsimd", rn, wn)
        cnt["CC"] += 1
        v = cnt["CC"]
        dep_record("gpsimd", "CC", v, rn, wn)

        def f(e, i=in_ap, o=out_ap, k=kind, a=op):
            e.collective_compute(k, a, replica_groups=PAIRS, ins=[i],
                                 outs=[o]).then_inc(sems["CC"], 1)
        emit("gpsimd", f)
        return cnt["CC"]

    def sb(stack, name, shape, dt=F32):
        return stack.enter_context(nc.sbuf_tensor(name, shape, dt))

    xn = [sb(es, f"xn{i}", [128, TF], BF16) for i in range(NDT)]
    wstg = [sb(es, f"wstg{i}", [128, 512], F32) for i in range(2)]
    scr = sb(es, "scr", [128, 1024], F32)
    scr2 = sb(es, "scr2", [128, 1024], F32)
    scrb = sb(es, "scrb", [128, 4160], BF16)
    onescol_b = sb(es, "onescol_b", [128, 1], BF16)
    onesrow_b = sb(es, "onesrow_b", [1, 128], BF16)
    ident_t = sb(es, "ident_t", [128, 128], BF16)
    bd2_t = sb(es, "bd2_t", [128, 2], BF16)
    bias_t = {nm: sb(es, nm + "_t", [128, 16]) for nm in
              ("bq_c", "bk_c", "bo_c", "mur_c", "muk_c", "convb_c", "bdt_c",
               "dd_c")}
    convw_t = sb(es, "convw_t", [128, NET * 4])
    bvrow_b = sb(es, "bvrow_b", [1, D], BF16)

    arow = sb(es, "arow", [1, S]); arow_b = sb(es, "arow_b", [1, S], BF16)
    acols = sb(es, "acols", [128, S])
    recipc = sb(es, "recipc", [128, 1])
    jcol = sb(es, "jcol", [128, 1])
    epsln_c = sb(es, "epsln_c", [128, 1])
    one_c = sb(es, "one_c", [128, 1])
    joff_t = sb(es, "joff_t", [1, 2], I32)
    wdtrow_b = sb(es, "wdtrow_b", [1, EI], BF16)

    pbk = [es.enter_context(nc.psum_tensor(f"pb{i}", [128, 512], F32))
           for i in range(8)]
    ptr = pbk[1].bitcast(BF16)

    ld("sync", ident_t[:], dram["ident"][:], "W")
    ld("sync", bd2_t[:], dram["bd2"][:], "W")
    widths = {"bq_c": NDT, "bk_c": NDT, "bo_c": NDT, "mur_c": NDT,
              "muk_c": NDT, "convb_c": NET, "bdt_c": NET, "dd_c": NET}
    for nm, wdt in widths.items():
        ld("sync", bias_t[nm][:, 0:wdt], dram[nm][:], "W")
    ld("sync", convw_t[:], dram["convw_c"][:], "W")
    ld("sync", scr[0:1, 0:1024], dram["bv_r"][:], "W")
    ld("sync", scr2[0:1, 0:1024], dram["wdt_r"][:, 0:1024], "W")
    ld("sync", wstg[0][0:1, :], dram["wdt_r"][:, 1024:1536], "W")
    ld("sync", wstg[1][0:1, :], dram["wdt_r"][:, 1536:2048], "W")
    ld("sync", arow[:], dram["alog_r"][:], "W")
    w_const = ld("sync", jcol[:], dram["core_j"][:], "W")
    g_joff = ld("gpsimd", joff_t[:], dram["joff"][:], "G")

    ve("memset", onescol_b[:], 1.0, inc=False)
    ve("memset", epsln_c[:], EPS_LN, inc=False)
    ve("memset", one_c[:], 1.0, inc=False)
    ve("memset", onesrow_b[:], 1.0)
    v_init = cnt["VE"]
    wait("vector", "W", w_const)
    ve("tensor_copy", bvrow_b[:], scr[0:1, 0:1024], inc=False)
    ve("tensor_copy", wdtrow_b[:, 0:1024], scr2[0:1, 0:1024], inc=False)
    ve("tensor_copy", wdtrow_b[:, 1024:1536], wstg[0][0:1, :], inc=False)
    ve("tensor_copy", wdtrow_b[:, 1536:2048], wstg[1][0:1, :], inc=False)
    ve("tensor_copy", arow_b[:], arow[:])
    v_rows0 = cnt["VE"]
    wait("tensor", "VE", v_rows0)
    mm(pbk[0][:, 0:S], onesrow_b[:], arow_b[:], True, True, inc=True)
    wait("scalar", "MM", cnt["MM"])
    ac_acol = act(acols[:], pbk[0][:, 0:S], AF.Exp)
    wait("scalar", "AC", ac_acol)
    ac_acol = act(acols[:], acols[:], AF.Copy, scale=-1.0)

    def emit_ln(tiles, rows, pre_sem, pre_val):
        mrow, m2row, mrow_b, rrow_b = rows
        chunks = [(0, 4), (4, 516), (516, 1028), (1028, 1540), (1540, 2052)]
        wait("scalar", pre_sem, pre_val)
        for (c0, c1) in chunks:
            n = c1 - c0
            wait("scalar", "MM", cnt["MM"])
            wait("scalar", "VE", cnt["VE"])
            for i in range(NDT):
                act(scrb[:, 516 * i:516 * i + n], tiles[i][:, c0:c1],
                    AF.Square, inc=(i == NDT - 1))
            ac_sq = cnt["AC"]
            wait("tensor", "AC", ac_sq)
            for i in range(NDT):
                mm(pbk[4][0:1, 0:n], onescol_b[:], tiles[i][:, c0:c1],
                   i == 0, i == NDT - 1, inc=(i == NDT - 1))
            for i in range(NDT):
                mm(pbk[5][0:1, 0:n], onescol_b[:],
                   scrb[:, 516 * i:516 * i + n], i == 0, i == NDT - 1,
                   inc=(i == NDT - 1))
            wait("scalar", "MM", cnt["MM"])
            act(mrow[:, 0:n], pbk[4][0:1, 0:n], AF.Copy, scale=1.0 / D,
                inc=False)
            act(m2row[:, 0:n], pbk[5][0:1, 0:n], AF.Copy, scale=1.0 / D)
            ac_rows = cnt["AC"]
            wait("vector", "AC", ac_rows)
            ve("tensor_tensor", scr[0:1, 0:n], mrow[:, 0:n], mrow[:, 0:n],
               ALU.mult, inc=False)
            ve("tensor_tensor", m2row[:, 0:n], m2row[:, 0:n],
               scr[0:1, 0:n], ALU.subtract)
            wait("scalar", "VE", cnt["VE"])
            act(m2row[:, 0:n], m2row[:, 0:n], AF.Sqrt, bias=epsln_c[0:1, :])
            wait("vector", "AC", cnt["AC"])
            ve("reciprocal", m2row[:, 0:n], m2row[:, 0:n], inc=False)
            ve("tensor_copy", mrow_b[:, 0:n], mrow[:, 0:n], inc=False)
            ve("tensor_copy", rrow_b[:, 0:n], m2row[:, 0:n])
            wait("tensor", "VE", cnt["VE"])
            mm(pbk[6][:, 0:n], onesrow_b[:], mrow_b[:, 0:n], True, True,
               inc=False)
            mm(pbk[7][:, 0:n], onesrow_b[:], rrow_b[:, 0:n], True, True,
               inc=True)
            wait("vector", "MM", cnt["MM"])
            for i in range(NDT):
                ve("tensor_tensor", scr[:, 0:n], tiles[i][:, c0:c1],
                   pbk[6][:, 0:n], ALU.subtract, inc=False)
                ve("tensor_tensor", xn[i][:, c0:c1], scr[:, 0:n],
                   pbk[7][:, 0:n], ALU.mult, inc=(i == NDT - 1))
        return cnt["VE"]

    def load_w_slab(wname, slabs, col0, guards):
        last = 0
        for i in range(NDT):
            for hf in range(2):
                par = (2 * i + hf) % 2
                wait("sync", "AC", guards[par])
                ld("sync", wstg[par][:],
                   dram[wname][128 * i:128 * (i + 1),
                               512 * hf:512 * (hf + 1)], "W")
                wait("scalar", "W", cnt["W"])
                last = act(slabs[i][:, col0 + 512 * hf:col0 + 512 * (hf + 1)],
                           wstg[par][:], AF.Copy)
                guards[par] = last
        return last

    ph = contextlib.ExitStack()
    xb = [sb(ph, f"xb{i}", [128, TF], BF16) for i in range(NDT)]
    wsl = [sb(ph, f"wsl{i}", [128, 3 * D], BF16) for i in range(NDT)]
    mrow = sb(ph, "mrow", [1, 516]); m2row = sb(ph, "m2row", [1, 516])
    mrow_b = sb(ph, "mrow_b", [1, 516], BF16)
    rrow_b = sb(ph, "rrow_b", [1, 516], BF16)
    rows = (mrow, m2row, mrow_b, rrow_b)
    qg = [sb(ph, f"qg{i}", [128, 512], BF16) for i in range(NDT)]
    kg = [sb(ph, f"kg{i}", [128, 512], BF16) for i in range(NDT)]
    sq2 = sb(ph, "sq2", [128, H]); sk2 = sb(ph, "sk2", [128, H])
    phik = sb(ph, "phik", [128, H * MFT], BF16)
    phiq_t = sb(ph, "phiq_t", [128, H * MFT], BF16)
    vplus = sb(ph, "vplus", [128, H * 65], BF16)
    kvacc = sb(ph, "kvacc", [128, H * 2 * 65], F32)
    kvbf = scrb[:, 2048:2048 + H * 2 * 65]
    phiqT = scrb[:, 1024:1280]
    merged = sb(ph, "merged", [128, D], BF16)
    sch = [sb(ph, f"sch{i}", [128, 512], BF16) for i in range(NDT)]
    rch = [sb(ph, f"rch{i}", [128, 512], BF16) for i in range(NDT)]
    xrg = qg
    xkg = kg
    gch = xkg
    h4b = sb(ph, "h4b", [128, 4 * NDT], BF16)
    orfpad_t = sb(ph, "orfpad_t", [128, 2 * MFT], BF16)
    ld("sync", orfpad_t[:, 0:MFT], dram["orfpad"][0], "W")
    ld("sync", orfpad_t[:, MFT:2 * MFT], dram["orfpad"][1], "W")

    wait("sync", "VE", v_rows0)
    cast_guard = [0, 0]
    for i in range(NDT):
        for half in range(4):
            par = half % 2
            wait("sync", "AC", cast_guard[par])
            ld("sync", wstg[par][:],
               dram["x_t"][128 * i:128 * (i + 1),
                           512 * half:512 * (half + 1)], "W")
            wait("scalar", "W", cnt["W"])
            cast_guard[par] = act(
                xb[i][:, 4 + 512 * half:4 + 512 * (half + 1)],
                wstg[par][:], AF.Copy)
    wait("sync", "AC", cast_guard[1])
    for i in range(NDT):
        ld("sync", wstg[1][:, 4 * i:4 * (i + 1)],
           dram["x_halo"][128 * i:128 * (i + 1), :], "W")
    wait("scalar", "W", cnt["W"])
    for i in range(NDT):
        act(xb[i][:, 0:4], wstg[1][:, 4 * i:4 * (i + 1)], AF.Copy,
            inc=(i == NDT - 1))
    x_cast = cnt["AC"]

    v_xn = emit_ln(xb, rows, "AC", x_cast)

    wguards = [x_cast, x_cast]
    ac_wq = load_w_slab("wqT", wsl, 0, wguards)
    ac_wk = load_w_slab("wkT", wsl, D, wguards)
    ac_wv = load_w_slab("wvT", wsl, 2 * D, wguards)

    def kv_slot(h, mc):
        idx = h * 2 + mc
        if idx < 28:
            return pbk[4 + idx // 7][:, 65 * (idx % 7):65 * (idx % 7) + 65]
        return pbk[3][:, 65 * (idx - 28):65 * (idx - 27)]

    wait("tensor", "AC", ac_wv)
    wait("tensor", "VE", v_xn)
    ve("memset", kvacc[:], 0.0)
    v_kvx0 = cnt["VE"]
    qk_guard = 0
    prev_kv_mm = 0
    prev_kvx_ve = v_kvx0
    prev_vplus_ac = 0
    prev_sq_ac = 0
    for tg in range(4):
        cg = 4 + 512 * tg
        wait("tensor", "AC", qk_guard)
        for (dst, col0, bias_nm) in ((qg, 0, "bq_c"), (kg, D, "bk_c")):
            for ot in range(NDT):
                bank = pbk[0 if dst is qg else 2]
                if not (dst is qg and ot == 0):
                    wait("tensor", "AC", cnt["AC"])
                for dc in range(NDT):
                    mm(bank[:, 0:512],
                       wsl[dc][:, col0 + 128 * ot:col0 + 128 * (ot + 1)],
                       xn[dc][:, cg:cg + 512], dc == 0, dc == NDT - 1,
                       inc=(dc == NDT - 1))
                wait("scalar", "MM", cnt["MM"])
                act(dst[ot][:], bank[:, 0:512], AF.Copy,
                    bias=bias_t[bias_nm][:, ot:ot + 1])
        qk_guard = cnt["AC"]

        for tcs in range(4):
            tcc = 4 * tg + tcs
            c0 = 4 + 128 * tcc
            cl = 128 * tcs
            wait("tensor", "AC", max(prev_vplus_ac, qk_guard))
            for half in range(2):
                bank = pbk[0 if half == 0 else 2]
                for dc in range(NDT):
                    mm(bank[:, 0:512], xn[dc][:, c0:c0 + 128],
                       wsl[dc][:, 2 * D + 512 * half:
                               2 * D + 512 * (half + 1)],
                       dc == 0, False)
                mm(bank[:, 0:512], onesrow_b[:],
                   bvrow_b[:, 512 * half:512 * (half + 1)], False, True,
                   inc=True)
            mm_v = cnt["MM"]
            wait("vector", "MM", prev_kv_mm)
            ve("memset", vplus[:], 1.0)
            v_vones = cnt["VE"]
            wait("scalar", "MM", mm_v)
            wait("scalar", "VE", v_vones)
            for half in range(2):
                act(vplus[:].rearrange("p (h c) -> p h c", c=65)
                    [:, 8 * half:8 * (half + 1), 0:64],
                    pbk[0 if half == 0 else 2][:, 0:512], AF.Copy,
                    inc=(half == 1))
            ac_vplus = cnt["AC"]
            prev_vplus_ac = ac_vplus

            wait("scalar", "AC", max(prev_sq_ac, qk_guard))
            for i in range(NDT):
                act(scrb[:, 128 * i:128 * (i + 1)], qg[i][:, cl:cl + 128],
                    AF.Square, inc=False)
                act(scrb[:, 1024 + 128 * i:1024 + 128 * (i + 1)],
                    kg[i][:, cl:cl + 128], AF.Square, inc=(i == NDT - 1))
            ac_sq = cnt["AC"]
            wait("tensor", "AC", max(ac_sq, ac_vplus))
            for i in range(NDT):
                mm(pbk[2][:, 2 * i:2 * i + 2],
                   scrb[:, 128 * i:128 * (i + 1)], bd2_t[:], True, True,
                   inc=False)
                mm(pbk[2][:, H + 2 * i:H + 2 * i + 2],
                   scrb[:, 1024 + 128 * i:1024 + 128 * (i + 1)], bd2_t[:],
                   True, True, inc=(i == NDT - 1))
            wait("scalar", "MM", cnt["MM"])
            act(sq2[:], pbk[2][:, 0:H], AF.Copy, scale=-1.0 / 16.0,
                inc=False)
            act(sk2[:], pbk[2][:, H:2 * H], AF.Copy, scale=-1.0 / 16.0)
            ac_sig = cnt["AC"]
            prev_sq_ac = ac_sig

            wait("tensor", "AC", ac_sig)
            wait("tensor", "VE", prev_kvx_ve)
            ac_phi_prev = [ac_sig, ac_sig]
            for (srct, dstphi, sig) in ((qg, phiq_t, sq2), (kg, phik, sk2)):
                for h in range(H):
                    pair, h2 = h // 2, h % 2
                    par = h % 2
                    bank = pbk[2 + par]
                    wait("tensor", "AC", ac_phi_prev[par])
                    mm(bank[:, 0:MFT], srct[pair][:, cl:cl + 128],
                       orfpad_t[:, MFT * h2:MFT * (h2 + 1)], True, True,
                       inc=True)
                    wait("scalar", "MM", cnt["MM"])
                    ac_phi_prev[par] = act(dstphi[:, MFT * h:MFT * (h + 1)],
                                           bank[:, 0:MFT], AF.Exp,
                                           scale=S14, bias=sig[:, h:h + 1])
            ac_phi = cnt["AC"]
            wait("vector", "AC", ac_phi)
            ve("tensor_scalar", phiq_t[:], phiq_t[:], 1.0, EPS_PHI,
               ALU.mult, ALU.add, inc=False)
            ve("tensor_scalar", phik[:], phik[:], 1.0, EPS_PHI, ALU.mult,
               ALU.add)
            v_phi = cnt["VE"]
            wait("scalar", "VE", v_phi)
            ld("scalar", dram["phiq_sp"][128 * tcc:128 * (tcc + 1), :],
               phiq_t[:], "X")

            wait("tensor", "VE", v_phi)
            for h in range(H):
                for mc in range(2):
                    mm(kv_slot(h, mc),
                       phik[:, MFT * h + 128 * mc:MFT * h + 128 * (mc + 1)],
                       vplus[:, 65 * h:65 * (h + 1)],
                       True, True,
                       inc=(h * 2 + mc) % 7 == 6 or (h == H - 1 and mc == 1))
            prev_kv_mm = cnt["MM"]
            wait("vector", "MM", prev_kv_mm)
            for r5 in range(5):
                w5 = 455 if r5 < 4 else 260
                srcp = pbk[4 + r5][:, 0:455] if r5 < 4 else pbk[3][:, 0:260]
                ve("tensor_tensor", kvacc[:, 455 * r5:455 * r5 + w5],
                   kvacc[:, 455 * r5:455 * r5 + w5], srcp, ALU.add)
            prev_kvx_ve = cnt["VE"]

    wait("gpsimd", "VE", prev_kvx_ve)
    wait("gpsimd", "G", g_joff)
    gv = ld("gpsimd", dram["kv_cin"][:], kvacc[:], "G")
    wait("gpsimd", "G", gv)
    cc_kv = collective("AllReduce", ALU.add, dram["kv_cin"][:],
                       dram["kv_cout"][:])
    wait("gpsimd", "CC", cc_kv)
    gv = ld("gpsimd", kvacc[:], dram["kv_cout"][:], "G")
    wait("scalar", "G", gv)
    ac_kvbf = act(scrb[:, 2048:2048 + 2080], kvacc[:], AF.Copy)

    def kvbf_slot(h, mc):
        idx = h * 2 + mc
        return scrb[:, 2048 + 65 * idx:2048 + 65 * (idx + 1)]

    ac_wo = load_w_slab("woT", wsl, 0, wguards)
    ac_wr = load_w_slab("wrT", wsl, D, wguards)
    ac_wkf = load_w_slab("wkfT", wsl, 2 * D, wguards)

    wait("tensor", "AC", ac_kvbf)
    v_x1 = 0
    for tg in range(4):
        cg = 4 + 512 * tg
        for tcs in range(4):
            tcc = 4 * tg + tcs
            cl = 128 * tcs
            v_phl = ld("scalar", phiq_t[:],
                       dram["phiq_sp"][128 * tcc:128 * (tcc + 1), :], "X")
            wait("tensor", "X", v_phl)
            if tg > 0 or tcs > 0:
                wait("tensor", "VE", v_x1)
            for h in range(H):
                for mc in range(2):
                    transpose(ptr[:, 0:128],
                              phiq_t[:, MFT * h + 128 * mc:
                                     MFT * h + 128 * (mc + 1)])
                    wait("scalar", "MM", cnt["MM"])
                    act(phiqT[:, 128 * mc:128 * (mc + 1)], ptr[:, 0:128],
                        AF.Copy)
                wait("tensor", "AC", cnt["AC"])
                for mc in range(2):
                    mm(pbk[0][:, 0:65], phiqT[:, 128 * mc:128 * (mc + 1)],
                       kvbf_slot(h, mc), mc == 0, mc == 1, inc=(mc == 1))
                wait("vector", "MM", cnt["MM"])
                ve("reciprocal", recipc[:], pbk[0][:, 64:65])
                wait("scalar", "VE", cnt["VE"])
                act(merged[:, 64 * h:64 * (h + 1)], pbk[0][:, 0:64],
                    AF.Copy, scale=recipc[:])
                wait("tensor", "AC", cnt["AC"])
            for i in range(NDT):
                transpose(ptr[:, 0:128], merged[:, 128 * i:128 * (i + 1)])
                wait("scalar", "MM", cnt["MM"])
                act(qg[i][:, cl:cl + 128], ptr[:, 0:128], AF.Copy,
                    inc=(i == NDT - 1))
            wait("tensor", "AC", cnt["AC"])
        ac_mrg = cnt["AC"]

        wait("tensor", "AC", max(ac_mrg, ac_wo))
        for ot in range(NDT):
            bank = pbk[2 + ot % 2]
            if ot >= 2:
                wait("tensor", "AC", cnt["AC"] - 1)
            for dc in range(NDT):
                mm(bank[:, 0:512], wsl[dc][:, 128 * ot:128 * (ot + 1)],
                   qg[dc][:], dc == 0, dc == NDT - 1, inc=(dc == NDT - 1))
            wait("scalar", "MM", cnt["MM"])
            act(sch[ot][:], bank[:, 0:512], AF.Copy,
                bias=bias_t["bo_c"][:, ot:ot + 1])
        ac_s = cnt["AC"]

        wait("vector", "AC", ac_s)
        for i in range(NDT):
            ve("tensor_tensor", scrb[:, 0:512], xn[i][:, cg:cg + 512],
               xn[i][:, cg - 1:cg + 511], ALU.subtract, inc=False)
            ve("scalar_tensor_tensor", xrg[i][:], scrb[:, 0:512],
               bias_t["mur_c"][:, i:i + 1], xn[i][:, cg - 1:cg + 511],
               ALU.mult, ALU.add, inc=False)
            ve("scalar_tensor_tensor", xkg[i][:], scrb[:, 0:512],
               bias_t["muk_c"][:, i:i + 1], xn[i][:, cg - 1:cg + 511],
               ALU.mult, ALU.add, inc=(i == NDT - 1))
        v_mix = cnt["VE"]
        wait("tensor", "VE", v_mix)
        wait("tensor", "AC", ac_wr)
        for ot in range(NDT):
            bank = pbk[2 + ot % 2]
            if ot >= 2:
                wait("tensor", "AC", cnt["AC"] - 1)
            for dc in range(NDT):
                mm(bank[:, 0:512],
                   wsl[dc][:, D + 128 * ot:D + 128 * (ot + 1)],
                   xrg[dc][:], dc == 0, dc == NDT - 1, inc=(dc == NDT - 1))
            wait("scalar", "MM", cnt["MM"])
            act(rch[ot][:], bank[:, 0:512], AF.Sigmoid)
        ac_r = cnt["AC"]
        wait("tensor", "AC", ac_wkf)
        for ot in range(NDT):
            bank = pbk[2 + ot % 2]
            if ot >= 2:
                wait("tensor", "VE", cnt["VE"] - 1)
            for dc in range(NDT):
                mm(bank[:, 0:512],
                   wsl[dc][:, 2 * D + 128 * ot:2 * D + 128 * (ot + 1)],
                   xkg[dc][:], dc == 0, dc == NDT - 1, inc=(dc == NDT - 1))
            wait("scalar", "MM", cnt["MM"])
            act(scrb[:, 512:1024], bank[:, 0:512], AF.Copy)
            wait("vector", "AC", cnt["AC"])
            ve("scalar_tensor_tensor", xrg[ot][:], scrb[:, 512:1024], 0.0,
               scrb[:, 512:1024], ALU.max, ALU.mult)
            wait("scalar", "VE", cnt["VE"])
        v_kact = cnt["VE"]
        wg2 = [ac_r, ac_r]
        ac_wvf = load_w_slab("wvfT", wsl, 0, wg2)
        wait("tensor", "VE", v_kact)
        wait("tensor", "AC", ac_wvf)
        for ot in range(NDT):
            bank = pbk[2 + ot % 2]
            if ot >= 2:
                wait("tensor", "AC", cnt["AC"] - 1)
            for dc in range(NDT):
                mm(bank[:, 0:512], wsl[dc][:, 128 * ot:128 * (ot + 1)],
                   xrg[dc][:], dc == 0, dc == NDT - 1, inc=(dc == NDT - 1))
            wait("vector", "MM", cnt["MM"])
            ve("tensor_tensor", scr[:, 0:512], rch[ot][:], bank[:, 0:512],
               ALU.mult)
            wait("scalar", "VE", cnt["VE"])
            act(gch[ot][:], scr[:, 0:512], AF.Silu)
            wait("vector", "AC", cnt["AC"])
        ac_g = cnt["AC"]
        wait("vector", "AC", ac_g)
        for i in range(NDT):
            ve("tensor_tensor", scrb[:, 0:512], sch[i][:], gch[i][:],
               ALU.mult, inc=False)
            ve("tensor_tensor", xb[i][:, cg:cg + 512],
               xb[i][:, cg:cg + 512], scrb[:, 0:512], ALU.add,
               inc=(i == NDT - 1))
        v_x1 = cnt["VE"]
        if tg < 3:
            wg3 = [ac_g, ac_g]
            ac_wo = load_w_slab("woT", wsl, 0, wg3)

    wait("gpsimd", "VE", v_x1)
    gh = 0
    for i in range(NDT):
        gh = ld("gpsimd", dram["h4_cin"][128 * i:128 * (i + 1), :],
                xb[i][:, TF - 4:TF], "G")
    wait("gpsimd", "G", gh)
    cc_h4 = collective("AllGather", ALU.bypass, dram["h4_cin"][:],
                       dram["h4_cout"][:])
    wait("gpsimd", "CC", cc_h4)
    gh2 = 0
    for i in range(NDT):
        gh2 = ld("gpsimd", h4b[:, 4 * i:4 * (i + 1)],
                 dram["h4_cout"][128 * i:128 * (i + 1), :], "G")
    wait("vector", "G", gh2)
    for i in range(NDT):
        ve("tensor_single_scalar", xb[i][:, 0:4], h4b[:, 4 * i:4 * (i + 1)],
           jcol[:], ALU.mult, inc=(i == NDT - 1))
    v_x1h = cnt["VE"]

    v_xn2 = emit_ln(xb, rows, "VE", v_x1h)
    wait("scalar", "VE", v_xn2)
    x1_sp_v = 0
    for i in range(NDT):
        x1_sp_v = ld("scalar", dram["x1_sp"][128 * i:128 * (i + 1), :],
                     xb[i][:], "X")
    ph.close()

    ph = contextlib.ExitStack()
    winbf = [sb(ph, f"winbf{i}", [128, D], BF16) for i in range(2)]
    xcch = [sb(ph, f"xcch{i}", [128, 516], BF16) for i in range(NET)]
    ssm_t = sb(ph, "ssm_t", [33, T], BF16)
    dtpre = sb(ph, "dtpre", [1, T], BF16)
    wxsl = sb(ph, "wxsl", [128, 33 * NET], BF16)
    xptail = [sb(ph, f"xpt{i}", [128, 4], BF16) for i in range(NET)]

    wcg = [0, 0]
    wv_ = 0
    for dc in range(NDT):
        for half in range(8):
            par = half % 2
            wait("sync", "AC", wcg[par])
            ld("sync", wstg[par][:],
               dram["winT"][128 * dc:128 * (dc + 1),
                            512 * half:512 * (half + 1)], "W")
            wait("scalar", "W", cnt["W"])
            wait("scalar", "X", wv_)
            wcg[par] = act(winbf[par][:, 0:512], wstg[par][:], AF.Copy)
            wait("scalar", "AC", wcg[par])
            wv_ = ld("scalar",
                     dram["winbf_sp"][128 * dc:128 * (dc + 1),
                                      512 * half:512 * (half + 1)],
                     winbf[par][:, 0:512], "X")
    winbf_done = cnt["X"]
    for i in range(8):
        ld("sync", wstg[0][:, 33 * i:33 * (i + 1)],
           dram["wxT"][128 * i:128 * (i + 1), :], "W")
    for i in range(8):
        ld("sync", wstg[1][:, 33 * i:33 * (i + 1)],
           dram["wxT"][128 * (8 + i):128 * (9 + i), :], "W")
    wait("scalar", "W", cnt["W"])
    wait("scalar", "X", winbf_done)
    act(wxsl[:, 0:264], wstg[0][:, 0:264], AF.Copy, inc=False)
    ac_wx = act(wxsl[:, 264:528], wstg[1][:, 0:264], AF.Copy)

    wait("tensor", "AC", ac_wx)
    wait("tensor", "X", winbf_done)
    wait("tensor", "VE", v_xn2)
    wld_guard = [0, 0]
    dtw_prev_ve = 0
    vdw = 0
    xc_sp_v = 0
    evac_guard = [0, 0]
    for tc4 in range(4):
        c0 = 4 + 512 * tc4
        res_v = 0
        for ot in range(32):
            par = ot % 2
            wait("sync", "MM", wld_guard[par])
            vw = ld("sync",
                    winbf[par][:].rearrange("p (dc q) -> p dc q", q=128),
                    dram["winbf_sp"].rearrange(
                        "(dc p) (o q) -> o p dc q", p=128, q=128)[ot], "W")
            bank = pbk[ot % 2]
            wait("tensor", "W", vw)
            wait("tensor", "AC", evac_guard[par])
            for dc in range(NDT):
                mm(bank[:, 0:512], winbf[par][:, 128 * dc:128 * (dc + 1)],
                   xn[dc][:, c0:c0 + 512], dc == 0, dc == NDT - 1,
                   inc=(dc == NDT - 1))
            wld_guard[par] = cnt["MM"]
            wait("scalar", "MM", cnt["MM"])
            if ot < NET:
                evac_guard[par] = act(xcch[ot][:, 4:516], bank[:, 0:512],
                                      AF.Copy)
            else:
                wait("scalar", "X", res_v)
                evac_guard[par] = act(scrb[:, 0:512], bank[:, 0:512],
                                      AF.Copy)
                wait("scalar", "AC", cnt["AC"])
                i = ot - NET
                res_v = ld("scalar",
                           dram["res_sp"][128 * i:128 * (i + 1),
                                          512 * tc4:512 * (tc4 + 1)],
                           scrb[:, 0:512], "X")
        ac_xpc = cnt["AC"]
        wait("vector", "AC", ac_xpc)
        if tc4 == 0:
            for et in range(NET):
                par = et % 2
                wait("sync", "MM", wld_guard[par])
                vw = ld("sync",
                        winbf[par][:].rearrange("p (dc q) -> p dc q", q=128),
                        dram["winbf_sp"].rearrange(
                            "(dc p) (o q) -> o p dc q", p=128, q=128)[et], "W")
                bank = pbk[2]
                wait("tensor", "W", vw)
                wait("tensor", "AC", cnt["AC"])
                for dc in range(NDT):
                    mm(bank[:, 0:3], winbf[par][:, 128 * dc:128 * (dc + 1)],
                       xn[dc][:, 1:4], dc == 0, dc == NDT - 1,
                       inc=(dc == NDT - 1))
                wld_guard[par] = cnt["MM"]
                wait("scalar", "MM", cnt["MM"])
                act(xcch[et][:, 1:4], bank[:, 0:3], AF.Copy,
                    inc=(et == NET - 1))
            ac_xph = cnt["AC"]
            wait("vector", "AC", ac_xph)
        for et in range(NET):
            if tc4 > 0:
                ve("tensor_copy", xcch[et][:, 0:4], xptail[et][:],
                   inc=False)
            ve("tensor_scalar", scrb[:, 512:1024], xcch[et][:, 1:513],
               convw_t[:, 4 * et:4 * et + 1], None, ALU.mult, inc=False)
            for tap in range(1, 4):
                ve("scalar_tensor_tensor", scrb[:, 512:1024],
                   xcch[et][:, 1 + tap:513 + tap],
                   convw_t[:, 4 * et + tap:4 * et + tap + 1],
                   scrb[:, 512:1024], ALU.mult, ALU.add, inc=(tap == 3))
            ve("tensor_copy", xptail[et][:], xcch[et][:, 512:516])
            wait("scalar", "VE", cnt["VE"])
            act(xcch[et][:, 4:516], scrb[:, 512:1024], AF.Silu,
                bias=bias_t["convb_c"][:, et:et + 1])
            wait("vector", "AC", cnt["AC"])
            wait("scalar", "AC", cnt["AC"])
            wait("scalar", "X", xc_sp_v)
            xc_sp_v = ld("scalar",
                         dram["xc_sp"][128 * et:128 * (et + 1),
                                       512 * tc4:512 * (tc4 + 1)],
                         xcch[et][:, 4:516], "X")
        ac_xcc = cnt["AC"]
        wait("tensor", "AC", ac_xcc)
        bank = pbk[4]
        for et in range(NET):
            mm(bank[0:33, 0:512], wxsl[:, 33 * et:33 * (et + 1)],
               xcch[et][:, 4:516], et == 0, et == NET - 1,
               inc=(et == NET - 1))
        wait("scalar", "MM", cnt["MM"])
        act(ssm_t[:, 512 * tc4:512 * (tc4 + 1)], bank[0:33, 0:512],
            AF.Copy)
        ac_ssmc = cnt["AC"]
        wait("gpsimd", "AC", ac_ssmc)
        gdt = ld("gpsimd", dtpre[:, 512 * tc4:512 * (tc4 + 1)],
                 ssm_t[32:33, 512 * tc4:512 * (tc4 + 1)], "G")
        wait("tensor", "G", gdt)
        wait("tensor", "VE", v_rows0)
        for et in range(NET):
            bank = pbk[5 + et % 2]
            wait("tensor", "VE", dtw_prev_ve)
            mm(bank[:, 0:512], wdtrow_b[:, 128 * et:128 * (et + 1)],
               dtpre[:, 512 * tc4:512 * (tc4 + 1)], True, True, inc=True)
            wait("scalar", "MM", cnt["MM"])
            wait("scalar", "X", vdw)
            act(scr[:, 0:512], bank[:, 0:512], AF.Abs,
                bias=bias_t["bdt_c"][:, et:et + 1], inc=False)
            act(scr[:, 512:1024], scr[:, 0:512], AF.Exp, scale=-1.0,
                inc=False)
            act(scr[:, 512:1024], scr[:, 512:1024], AF.Ln,
                bias=one_c[:], inc=False)
            act(scr2[:, 0:512], bank[:, 0:512], AF.Copy,
                bias=bias_t["bdt_c"][:, et:et + 1])
            wait("vector", "AC", cnt["AC"])
            ve("scalar_tensor_tensor", scrb[:, 1024:1536], scr2[:, 0:512],
               0.0, scr[:, 512:1024], ALU.max, ALU.add, inc=False)
            ve("tensor_tensor", scrb[:, 1536:2048], scrb[:, 1024:1536],
               xcch[et][:, 4:516], ALU.mult)
            dtw_prev_ve = cnt["VE"]
            wait("scalar", "VE", dtw_prev_ve)
            ld("scalar", dram["dt_sp"][128 * et:128 * (et + 1),
                                       512 * tc4:512 * (tc4 + 1)],
               scrb[:, 1024:1536], "X")
            vdw = ld("scalar", dram["w_sp"][128 * et:128 * (et + 1),
                                            512 * tc4:512 * (tc4 + 1)],
                     scrb[:, 1536:2048], "X")
    dtw_sp = cnt["X"]
    wait("gpsimd", "AC", cnt["AC"])
    ld("gpsimd", dram["bc_cin"][0:16, :], ssm_t[0:16, :], "G")
    gbc = ld("gpsimd", dram["bc_cin"][16:32, :], ssm_t[16:32, :], "G")
    ph.close()

    wait("gpsimd", "X", dtw_sp)
    wait("gpsimd", "G", gbc)
    collective("AllGather", ALU.bypass, dram["dt_sp"][:], dram["dt_cout"][:])
    collective("AllGather", ALU.bypass, dram["w_sp"][:], dram["w_cout"][:])
    cc_bc = collective("AllGather", ALU.bypass, dram["bc_cin"][:],
                       dram["bc_cout"][:])
    wait("gpsimd", "CC", cc_bc)

    dynsems = [sems[_issue("G")[0]] for _ in range(4)]

    def dyn_assemble(e):
        with e.register("jrow") as jr:
            e.reg_load(jr, joff_t[0:1, 0:1])
            off = e.snap(jr)
            e.dma_start(out=dram["dtg_sp"][:, 0:T],
                        in_=dram["dt_cout"][bass.ds(off, EI // 2), :]
                        ).then_inc(dynsems[0], 16)
            e.dma_start(out=dram["wg_sp"][:, 0:T],
                        in_=dram["w_cout"][bass.ds(off, EI // 2), :]
                        ).then_inc(dynsems[1], 16)
        with e.register("jrow2") as jr2:
            e.reg_load(jr2, joff_t[0:1, 0:1])
            e.reg_add(jr2, jr2, EI)
            off2 = e.snap(jr2)
            e.dma_start(out=dram["dtg_sp"][:, T:2 * T],
                        in_=dram["dt_cout"][bass.ds(off2, EI // 2), :]
                        ).then_inc(dynsems[2], 16)
            e.dma_start(out=dram["wg_sp"][:, T:2 * T],
                        in_=dram["w_cout"][bass.ds(off2, EI // 2), :]
                        ).then_inc(dynsems[3], 16)
    emit("gpsimd", dyn_assemble)
    g_asm = cnt["G"]
    dep_record("gpsimd", "G", g_asm, ["dt_cout", "w_cout"],
               ["dtg_sp", "wg_sp"])
    wait("gpsimd", "G", g_asm)

    ph = contextlib.ExitStack()
    ball = sb(ph, "ball", [16, 2 * T], BF16)
    call = sb(ph, "call", [16, 2 * T], BF16)
    dtw = [sb(ph, f"dtw{i}", [128, TC], BF16) for i in range(2)]
    wwt = [sb(ph, f"wwt{i}", [128, TC], BF16) for i in range(2)]
    abuf = [sb(ph, f"abuf{i}", [128, SP], F32) for i in range(2)]
    ubuf = [sb(ph, f"ubuf{i}", [128, SP], BF16) for i in range(2)]
    hsup = sb(ph, "hsup", [128, S * SP], BF16)
    cbsup = sb(ph, "cbsup", [128, S * SP], BF16)
    brsup = sb(ph, "brsup", [128, S * SP], BF16)
    carryt = [sb(ph, f"carry{i}", [128, S], BF16) for i in range(NETH)]
    yred = sb(ph, "yred", [128, TC], F32)
    yredb = sb(ph, "yredb", [128, TC], BF16)
    srep = [sb(ph, f"srep{i}", [128, T], BF16) for i in range(N_EXP)]
    glrows = sb(ph, "glrows", [8, T], BF16)
    scrows = sb(ph, "scrows", [8, T], BF16)
    scrow1 = sb(ph, "scrow1", [1, T], BF16)
    sctok = sb(ph, "sctok", [128, 8], BF16)
    gtok = sb(ph, "gtok", [128, 8])
    gcol = sb(ph, "gcol", [128, 1])
    xs_t = [sb(ph, f"xs{i}", [128, 512], BF16) for i in range(NDT)]
    fin_t = [sb(ph, f"fin{i}", [128, 512], BF16) for i in range(NDT)]
    sel16_t = sb(ph, "sel16_t", [16, 16 * 128], BF16)
    moewt = [sb(ph, f"moewt{i}", [128, D], BF16) for i in range(2)]
    ld("sync", sel16_t[:], dram["sel16"][:], "W")

    gb1 = ld("gpsimd", ball[:, 0:T], dram["bc_cout"][0:16, :], "G")
    gb2 = ld("gpsimd", ball[:, T:2 * T], dram["bc_cout"][32:48, :], "G")
    gb3 = ld("gpsimd", call[:, 0:T], dram["bc_cout"][16:32, :], "G")
    g_bc2 = ld("gpsimd", call[:, T:2 * T], dram["bc_cout"][48:64, :], "G")

    for dc in range(NDT):
        ld("sync", wstg[0][:, 8 * dc:8 * (dc + 1)],
           dram["gwT"][128 * dc:128 * (dc + 1), :], "W")
    wait("scalar", "W", cnt["W"])
    wait("scalar", "X", dtw_sp)
    ac_gw = act(scrb[:, 0:64], wstg[0][:, 0:64], AF.Copy)
    wait("tensor", "AC", ac_gw)
    gl_prev = 0
    for tc4 in range(4):
        bank = pbk[4]
        wait("tensor", "AC", gl_prev)
        for dc in range(NDT):
            mm(bank[0:8, 0:512], scrb[:, 8 * dc:8 * (dc + 1)],
               xn[dc][:, 4 + 512 * tc4:4 + 512 * (tc4 + 1)],
               dc == 0, dc == NDT - 1, inc=(dc == NDT - 1))
        wait("scalar", "MM", cnt["MM"])
        gl_prev = act(glrows[:, 512 * tc4:512 * (tc4 + 1)],
                      bank[0:8, 0:512], AF.Copy)
    ac_gl = cnt["AC"]
    wait("tensor", "AC", ac_gl)
    sc_prev_ac = 0
    for tcc in range(16):
        transpose(ptr[:, 0:8], glrows[:, 128 * tcc:128 * (tcc + 1)], kdim=8)
        wait("vector", "MM", cnt["MM"])
        wait("vector", "AC", sc_prev_ac)
        ve("tensor_copy", gtok[:], ptr[:, 0:8], inc=False)
        ve("tensor_reduce", gcol[:], gtok[:], AX.X, ALU.max, negate=True)
        wait("scalar", "VE", cnt["VE"])
        act(gtok[:], gtok[:], AF.Exp, bias=gcol[:])
        wait("vector", "AC", cnt["AC"])
        ve("tensor_reduce", gcol[:], gtok[:], AX.X, ALU.add, inc=False)
        ve("reciprocal", gcol[:], gcol[:], inc=False)
        ve("tensor_scalar", gtok[:], gtok[:], gcol[:], None, ALU.mult,
           inc=False)
        ve("tensor_single_scalar", sctok[:], gtok[:], 0.3, ALU.is_gt,
           inc=False)
        ve("tensor_reduce", gcol[:], sctok[:], AX.X, ALU.add, inc=False)
        ve("memset", sctok[:], 0.0, inc=False)
        ve("tensor_scalar", sctok[:, 1:2], gtok[:, 1:2], gcol[:], None,
           ALU.mult, inc=False)
        ve("tensor_scalar", gcol[:], gcol[:], -1.0, 8.0, ALU.mult, ALU.add,
           inc=False)
        ve("tensor_scalar", sctok[:, 0:1], gtok[:, 0:1], gcol[:], None,
           ALU.mult)
        v_sc = cnt["VE"]
        wait("tensor", "VE", v_sc)
        transpose(ptr[0:8, 896:1024], sctok[:])
        wait("scalar", "MM", cnt["MM"])
        sc_prev_ac = act(scrows[:, 128 * tcc:128 * (tcc + 1)],
                         ptr[0:8, 896:1024], AF.Copy)
    ac_scrow = cnt["AC"]
    wait("tensor", "AC", ac_scrow)
    srep_prev = 0
    for e8 in range(N_EXP):
        gvv = ld("gpsimd", scrow1[:], scrows[e8:e8 + 1, :], "G")
        wait("tensor", "G", gvv)
        for tc4 in range(4):
            wait("tensor", "AC", srep_prev)
            mm(pbk[5][:, 0:512], onesrow_b[:],
               scrow1[:, 512 * tc4:512 * (tc4 + 1)], True, True, inc=True)
            wait("scalar", "MM", cnt["MM"])
            srep_prev = act(srep[e8][:, 512 * tc4:512 * (tc4 + 1)],
                            pbk[5][:, 0:512], AF.Copy)
        wait("gpsimd", "AC", srep_prev)
    ac_srep = cnt["AC"]

    wc_prev = [0, 0]
    for widx in list(range(N_EXP)) + [8]:
        for dc in range(NDT):
            for hf in range(2):
                par = hf
                wait("sync", "X", wc_prev[par])
                wsrc = (dram["expT"][widx, 128 * dc:128 * (dc + 1),
                                     512 * hf:512 * (hf + 1)]
                        if widx < 8 else
                        dram["outwT"][128 * dc:128 * (dc + 1),
                                      512 * hf:512 * (hf + 1)])
                ld("sync", wstg[par][:], wsrc, "W")
                wait("scalar", "W", cnt["W"])
                ac_c = act(scrb[:, 2048 + 512 * par:2048 + 512 * (par + 1)],
                           wstg[par][:], AF.Copy)
                wait("scalar", "AC", ac_c)
                wc_prev[par] = ld(
                    "scalar",
                    dram["wbf_sp"][widx, 128 * dc:128 * (dc + 1),
                                   512 * hf:512 * (hf + 1)],
                    scrb[:, 2048 + 512 * par:2048 + 512 * (par + 1)], "X")
    wcache_done = cnt["X"]

    wait("gpsimd", "AC", ac_srep)
    wait("vector", "G", g_asm)
    wait("vector", "G", max(gb1, gb2, gb3, g_bc2))
    wait("vector", "AC", ac_acol)
    wait("scalar", "G", g_asm)
    wait("tensor", "AC", ac_srep)
    wait("tensor", "X", wcache_done)

    dtw_guard_ac = [0, 0]
    dtw_guard_ve = [0, 0]
    moewt_guard = [0, 0]
    scan_hist = []
    sup_ve_prev = 0
    g2_prev_ac = 0
    gp_hist = []
    y_done = 0
    g2_sp_v = 0

    def emit_moe_chunk(mtc):
        nonlocal g2_prev_ac, g2_sp_v
        c0 = 4 + 512 * mtc
        for e8 in range(N_EXP):
            si = mtc * N_EXP + e8
            if len(gp_hist) >= 1:
                wait("gpsimd", "MM", gp_hist[-1][1])
            for dc in range(NDT):
                ve("tensor_tensor", xs_t[dc][:], xn[dc][:, c0:c0 + 512],
                   srep[e8][:, 512 * mtc:512 * (mtc + 1)], ALU.mult,
                   eng="gpsimd", inc=(dc == NDT - 1))
            v_xs = cnt["GP"]
            for dc in range(NDT):
                wpar = (si * NDT + dc) % 2
                wait("sync", "MM", moewt_guard[wpar])
                ld("sync", moewt[wpar][:],
                   dram["wbf_sp"][e8, 128 * dc:128 * (dc + 1), :], "W")
                wait("tensor", "W", cnt["W"])
                if dc == 0:
                    wait("tensor", "GP", v_xs)
                if mtc > 0 and e8 == 0 and dc == 0:
                    wait("tensor", "AC", g2_prev_ac)
                for dt_o in range(NDT):
                    mm(pbk[dt_o][:, 0:512],
                       moewt[wpar][:, 128 * dt_o:128 * (dt_o + 1)],
                       xs_t[dc][:], e8 == 0 and dc == 0,
                       e8 == N_EXP - 1 and dc == NDT - 1,
                       inc=(dt_o == NDT - 1))
                moewt_guard[wpar] = cnt["MM"]
            gp_hist.append((v_xs, cnt["MM"]))
        wait("scalar", "MM", cnt["MM"])
        for dt_o in range(NDT):
            act(fin_t[dt_o][:], pbk[dt_o][:, 0:512], AF.Copy,
                inc=(dt_o == NDT - 1))
        ac_fin = cnt["AC"]
        for dc in range(NDT):
            wpar = dc % 2
            wait("sync", "MM", moewt_guard[wpar])
            ld("sync", moewt[wpar][:],
               dram["wbf_sp"][8, 128 * dc:128 * (dc + 1), :], "W")
            wait("tensor", "W", cnt["W"])
            if dc == 0:
                wait("tensor", "AC", ac_fin)
            for dt_o in range(NDT):
                mm(pbk[dt_o][:, 0:512],
                   moewt[wpar][:, 128 * dt_o:128 * (dt_o + 1)],
                   fin_t[dc][:], dc == 0, dc == NDT - 1,
                   inc=(dt_o == NDT - 1))
            moewt_guard[wpar] = cnt["MM"]
        wait("scalar", "MM", cnt["MM"])
        for dt_o in range(NDT):
            wait("scalar", "X", g2_sp_v)
            act(scrb[:, 0:512], pbk[dt_o][:, 0:512], AF.Silu)
            wait("scalar", "AC", cnt["AC"])
            g2_sp_v = ld("scalar",
                         dram["g2_sp"][128 * dt_o:128 * (dt_o + 1),
                                       512 * mtc:512 * (mtc + 1)],
                         scrb[:, 0:512], "X")
        g2_prev_ac = cnt["AC"]

    for tcg in range(NTC):
        c0g = TC * tcg
        sup_ac0 = 0
        for s_ in range(S):
            for src_t, dst_sup in ((ball, brsup), (call, cbsup)):
                wait("tensor", "AC", sup_ac0)
                if tcg > 0:
                    wait("tensor", "VE", sup_ve_prev)
                mm(pbk[6][:, 0:512], sel16_t[:, 128 * s_:128 * (s_ + 1)],
                   src_t[:, c0g:c0g + TC], True, True, inc=True)
                wait("scalar", "MM", cnt["MM"])
                if tcg > 0:
                    wait("scalar", "VE", sup_ve_prev)
                sup_ac0 = act(dst_sup[:, SP * s_ + 1:SP * (s_ + 1)],
                              pbk[6][:, 0:TC], AF.Copy)
        ac_sup = cnt["AC"]

        if tcg % 2 == 0:
            emit_moe_chunk(tcg // 2)

        for et in range(NETH):
            par = et % 2
            wait("scalar", "AC", dtw_guard_ac[par])
            wait("scalar", "VE", dtw_guard_ve[par])
            ld("scalar", dtw[par][:],
               dram["dtg_sp"][128 * et:128 * (et + 1), c0g:c0g + TC], "X")
            vww = ld("scalar", wwt[par][:],
                     dram["wg_sp"][128 * et:128 * (et + 1), c0g:c0g + TC],
                     "X")
            wait("scalar", "X", vww)
            wait("vector", "X", vww)
            wait("vector", "AC", ac_sup)
            if tcg == 0:
                ve("memset", carryt[et][:], 0.0, inc=False)
                ve("memset", abuf[0][:, 0:1], 0.0, inc=False)
                ve("memset", abuf[1][:, 0:1], 0.0)
            for s_ in range(S):
                sp = s_ % 2
                if len(scan_hist) >= 2:
                    wait("scalar", "VE", scan_hist[-2])
                act(abuf[sp][:, 1:SP], dtw[par][:], AF.Exp,
                    scale=acols[:, s_:s_ + 1])
                ac_a = cnt["AC"]
                ve("tensor_tensor", ubuf[sp][:, 1:SP], wwt[par][:],
                   brsup[:, SP * s_ + 1:SP * (s_ + 1)], ALU.mult, inc=False)
                ve("tensor_copy", ubuf[sp][:, 0:1],
                   carryt[et][:, s_:s_ + 1], inc=False)
                wait("vector", "AC", ac_a)
                ve("tensor_tensor_scan", hsup[:, SP * s_:SP * (s_ + 1)],
                   abuf[sp][:], ubuf[sp][:], 0.0, ALU.mult, ALU.add)
                scan_hist.append(cnt["VE"])
            v_scans = cnt["VE"]
            dtw_guard_ve[par] = v_scans
            wait("scalar", "VE", v_scans)
            ac_carry = act(
                carryt[et][:],
                hsup[:].rearrange("p (s c) -> p s c", c=SP)[:, :, SP - 1:SP]
                .rearrange("p s c -> p (s c)"), AF.Copy)
            dtw_guard_ac[par] = ac_carry
            wait("vector", "AC", ac_carry)
            for s_ in range(S):
                ve("tensor_tensor", hsup[:, SP * s_ + 1:SP * (s_ + 1)],
                   hsup[:, SP * s_ + 1:SP * (s_ + 1)],
                   cbsup[:, SP * s_ + 1:SP * (s_ + 1)], ALU.mult,
                   inc=(s_ == S - 1))
            ve("tensor_reduce", yred[:],
               hsup[:].rearrange("p (s c) -> p s c", c=SP)[:, :, 1:SP]
               .rearrange("p s c -> p c s"), AX.X, ALU.add)
            v_y = cnt["VE"]
            sup_ve_prev = v_y
            wait("scalar", "VE", v_y)
            ac_y = act(yredb[:], yred[:], AF.Copy)
            wait("scalar", "AC", ac_y)
            y_done = ld("scalar", dram["y_sp"][128 * et:128 * (et + 1),
                                               c0g:c0g + TC], yredb[:], "X")
            wait("vector", "AC", ac_y)
    ph.close()

    ph = contextlib.ExitStack()
    wosl = [sb(ph, f"wosl{i}", [128, 2 * D], BF16) for i in range(NDT)]
    zt = [sb(ph, f"zt{i}", [128, T], BF16) for i in range(NET)]
    g2t = [sb(ph, f"g2t{i}", [128, 512], BF16) for i in range(2)]
    x1t = [sb(ph, f"x1t{i}", [128, 512], BF16) for i in range(2)]

    wait("gpsimd", "X", y_done)
    cc_y = collective("AllGather", ALU.bypass, dram["y_sp"][:],
                      dram["y_cout"][:])
    wait("gpsimd", "CC", cc_y)

    ymsem = sems[_issue("G")[0]]

    def dyn_ymine(e):
        with e.register("jc2") as jr:
            e.reg_load(jr, joff_t[0:1, 1:2])
            off = e.snap(jr)
            e.dma_start(out=dram["ymine_sp"][:],
                        in_=dram["y_cout"][:, bass.ds(off, T)]
                        ).then_inc(ymsem, 16)
    emit("gpsimd", dyn_ymine)
    g_ym = cnt["G"]
    dep_record("gpsimd", "G", g_ym, ["y_cout"], ["ymine_sp"])
    wait("gpsimd", "G", g_ym)

    def wout_slot(i):
        return wosl[i % 8][:, (i // 8) * D:(i // 8 + 1) * D]

    wg4 = [0, 0]
    ac_wout = 0
    for i in range(NET):
        for hf in range(2):
            par = hf
            wait("sync", "AC", wg4[par])
            ld("sync", wstg[par][:],
               dram["woutT"][128 * i:128 * (i + 1),
                             512 * hf:512 * (hf + 1)], "W")
            wait("scalar", "W", cnt["W"])
            ac_wout = act(wout_slot(i)[:, 512 * hf:512 * (hf + 1)],
                          wstg[par][:], AF.Copy)
            wg4[par] = ac_wout

    wait("scalar", "G", g_ym)
    z_prev = 0
    for i in range(NET):
        wait("scalar", "AC", z_prev)
        ld("scalar", zt[i][:], dram["xc_sp"][128 * i:128 * (i + 1), :], "X")
        ld("scalar", scrb[:, 0:T],
           dram["ymine_sp"][128 * i:128 * (i + 1), :], "X")
        vrs = ld("scalar", scrb[:, 2048:2048 + T],
                 dram["res_sp"][128 * i:128 * (i + 1), :], "X")
        wait("vector", "X", vrs)
        ve("scalar_tensor_tensor", scrb[:, 0:T], zt[i][:],
           bias_t["dd_c"][:, i:i + 1], scrb[:, 0:T], ALU.mult, ALU.add,
           inc=False)
        ve("tensor_tensor", scrb[:, 0:T], scrb[:, 0:T],
           scrb[:, 2048:2048 + T], ALU.add)
        wait("scalar", "VE", cnt["VE"])
        z_prev = act(zt[i][:], scrb[:, 0:T], AF.Silu)
    ac_z = cnt["AC"]

    wait("tensor", "AC", max(ac_z, ac_wout))
    out_prev_g = [0, 0]
    for dt_o in range(NDT):
        for tc4 in range(4):
            par = tc4 % 2
            bank = pbk[par]
            wait("scalar", "G", out_prev_g[par])
            vg2 = ld("scalar", g2t[par][:],
                     dram["g2_sp"][128 * dt_o:128 * (dt_o + 1),
                                   512 * tc4:512 * (tc4 + 1)], "X")
            vx1 = ld("scalar", x1t[par][:],
                     dram["x1_sp"][128 * dt_o:128 * (dt_o + 1),
                                   4 + 512 * tc4:4 + 512 * (tc4 + 1)], "X")
            wait("tensor", "G", out_prev_g[par])
            wait("tensor", "VE", cnt["VE"])
            for ec in range(NET):
                mm(bank[:, 0:512],
                   wout_slot(ec)[:, 128 * dt_o:128 * (dt_o + 1)],
                   zt[ec][:, 512 * tc4:512 * (tc4 + 1)], ec == 0,
                   ec == NET - 1, inc=(ec == NET - 1))
            wait("vector", "MM", cnt["MM"])
            wait("vector", "X", vx1)
            wait("vector", "G", out_prev_g[par])
            ve("tensor_tensor", scr[:, 512 * par:512 * (par + 1)],
               bank[:, 0:512], g2t[par][:], ALU.mult, inc=False)
            ve("tensor_tensor", scr2[:, 512 * par:512 * (par + 1)],
               x1t[par][:], scr[:, 512 * par:512 * (par + 1)], ALU.add)
            wait("gpsimd", "VE", cnt["VE"])
            out_prev_g[par] = ld(
                "gpsimd",
                dram["out_t"][128 * dt_o:128 * (dt_o + 1),
                              512 * tc4:512 * (tc4 + 1)],
                scr2[:, 512 * par:512 * (par + 1)], "G")
    out_done = cnt["G"]
    wait("gpsimd", "G", out_done)
    ph.close()

    for q in DMA_SEMS:
        flush_bar(q)

    with nc.Block() as block:
        @block.sync
        def _(e):
            for f in thunks["sync"]:
                f(e)

        @block.scalar
        def _(e):
            for f in thunks["scalar"]:
                f(e)

        @block.vector
        def _(e):
            for f in thunks["vector"]:
                f(e)

        @block.tensor
        def _(e):
            for f in thunks["tensor"]:
                f(e)

        @block.gpsimd
        def _(e):
            for f in thunks["gpsimd"]:
                f(e)

    es.close()
    return nc


_NC_CACHE = {}


def kernel(**inputs):
    in_maps = _host_prepare(inputs)
    if "nc" not in _NC_CACHE:
        _NC_CACHE["nc"] = build_program()
    nc = _NC_CACHE["nc"]
    res = run_bass_kernel_spmd(nc, in_maps, core_ids=list(range(8)))
    out = np.zeros((4, 4096, 1024), np.float32)
    for c in range(8):
        b, j = c // 2, c % 2
        out[b, j * T:(j + 1) * T] = np.asarray(res.results[c]["out_t"]).T
    return out


